# version 12
# baseline (speedup 1.0000x reference)
import numpy as np
import concourse.bass as bass
import concourse.mybir as mybir
from concourse.bass_utils import run_bass_kernel_spmd

F32 = mybir.dt.float32
BF16 = mybir.dt.bfloat16
AF = mybir.ActivationFunctionType
ALU = mybir.AluOpType
AX = mybir.AxisListType

D = 1024
T = 2048
NCH = 8
NTOK = 16
LN_EPS = 1e-5


def _np_bf16(x):
    import ml_dtypes
    return np.asarray(np.asarray(x, np.float32), dtype=ml_dtypes.bfloat16)


class EngWrap:

    _NOWAIT = {"wait_ge", "wait_op", "sem_inc", "nop", "drain"}

    def __init__(self, k, name, raw):
        self._k = k
        self._name = name
        self._raw = raw

    def __getattr__(self, attr):
        fn = getattr(self._raw, attr)
        if attr in self._NOWAIT or not callable(fn):
            return fn
        k, name = self._k, self._name

        def wrapped(*a, **kw):
            v = k.cnt[name]
            if v > k.watermark[name][name]:
                self._raw.wait_ge(k.sem[name], v)
                k.watermark[name][name] = v
            return fn(*a, **kw)
        return wrapped


class DmaWrap:

    _DMA = {"dma_start", "dma_start_transpose", "indirect_dma_start"}

    def __init__(self, k, name, raw):
        self._k = k
        self._name = name
        self._raw = raw

    def __getattr__(self, attr):
        fn = getattr(self._raw, attr)
        if attr not in self._DMA or not callable(fn):
            return fn
        k, name = self._k, self._name

        def wrapped(*a, **kw):
            v = k.cnt[name]
            if v > k.watermark[name][name]:
                self._raw.wait_ge(k.sem[name], v)
                k.watermark[name][name] = v
            return fn(*a, **kw)
        return wrapped


class Ctx:
    ENGS = ("pe", "act", "dve", "pool", "sp")

    def __init__(self, nc):
        self.nc = nc
        self.sem = {e: nc.alloc_semaphore(f"sem_{e}") for e in self.ENGS}
        self.sem["bgc"] = nc.alloc_semaphore("sem_bgc")
        self.cnt = {e: 0 for e in self.ENGS}
        self.cnt["bgc"] = 0
        self.watermark = {e: {p: 0 for p in self.ENGS + ("bgc",)} for e in self.ENGS}
        raw = {
            "pe": nc.tensor, "act": nc.scalar, "dve": nc.vector,
            "pool": nc.gpsimd, "sp": nc.sync,
        }
        self.raw = raw
        self.eng = {
            "pe": raw["pe"], "sp": DmaWrap(self, "sp", raw["sp"]),
            "act": EngWrap(self, "act", raw["act"]),
            "dve": EngWrap(self, "dve", raw["dve"]),
            "pool": EngWrap(self, "pool", raw["pool"]),
        }

    def inc(self, e, inst, amt=1):
        inst.then_inc(self.sem[e], amt)
        self.cnt[e] += amt
        return (e, self.cnt[e])

    def tok(self, e):
        return (e, self.cnt[e])

    def wait(self, consumer, *tokens):
        for tk in tokens:
            if tk is None:
                continue
            (p, v) = tk
            if v <= 0:
                continue
            if v > self.watermark[consumer][p]:
                self.eng[consumer].wait_ge(self.sem[p], v)
                self.watermark[consumer][p] = v


def make_consts(nc):
    c = {}
    c["ones128"] = nc.inline_tensor(np.ones((128, 128), np.float32), "c_ones128")
    c["ident_f32"] = nc.inline_tensor(np.eye(128, dtype=np.float32), "c_ident_f32")
    sel = np.zeros((8, 8 * 128), np.float32)
    for e in range(8):
        sel[e, e * 128:(e + 1) * 128] = 1.0
    c["sel8"] = nc.inline_tensor(sel, "c_sel8")
    c["epscols"] = nc.inline_tensor(
        np.array([[1e-5, 1e-6, 1.0]], np.float32) * np.ones((128, 1), np.float32),
        "c_epscols")
    bo = np.zeros((128, 128), np.float32)
    bo[0:64, 0:64] = 1.0
    bo[64:128, 64:128] = 1.0
    c["blockones_sh"] = nc.inline_tensor(bo, "c_blockones_sh")
    c["ones_bf_sh"] = nc.inline_tensor(_np_bf16(np.ones((128, 128))), "c_ones_bf_sh")
    return c


def load_consts(k, c):
    nc, sp = k.nc, k.eng["sp"]
    sb = {}
    shapes = {
        "ones128": ([128, 128], F32), "ident_f32": ([128, 128], F32),
        "epscols": ([128, 3], F32),
    }
    tok = None
    for name, (shape, dt) in shapes.items():
        sb[name] = nc.alloc_sbuf_tensor("cs_" + name, shape, dt)
        i = sp.dma_start(out=sb[name][:], in_=c[name].ap())
        tok = k.inc("sp", i, 16)
    sb["_tok"] = tok
    return sb


def emit_load_x(k, B, x_dram):
    cs, x_fm, ps_b, xio = B["cs"], B["x_fm"], B["ps_b"], B["xio"]
    pe, dve, sp = k.eng["pe"], k.eng["dve"], k.eng["sp"]
    t_evac = {}
    t_pe_done = {}
    for tt in range(NTOK):
        bi = tt % 2
        if tt >= 2:
            k.wait("sp", t_pe_done[tt - 2])
        i = sp.dma_start(out=xio[:, bi * D:(bi + 1) * D],
                         in_=x_dram[tt * 128:(tt + 1) * 128, :])
        t_dma = k.inc("sp", i, 16)
        k.wait("pe", t_dma, cs["_tok"])
        for c in range(NCH):
            j = tt * NCH + c
            pcol = (j % 2) * 512
            if j >= 2:
                k.wait("pe", t_evac[j - 2])
            inst = pe.transpose(ps_b[:, pcol:pcol + 128],
                                xio[:, bi * D + c * 128: bi * D + (c + 1) * 128],
                                cs["ident_f32"][:])
            t_t = k.inc("pe", inst)
            k.wait("dve", t_t)
            i = dve.tensor_copy(x_fm[:, c * T + tt * 128: c * T + (tt + 1) * 128],
                                ps_b[:, pcol:pcol + 128])
            t_evac[j] = k.inc("dve", i)
        t_pe_done[tt] = k.tok("pe")
    return k.tok("dve")


def emit_store_x(k, B, out_dram, after):
    cs, x_fm, ps_b, xio = B["cs"], B["x_fm"], B["ps_b"], B["xio"]
    pe, dve, sp = k.eng["pe"], k.eng["dve"], k.eng["sp"]
    k.wait("pe", *after)
    t_evac = {}
    t_dma = {}
    last = None
    for tt in range(NTOK):
        bi = tt % 2
        if tt >= 2:
            k.wait("dve", t_dma[tt - 2])
        for c in range(NCH):
            j = tt * NCH + c
            pcol = (j % 2) * 512
            if j >= 2:
                k.wait("pe", t_evac[j - 2])
            inst = pe.transpose(ps_b[:, pcol:pcol + 128],
                                x_fm[:, c * T + tt * 128: c * T + (tt + 1) * 128],
                                cs["ident_f32"][:])
            t_t = k.inc("pe", inst)
            k.wait("dve", t_t)
            i = dve.tensor_copy(xio[:, bi * D + c * 128: bi * D + (c + 1) * 128],
                                ps_b[:, pcol:pcol + 128])
            t_evac[j] = k.inc("dve", i)
        k.wait("sp", k.tok("dve"))
        i = sp.dma_start(out=out_dram[tt * 128:(tt + 1) * 128, :],
                         in_=xio[:, bi * D:(bi + 1) * D])
        t_dma[tt] = last = k.inc("sp", i, 16)
    return last


def emit_ln(k, B, after=(), affine=None, fold_ln2=False):
    cs = B["cs"]
    x_fm, y_bf = B["x_fm"], B["y_bf"]
    scr0, m_b, rstd_b = B["xio"], B["scr"][0], B["scr"][1]
    ps_sum = [B["ps_a"], B["ps_b"]]
    ps_sqm = [B["ps_up"], B["ps_dn"]]
    pe, act, dve, pool = k.eng["pe"], k.eng["act"], k.eng["dve"], k.eng["pool"]

    def sum_tile(idx, ts):
        return ps_sum[ts // 2][:, (ts % 2) * 512:(ts % 2) * 512 + 512] if idx == 0 else \
            ps_sqm[ts // 2][:, (ts % 2) * 512:(ts % 2) * 512 + 512]

    k.wait("pe", *after, cs["_tok"])
    k.wait("act", *after)
    t_sq = {}
    for c in range(NCH):
        if c > 0:
            k.wait("act", t_mm_sq)
        i = act.activation(scr0[:], x_fm[:, c * T:(c + 1) * T], AF.Square)
        t_sqc = k.inc("act", i)
        k.wait("pe", t_sqc)
        for ts in range(4):
            inst = pe.matmul(sum_tile(0, ts), cs["ones128"][:],
                             x_fm[:, c * T + ts * 512: c * T + ts * 512 + 512],
                             start=(c == 0), stop=(c == NCH - 1))
        for ts in range(4):
            inst2 = pe.matmul(sum_tile(1, ts), cs["ones128"][:],
                              scr0[:, ts * 512:(ts + 1) * 512],
                              start=(c == 0), stop=(c == NCH - 1))
        t_mm_sq = k.inc("pe", inst2)
    t_stats = t_mm_sq

    k.wait("dve", t_stats)
    for ts in range(4):
        i = dve.tensor_scalar(m_b[:, ts * 512:(ts + 1) * 512], sum_tile(0, ts),
                              1.0 / D, None, ALU.mult)
        k.inc("dve", i)
        i = dve.tensor_scalar(rstd_b[:, ts * 512:(ts + 1) * 512], sum_tile(1, ts),
                              1.0 / D, None, ALU.mult)
        k.inc("dve", i)
    t_msum = k.tok("dve")
    k.wait("act", t_msum)
    i = act.activation(scr0[:], m_b[:], AF.Square)
    t_msq = k.inc("act", i)
    k.wait("dve", t_msq)
    i = dve.tensor_sub(rstd_b[:], rstd_b[:], scr0[:])
    t_var = k.inc("dve", i)
    k.wait("act", t_var)
    if fold_ln2:
        Rt = B["ln2_R"]
        k.wait("dve", t_var)
        i = dve.tensor_copy(Rt[:], rstd_b[:])
        t_varc = k.inc("dve", i)
        k.wait("act", t_varc)
    i = act.activation(rstd_b[:], rstd_b[:], AF.Ln, bias=cs["epscols"][:, 0:1])
    k.inc("act", i)
    i = act.activation(rstd_b[:], rstd_b[:], AF.Exp, scale=-0.5)
    t_rstd = k.inc("act", i)
    applyR = rstd_b
    if fold_ln2:
        k.wait("dve", t_rstd)
        i = dve.tensor_mul(Rt[:], Rt[:], rstd_b[:])
        k.inc("dve", i)
        i = dve.tensor_mul(Rt[:], Rt[:], rstd_b[:])
        t_v2 = k.inc("dve", i)
        k.wait("act", t_v2)
        i = act.activation(Rt[:], Rt[:], AF.Ln, bias=cs["epscols"][:, 0:1])
        k.inc("act", i)
        i = act.activation(Rt[:], Rt[:], AF.Exp, scale=-0.5)
        t_r2 = k.inc("act", i)
        k.wait("dve", t_r2)
        i = dve.tensor_mul(Rt[:], Rt[:], rstd_b[:])
        t_rstd = k.inc("dve", i)
        applyR = Rt

    toks = []
    k.wait("dve", t_rstd)
    for c in range(NCH):
        i = dve.tensor_sub(scr0[:], x_fm[:, c * T:(c + 1) * T], m_b[:])
        k.inc("dve", i)
        if affine is None:
            i = dve.tensor_mul(y_bf[:, c * T:(c + 1) * T], scr0[:], applyR[:])
        else:
            gcol, bcol = affine
            i = dve.tensor_mul(scr0[:], scr0[:], applyR[:])
            k.inc("dve", i)
            i = dve.tensor_scalar(y_bf[:, c * T:(c + 1) * T], scr0[:],
                                  gcol[:, c:c + 1], bcol[:, c:c + 1],
                                  ALU.mult, ALU.add)
        toks.append(k.inc("dve", i))
    return m_b, rstd_b, toks


def emit_l1ffn(k, B, W, after, dbg=None, parts="full"):
    nc = k.nc
    pe, act, dve, pool, sp = (k.eng[e] for e in ("pe", "act", "dve", "pool", "sp"))
    cs = B["cs"]
    x_fm, y_bf, scr = B["x_fm"], B["y_bf"], B["scr"]
    ps_a, ps_b, ps_up, ps_dn = B["ps_a"], B["ps_b"], B["ps_up"], B["ps_dn"]

    m_b, rstd_b, ln_toks = emit_ln(k, B, after)
    if dbg is not None:
        k.wait("pool", *ln_toks)
        i = pool.dma_start(out=dbg["m"], in_=m_b[:])
        k.inc("pool", i, 16)
        i = pool.dma_start(out=dbg["y"], in_=y_bf[:])
        k.inc("pool", i, 16)
        i = pool.dma_start(out=dbg["rstd"], in_=rstd_b[:])
        k.inc("pool", i, 16)

    gwT = B["g_gwT"]
    i = sp.dma_start(out=gwT[:], in_=W["gate_wT"].rearrange("(c p) e -> p c e", p=128))
    t_gw = k.inc("sp", i, 16)
    bias_bc = B["g_biasbc"]
    i = sp.dma_start(out=bias_bc[:, 0:8], in_=W["gate_b"][None, :].partition_broadcast(128))
    k.inc("sp", i, 16)
    i = sp.dma_start(out=bias_bc[:, 8:16], in_=W["expert_bias"][None, :].partition_broadcast(128))
    t_brow = k.inc("sp", i, 16)
    b1cols = B["f_b1cols"]
    i = sp.dma_start(out=b1cols[:], in_=W["b1"].rearrange("e (c p) -> p e c", p=128),
                     allow_slow_non_contiguous=True)
    t_b1 = k.inc("sp", i, 16)
    b2cols = B["f_b2cols"]
    i = sp.dma_start(out=b2cols[:], in_=W["b2"].rearrange("e (c p) -> p e c", p=128),
                     allow_slow_non_contiguous=True)
    t_b2 = k.inc("sp", i, 16)
    k.wait("dve", t_brow)
    i = dve.tensor_add(bias_bc[:, 0:8], bias_bc[:, 0:8], bias_bc[:, 8:16])
    t_biasbc = k.inc("dve", i)

    G_bc = B["g_grow"]
    k.wait("pe", t_gw, cs["_tok"], k.tok("dve"))
    for c in range(NCH):
        inst = pe.matmul(ps_a[:, 512:520], cs["ones128"][:], gwT[:, c * 8:(c + 1) * 8],
                         start=(c == 0), stop=(c == NCH - 1), skip_group_check=True)
    t_G = k.inc("pe", inst)
    k.wait("dve", t_G)
    i = dve.tensor_copy(G_bc[:], ps_a[:, 512:520])
    t_Gbc = k.inc("dve", i)

    rstd_tok = B["g_rstdtok"]
    s_tok = B["g_stok"]
    mblk = B["g_gterm"]
    t_prev = None
    for ts in range(NTOK):
        if t_prev is not None:
            k.wait("pe", t_prev)
        inst = pe.transpose(ps_b[:, 0:128],
                            m_b[:, ts * 128:(ts + 1) * 128], cs["ident_f32"][:])
        k.inc("pe", inst)
        inst = pe.transpose(ps_b[:, 512:640],
                            rstd_b[:, ts * 128:(ts + 1) * 128], cs["ident_f32"][:])
        t_t = k.inc("pe", inst)
        k.wait("dve", t_t)
        i = dve.tensor_copy(mblk[:], ps_b[:, 0:8])
        k.inc("dve", i)
        i = dve.tensor_copy(rstd_tok[:, ts * 8:(ts + 1) * 8], ps_b[:, 512:520])
        t_prev = k.inc("dve", i)
        i = dve.tensor_mul(s_tok[:, ts * 8:(ts + 1) * 8], mblk[:],
                           rstd_tok[:, ts * 8:(ts + 1) * 8])
        k.inc("dve", i)
    t_stok = k.tok("dve")

    if dbg is not None:
        k.wait("pool", t_stok)
        i = pool.dma_start(out=dbg["rstdtok"], in_=rstd_tok[:, 0:16])
        k.inc("pool", i, 16)

    mask05, lv, mx = B["g_mask"], B["g_l"], B["g_mx"]
    gterm = B["g_gterm"]
    k.wait("pe", t_biasbc, t_stok, *ln_toks)
    t_mask_ts = {}
    for ts in range(NTOK):
        if ts >= 1:
            k.wait("pe", t_mask_ts[ts - 1])
        for c in range(NCH):
            inst = pe.matmul(ps_a[:, 0:8],
                             x_fm[:, c * T + ts * 128: c * T + (ts + 1) * 128],
                             gwT[:, c * 8:(c + 1) * 8],
                             start=(c == 0), stop=(c == NCH - 1), skip_group_check=True)
        t_A = k.inc("pe", inst)
        k.wait("dve", t_A)
        i = dve.scalar_tensor_tensor(lv[:], ps_a[:, 0:8], rstd_tok[:, ts * 8:ts * 8 + 1],
                                     bias_bc[:, 0:8], ALU.mult, ALU.add)
        k.inc("dve", i)
        i = dve.tensor_scalar(gterm[:], G_bc[:], s_tok[:, ts * 8:ts * 8 + 1], None, ALU.mult)
        k.inc("dve", i)
        i = dve.tensor_sub(lv[:], lv[:], gterm[:])
        t_lv = k.inc("dve", i)
        i = dve.tensor_reduce(mx[:], lv[:], AX.X, ALU.max)
        k.inc("dve", i)
        i = dve.tensor_scalar(mask05[:, ts * 8:(ts + 1) * 8], lv[:], mx[:], 0.5,
                              ALU.is_ge, ALU.mult)
        t_mask_ts[ts] = k.inc("dve", i)
        if dbg is not None:
            i = dve.tensor_copy(B["g_lvdump"][:, ts * 8:(ts + 1) * 8], lv[:])
            k.inc("dve", i)

    if dbg is not None:
        k.wait("pool", t_mask_ts[NTOK - 1], k.tok("dve"))
        i = pool.dma_start(out=dbg["mask"], in_=mask05[:])
        k.inc("pool", i, 16)
        i = pool.dma_start(out=dbg["lv"], in_=B["g_lvdump"][:])
        k.inc("pool", i, 16)

    mask_fm = B["g_maskfm"]
    maskT = B["g_maskT"]
    k.wait("pe", t_mask_ts[NTOK - 1], t_prev)
    inst = pe.transpose(ps_b[:, 0:128], mask05[:], cs["ident_f32"][:])
    t_t = k.inc("pe", inst)
    k.wait("dve", t_t)
    i = dve.tensor_copy(maskT[:], ps_b[:, 0:128])
    t_mT = k.inc("dve", i)
    k.wait("sp", t_mT)
    i = sp.dma_start(out=W["mask_scratch"], in_=maskT[:])
    t_ms = k.inc("sp", i, 16)
    k.wait("sp", t_ms)
    i = sp.dma_start(out=mask_fm[:],
                     in_=W["mask_scratch"].rearrange("(ts e) p -> e ts p", e=8))
    t_maskfm = k.inc("sp", i, 16)

    if dbg is not None:
        k.wait("pool", t_maskfm)
        i = pool.dma_start(out=dbg["mt"], in_=mask_fm[:])
        k.inc("pool", i, 16)
    if parts == "gate":
        return (k.tok("dve"), k.tok("sp"))
    w1c = B["f_w1c"]
    w2c = B["f_w2c"]
    hbuf = B["f_h"]
    mbc = B["f_mbc"]
    w1cache, w2cache = W["w1cache"], W["w2cache"]
    tmp = B["xio"]
    k.wait("sp", W.get("bg_e1"))
    i = sp.dma_start(out=B["sel8_sb"][:], in_=B["c_inline"]["sel8"].ap())
    t_sel = k.inc("sp", i, 16)

    def bank(fc):
        t = (ps_a, ps_b, ps_up, ps_dn)[fc // 2]
        return t[:, (fc % 2) * 512:(fc % 2) * 512 + 512]

    t_w1_used = {}
    t_w2_used = {}
    t_up_done = {}
    t_evac = {}
    t_slot_read = {0: None, 1: None}
    t_mbc_read = ()
    t_mbc_last = None
    t_h_read = None
    w1_step = 0
    w2_step = 0
    for tq in range(4):
        tok0 = tq * 512
        for e in range(8):
            for hcq in range(8):
                bi = w1_step % 2
                prev = t_w1_used.get(w1_step - 2)
                if prev is not None:
                    k.wait("sp", prev)
                i = sp.dma_start(out=w1c[bi][:],
                                 in_=w1cache[e, hcq].rearrange("(c p) m -> p c m", p=128))
                t_w1 = k.inc("sp", i, 16)
                k.wait("pe", t_w1, *ln_toks)
                for hcl in range(4):
                    hc = hcq * 4 + hcl
                    up_ps = ps_up[:, (hc % 2) * 512:(hc % 2) * 512 + 512]
                    prev = t_up_done.get(hc - 2)
                    if prev is not None:
                        k.wait("pe", prev)
                    if hc < 2:
                        k.wait("pe", t_evac.get(4 + hc))
                    for kc in range(NCH):
                        inst = pe.matmul(
                            up_ps,
                            w1c[bi][:, kc * 512 + hcl * 128: kc * 512 + (hcl + 1) * 128],
                            y_bf[:, kc * T + tok0: kc * T + tok0 + 512],
                            start=(kc == 0), stop=(kc == NCH - 1))
                    t_up = k.inc("pe", inst)
                    eng, en = (act, "act") if hc % 2 == 0 else (dve, "dve")
                    k.wait(en, t_up, t_b1, t_h_read)
                    if en == "act":
                        i = act.activation(hbuf[:, hc * 512:(hc + 1) * 512], up_ps, AF.Relu,
                                           bias=b1cols[:, e * 32 + hc: e * 32 + hc + 1])
                    else:
                        i = dve.tensor_scalar(hbuf[:, hc * 512:(hc + 1) * 512], up_ps,
                                              b1cols[:, e * 32 + hc: e * 32 + hc + 1], 0.0,
                                              ALU.add, ALU.max)
                    t_up_done[hc] = k.inc(en, i)
                t_w1_used[w1_step] = k.tok("pe")
                w1_step += 1
            t_h_act, t_h_dve = k.tok("act"), k.tok("dve")

            k.wait("pe", t_maskfm, t_sel, t_evac.get(1), t_mbc_last)
            inst = pe.matmul(ps_a[:, 512:1024],
                             B["sel8_sb"][:, e * 128:(e + 1) * 128],
                             mask_fm[:, tok0:tok0 + 512], start=True, stop=True,
                             skip_group_check=True)
            t_mm = k.inc("pe", inst)
            k.wait("act", t_mm, *t_mbc_read)
            i = act.activation(mbc[:], ps_a[:, 512:1024][:, 0:512], AF.Copy)
            t_mbc = t_mbc_last = k.inc("act", i)
            if dbg is not None and e == 0 and tq == 0:
                k.wait("pool", t_h_act, t_h_dve)
                i = pool.dma_start(out=dbg["h"], in_=hbuf[:])
                k.inc("pool", i, 16)

            for j2 in range(16):
                bi = w2_step % 2
                prev = t_w2_used.get(w2_step - 2)
                if prev is not None:
                    k.wait("sp", prev)
                i = sp.dma_start(out=w2c[bi][:].rearrange("p (a m) -> p a m", a=2),
                                 in_=w2cache[e, j2 * 2:j2 * 2 + 2].transpose([1, 0, 2]))
                t_w2 = k.inc("sp", i, 16)
                k.wait("pe", t_w2, t_h_act, t_h_dve, t_mbc)
                if j2 == 0:
                    k.wait("pe", *[t_evac.get(fc) for fc in range(8)])
                for sub in range(2):
                    hc = j2 * 2 + sub
                    for fc in range(8):
                        inst = pe.matmul(bank(fc),
                                         w2c[bi][:, sub * 1024 + fc * 128:
                                                 sub * 1024 + (fc + 1) * 128],
                                         hbuf[:, hc * 512:(hc + 1) * 512],
                                         start=(hc == 0), stop=(hc == 31))
                        k.inc("pe", inst)
                t_w2_used[w2_step] = k.tok("pe")
                w2_step += 1
            t_dn = t_h_read = k.tok("pe")
            for fc in range(8):
                xsl = slice(fc * T + tok0, fc * T + tok0 + 512)
                if fc % 2 == 0:
                    k.wait("dve", t_dn, t_b2, t_mbc)
                    i = dve.scalar_tensor_tensor(tmp[:, 0:512], bank(fc),
                                                 b2cols[:, e * 8 + fc: e * 8 + fc + 1],
                                                 mbc[:], ALU.add, ALU.mult)
                    t_evac[fc] = k.inc("dve", i)
                    i = dve.tensor_add(x_fm[:, xsl], x_fm[:, xsl], tmp[:, 0:512])
                    k.inc("dve", i)
                else:
                    s = (fc // 2) % 2
                    off = 512 + s * 512
                    k.wait("dve", t_dn, t_b2, t_mbc, t_slot_read[s])
                    i = dve.scalar_tensor_tensor(tmp[:, off:off + 512], bank(fc),
                                                 b2cols[:, e * 8 + fc: e * 8 + fc + 1],
                                                 mbc[:], ALU.add, ALU.mult)
                    t_evac[fc] = k.inc("dve", i)
                    k.wait("pool", t_evac[fc])
                    i = pool.tensor_add(x_fm[:, xsl], x_fm[:, xsl], tmp[:, off:off + 512])
                    t_slot_read[s] = k.inc("pool", i)
            t_mbc_read = (k.tok("dve"),)
        t_up_done = {}
    return (k.tok("dve"), k.tok("pool"))


def emit_cast_l1ffn(k, B, W):
    nc = k.nc
    dve, pool, act, sp = k.eng["dve"], k.eng["pool"], k.eng["act"], k.eng["sp"]
    fbuf = [B["scr"][0], B["scr"][1]]
    bbuf = B["cast_b"]
    w1, w2 = W["w1"], W["w2"]
    w1cache, w2cache = W["w1cache"], W["w2cache"]
    engs = [("dve", dve), ("pool", pool)]
    t_store = {}
    t_cast = {}
    step = 0

    def do(load_in, store_out):
        nonlocal step
        bi = step % 2
        prev = t_cast.get(step - 2)
        if prev is not None:
            k.wait("sp", prev)
        i = sp.dma_start(out=fbuf[bi][:], in_=load_in)
        t_l = k.inc("sp", i, 16)
        en, eng = engs[step % 2]
        k.wait(en, t_l, t_store.get(step - 2))
        i = eng.tensor_copy(bbuf[bi], fbuf[bi][:])
        t_cast[step] = k.inc(en, i)
        k.wait("sp", t_cast[step])
        i = sp.dma_start(out=store_out, in_=bbuf[bi])
        t_store[step] = k.inc("sp", i, 16)
        step += 1

    for e in range(8):
        for kc in range(8):
            for half in range(2):
                do(w1[e, kc * 128:(kc + 1) * 128, half * 2048:(half + 1) * 2048],
                   w1cache[e, half * 4:(half + 1) * 4, kc * 128:(kc + 1) * 128, :]
                   .transpose([1, 0, 2]))
        for h2 in range(16):
            do(w2[e, h2 * 256:(h2 + 1) * 256, :].rearrange("(c p) m -> p c m", p=128),
               w2cache[e, h2 * 2:(h2 + 1) * 2].transpose([1, 0, 2]))
    return t_store[step - 1], t_cast[step - 1], k.tok("dve"), k.tok("pool")


def build_common(nc, k):
    B = {}
    c_inline = make_consts(nc)
    B["c_inline"] = c_inline
    B["cs"] = load_consts(k, c_inline)
    B["x_fm"] = nc.alloc_sbuf_tensor("x_fm", [128, NCH * T], F32)
    B["y_bf"] = nc.alloc_sbuf_tensor("y_bf", [128, NCH * T], BF16)
    B["scr"] = [nc.alloc_sbuf_tensor(f"scr{i}", [128, T], F32) for i in range(2)]
    B["xio"] = nc.alloc_sbuf_tensor("xio", [128, 2 * D], F32)
    B["ps_a"] = nc.alloc_psum_tensor("ps_a", [128, 1024], F32)
    B["ps_b"] = nc.alloc_psum_tensor("ps_b", [128, 1024], F32)
    B["ps_up"] = nc.alloc_psum_tensor("ps_up", [128, 1024], F32)
    B["ps_dn"] = nc.alloc_psum_tensor("ps_dn", [128, 1024], F32)
    return B


def build_l1ffn_bufs(nc, B):
    B["g_gwT"] = nc.alloc_sbuf_tensor("g_gwT", [128, 64], F32)
    B["g_brow"] = nc.alloc_sbuf_tensor("g_brow", [1, 24], F32)
    B["g_grow"] = nc.alloc_sbuf_tensor("g_grow", [128, 8], F32)
    B["g_biasbc"] = nc.alloc_sbuf_tensor("g_biasbc", [128, 16], F32)
    B["g_rstdtok"] = nc.alloc_sbuf_tensor("g_rstdtok", [128, 128], F32)
    B["g_mask"] = nc.alloc_sbuf_tensor("g_mask", [128, 16 * 8], F32)
    B["g_maskfm"] = nc.alloc_sbuf_tensor("g_maskfm", [8, T], F32)
    B["g_l"] = nc.alloc_sbuf_tensor("g_l", [128, 8], F32)
    B["g_mx"] = nc.alloc_sbuf_tensor("g_mx", [128, 1], F32)
    B["f_b1cols"] = nc.alloc_sbuf_tensor("f_b1cols", [128, 8 * 32], F32)
    B["f_b2cols"] = nc.alloc_sbuf_tensor("f_b2cols", [128, 8 * 8], F32)
    B["f_w1c"] = [nc.alloc_sbuf_tensor(f"f_w1c{i}", [128, 4096], BF16) for i in range(2)]
    B["f_w2c"] = [nc.alloc_sbuf_tensor(f"f_w2c{i}", [128, 2 * 1024], BF16) for i in range(2)]
    B["f_h"] = nc.alloc_sbuf_tensor("f_h", [128, 32 * 512], BF16)
    B["f_mbc"] = nc.alloc_sbuf_tensor("f_mbc", [128, 512], F32)
    B["g_lvdump"] = nc.alloc_sbuf_tensor("g_lvdump", [128, 128], F32)
    B["g_stok"] = nc.alloc_sbuf_tensor("g_stok", [128, 128], F32)
    B["g_gterm"] = nc.alloc_sbuf_tensor("g_gterm", [128, 8], F32)
    B["g_maskT"] = nc.alloc_sbuf_tensor("g_maskT", [128, 128], F32)
    B["sel8_sb"] = nc.alloc_sbuf_tensor("sel8_sb", [8, 8 * 128], F32)
    B["cast_b"] = [B["y_bf"][:, 0:2048], B["y_bf"][:, 2048:4096]]


def declare_l1ffn_weights(nc, pfx=""):
    W = {}
    W["gate_wT"] = nc.dram_tensor(pfx + "gate_wT", [D, 8], F32, kind="ExternalInput").ap()
    W["gate_b"] = nc.dram_tensor(pfx + "gate_b", [8], F32, kind="ExternalInput").ap()
    W["expert_bias"] = nc.dram_tensor(pfx + "expert_bias", [8], F32, kind="ExternalInput").ap()
    W["w1"] = nc.dram_tensor(pfx + "w1", [8, D, 4096], F32, kind="ExternalInput").ap()
    W["b1"] = nc.dram_tensor(pfx + "b1", [8, 4096], F32, kind="ExternalInput").ap()
    W["w2"] = nc.dram_tensor(pfx + "w2", [8, 4096, D], F32, kind="ExternalInput").ap()
    W["b2"] = nc.dram_tensor(pfx + "b2", [8, D], F32, kind="ExternalInput").ap()
    W["w1cache"] = nc.dram_tensor(pfx + "w1cache", [8, 8, 1024, 512], BF16, kind="Internal").ap()
    W["w2cache"] = nc.dram_tensor(pfx + "w2cache", [8, 32, 128, 1024], BF16, kind="Internal").ap()
    W["mask_scratch"] = nc.dram_tensor(pfx + "mask_scratch", [128, 128], F32, kind="Internal").ap()
    return W


def l1ffn_in_map(inputs, shard):
    return {
        "x_in": np.ascontiguousarray(shard),
        "gate_wT": np.ascontiguousarray(inputs["L1_ffn_gate_w"].T),
        "gate_b": inputs["L1_ffn_gate_b"],
        "expert_bias": inputs["L1_ffn_expert_bias"],
        "w1": inputs["L1_ffn_w1"],
        "b1": inputs["L1_ffn_b1"],
        "w2": inputs["L1_ffn_w2"],
        "b2": inputs["L1_ffn_b2"],
    }


def build_l1ffn_program(debug=False, parts="full"):
    nc = bass.Bass("TRN2", target_bir_lowering=False)
    x_in = nc.dram_tensor("x_in", [T, D], F32, kind="ExternalInput").ap()
    x_out = nc.dram_tensor("x_out", [T, D], F32, kind="ExternalOutput").ap()
    W = declare_l1ffn_weights(nc)
    k = Ctx(nc)
    B = build_common(nc, k)
    build_l1ffn_bufs(nc, B)
    if debug:
        B["dbg"] = {
            "x": nc.dram_tensor("dbg_x", [128, NCH * T], F32, kind="ExternalOutput").ap(),
            "y": nc.dram_tensor("dbg_y", [128, NCH * T], BF16, kind="ExternalOutput").ap(),
            "mask": nc.dram_tensor("dbg_mask", [128, 128], F32, kind="ExternalOutput").ap(),
            "h": nc.dram_tensor("dbg_h", [128, 32 * 512], BF16, kind="ExternalOutput").ap(),
            "m": nc.dram_tensor("dbg_m", [128, T], F32, kind="ExternalOutput").ap(),
            "rstd": nc.dram_tensor("dbg_rstd", [128, T], F32, kind="ExternalOutput").ap(),
            "lv": nc.dram_tensor("dbg_lv", [128, 16 * 8], F32, kind="ExternalOutput").ap(),

            "rstdtok": nc.dram_tensor("dbg_rstdtok", [128, 16], F32, kind="ExternalOutput").ap(),
            "mt": nc.dram_tensor("dbg_mt", [8, T], F32, kind="ExternalOutput").ap(),
            "mx": nc.dram_tensor("dbg_mx", [128, 16], F32, kind="ExternalOutput").ap(),
        }

    if parts in ("full", "castln", "castgate"):
        t_cast = emit_cast_l1ffn(k, B, W)
    else:
        t_cast = ()
    t_load = emit_load_x(k, B, x_in)
    if debug:
        k.wait("pool", t_load)
        i = k.eng["pool"].dma_start(out=B["dbg"]["x"], in_=B["x_fm"][:])
        k.inc("pool", i, 16)
    after = (t_load,) + tuple(t_cast)
    if parts in ("ln", "castln"):
        m_b, rstd_b, ln_toks = emit_ln(k, B, after)
        if debug:
            k.wait("pool", *ln_toks)
            i = k.eng["pool"].dma_start(out=B["dbg"]["y"], in_=B["y_bf"][:])
            k.inc("pool", i, 16)
            i = k.eng["pool"].dma_start(out=B["dbg"]["m"], in_=m_b[:])
            k.inc("pool", i, 16)
            i = k.eng["pool"].dma_start(out=B["dbg"]["rstd"], in_=rstd_b[:])
            k.inc("pool", i, 16)
        toks = tuple(ln_toks)
    else:
        toks = emit_l1ffn(k, B, W, after, dbg=B.get("dbg"),
                          parts="gate" if parts in ("gate", "gate1", "castgate") else "full")
    t_store = emit_store_x(k, B, x_out, toks)
    k.eng["sp"].wait_ge(k.sem["sp"], k.cnt["sp"])
    k.eng["pool"].wait_ge(k.sem["pool"], k.cnt["pool"])
    return nc



def make_l0attn_consts(nc):
    import ml_dtypes
    c = {}
    j = np.arange(T, dtype=np.float32)
    c["iota_rep"] = nc.inline_tensor(_np_bf16(np.tile(j[None, :1024], (128, 1))), "c_iota_rep")
    aug = np.zeros((16, 4, T), np.float32)
    for h in range(16):
        slope = 2.0 ** (-(h + 1) / 2.0)
        v = (slope * j).astype(np.float64)
        h1 = v.astype(ml_dtypes.bfloat16).astype(np.float64)
        h2 = (v - h1).astype(np.float32).astype(ml_dtypes.bfloat16).astype(np.float64)
        h3 = (v - h1 - h2).astype(np.float32)
        aug[h, 0] = 1.0
        aug[h, 1] = h1.astype(np.float32)
        aug[h, 2] = h2.astype(np.float32)
        aug[h, 3] = h3
    c["aug_k"] = nc.inline_tensor(_np_bf16(aug.reshape(64, T)), "c_aug_k")
    mask = np.zeros((4, 128, 512), np.float32)
    for d in range(4):
        jl = np.arange(128)[:, None] + 128 * d
        ql = np.arange(512)[None, :]
        mask[d] = np.where(jl > ql, -1e9, 0.0)
    c["cmask"] = nc.inline_tensor(_np_bf16(mask.reshape(512, 512)), "c_cmask")
    c["ident_bf"] = nc.inline_tensor(_np_bf16(np.eye(128)), "c_identbf_a")
    c["ones_bf"] = nc.inline_tensor(_np_bf16(np.ones((128, 128))), "c_ones_bf")
    bo = np.zeros((128, 128), np.float32)
    bo[0:64, 0:64] = 1.0
    bo[64:128, 64:128] = 1.0
    c["blockones"] = nc.inline_tensor(bo, "c_blockones")
    c["ones3"] = nc.inline_tensor(_np_bf16(np.ones((3, T))), "c_ones3")
    return c


def build_l0attn_bufs(nc, B):
    B["a_qT"] = nc.alloc_sbuf_tensor("a_qT", [128, 8 * T], BF16)
    B["a_kT"] = nc.alloc_sbuf_tensor("a_kT", [128, 2 * T], BF16)
    B["a_vtok"] = nc.alloc_sbuf_tensor("a_vtok", [128, 4 * 16 * 64], BF16)
    B["a_wbuf"] = nc.alloc_sbuf_tensor("a_wbuf", [128, 8 * 1024], BF16)
    B["a_kaug"] = nc.alloc_sbuf_tensor("a_kaug", [4, T], BF16)
    B["a_qaug"] = nc.alloc_sbuf_tensor("a_qaug", [4, T], BF16)
    B["a_qscr"] = nc.alloc_sbuf_tensor("a_qscr", [64, 512], BF16)
    B["a_kscr"] = nc.alloc_sbuf_tensor("a_kscr", [64, T], BF16)
    B["a_stexp"] = nc.alloc_sbuf_tensor("a_stexp", [128, 2 * 512], BF16)
    B["a_iota"] = nc.alloc_sbuf_tensor("a_iota", [128, 1024], BF16)
    B["a_onesbf"] = nc.alloc_sbuf_tensor("a_onesbf", [128, 128], BF16)
    B["a_blockones"] = nc.alloc_sbuf_tensor("a_blockones", [128, 128], F32)
    B["a_cmask"] = nc.alloc_sbuf_tensor("a_cmask", [128, 4 * 512], BF16)
    B["a_identbf"] = nc.alloc_sbuf_tensor("a_identbf", [128, 128], BF16)
    B["a_mcol"] = nc.alloc_sbuf_tensor("a_mcol", [128, 18], F32)


def emit_cast_pairs(k, B, pairs, scale=None, engines=("dve", "pool")):
    sp = k.eng["sp"]
    engs = [(e, k.eng[e]) for e in engines]
    fbuf = [B["scr"][0], B["scr"][1]]
    bbuf = B["cast_b"]
    t_store = {}
    t_cast = {}
    for step, pr in enumerate(pairs):
        src, dst = pr[0], pr[1]
        sc = pr[2] if len(pr) > 2 else None
        ncols = src.shape[-1]
        bi = step % 2
        prev = t_cast.get(step - 2)
        if prev is not None:
            k.wait("sp", prev)
        i = sp.dma_start(out=fbuf[bi][:, 0:ncols], in_=src)
        t_l = k.inc("sp", i, 16)
        en, eng = engs[step % len(engs)]
        k.wait(en, t_l, t_store.get(step - 2))
        if sc is None:
            i = eng.tensor_copy(bbuf[bi][:, 0:ncols], fbuf[bi][:, 0:ncols])
        else:
            i = eng.tensor_scalar(bbuf[bi][:, 0:ncols], fbuf[bi][:, 0:ncols],
                                  float(sc), None, ALU.mult)
        t_cast[step] = k.inc(en, i)
        k.wait("sp", t_cast[step])
        i = sp.dma_start(out=dst, in_=bbuf[bi][:, 0:ncols])
        t_store[step] = k.inc("sp", i, 16)
    return (t_store[len(pairs) - 1], t_cast[len(pairs) - 1],
            k.tok("dve"), k.tok("pool"))


def emit_cast_l0attn(k, B, W):
    pairs = []
    for kc in range(8):
        pairs.append((W["wqT"][kc * 128:(kc + 1) * 128, :],
                      W["wqT_c"][kc * 128:(kc + 1) * 128, :], 0.125))
        pairs.append((W["wkvT"][kc * 128:(kc + 1) * 128, :],
                      W["wkvT_c"][kc * 128:(kc + 1) * 128, :]))
        pairs.append((W["woT"][kc * 128:(kc + 1) * 128, :],
                      W["woT_c"][kc * 128:(kc + 1) * 128, :]))
    return emit_cast_pairs(k, B, pairs)


def emit_l0attn(k, B, W, after, dbg=None):
    nc = k.nc
    pe = k.eng["pe"]
    act, dve, pool, sp = k.eng["act"], k.eng["dve"], k.eng["pool"], k.eng["sp"]
    cs = B["cs"]
    x_fm, y_bf, scr = B["x_fm"], B["y_bf"], B["scr"]
    ps_a, ps_b, ps_up, ps_dn = B["ps_a"], B["ps_b"], B["ps_up"], B["ps_dn"]
    qT, kT, vtok = B["a_qT"], B["a_kT"], B["a_vtok"]
    wbuf = B["a_wbuf"]
    kvw = wbuf

    m_b, rstd_b, ln_toks = emit_ln(k, B, after)

    i = sp.dma_start(out=B["a_iota"][:], in_=W["iota_rep"].ap())
    k.inc("sp", i, 16)
    i = sp.dma_start(out=B["a_onesbf"][:], in_=W["ones_bf"].ap())
    k.inc("sp", i, 16)
    i = sp.dma_start(out=B["a_blockones"][:], in_=W["blockones"].ap())
    k.inc("sp", i, 16)
    i = sp.dma_start(out=B["a_cmask"][:],
                     in_=W["cmask"].ap().rearrange("(d p) m -> p d m", p=128))
    k.inc("sp", i, 16)
    i = sp.dma_start(out=B["a_identbf"][:], in_=W["ident_bf"].ap())
    t_csts = k.inc("sp", i, 16)

    i = sp.dma_start(out=wbuf[:], in_=W["wqT_c"].rearrange("(c p) m -> p c m", p=128))
    t_wq = k.inc("sp", i, 16)

    t_evac = {}
    k.wait("pe", t_wq, t_csts, *ln_toks)
    pj = 0
    for fc in range(8):
        for tsub in range(4):
            pcol = (pj % 2) * 512
            prev = t_evac.get(pj - 2)
            if prev is not None:
                k.wait("pe", prev)
            for kc in range(NCH):
                inst = pe.matmul(ps_b[:, pcol:pcol + 512],
                                 wbuf[:, kc * 1024 + fc * 128: kc * 1024 + (fc + 1) * 128],
                                 y_bf[:, kc * T + tsub * 512: kc * T + tsub * 512 + 512],
                                 start=(kc == 0), stop=(kc == NCH - 1))
            t_mm = k.inc("pe", inst)
            en, eng = ("act", act) if pj % 2 == 0 else ("dve", dve)
            k.wait(en, t_mm)
            if en == "act":
                i = act.activation(qT[:, fc * T + tsub * 512: fc * T + tsub * 512 + 512],
                                   ps_b[:, pcol:pcol + 512], AF.Copy)
            else:
                i = dve.tensor_copy(qT[:, fc * T + tsub * 512: fc * T + tsub * 512 + 512],
                                    ps_b[:, pcol:pcol + 512])
            t_evac[pj] = k.inc(en, i)
            pj += 1
    k.wait("sp", k.tok("pe"), W.get("bg_a0"))
    i = sp.dma_start(out=wbuf[:, 0:4096], in_=W["wkvT_c"].rearrange("(c p) m -> p c m", p=128))
    t_wkv = k.inc("sp", i, 16)
    k.wait("pe", t_wkv)
    for g in range(4):
        for tsub in range(4):
            pcol = (pj % 2) * 512
            prev = t_evac.get(pj - 2)
            if prev is not None:
                k.wait("pe", prev)
            for kc in range(NCH):
                inst = pe.matmul(ps_b[0:64, pcol:pcol + 512],
                                 kvw[:, kc * 512 + g * 128: kc * 512 + g * 128 + 64],
                                 y_bf[:, kc * T + tsub * 512: kc * T + tsub * 512 + 512],
                                 start=(kc == 0), stop=(kc == NCH - 1))
            t_mm = k.inc("pe", inst)
            en, eng = ("act", act) if pj % 2 == 0 else ("dve", dve)
            k.wait(en, t_mm)
            dstr = kT[(g % 2) * 64:(g % 2) * 64 + 64,
                      (g // 2) * T + tsub * 512:(g // 2) * T + tsub * 512 + 512]
            if en == "act":
                i = act.activation(dstr, ps_b[0:64, pcol:pcol + 512], AF.Copy)
            else:
                i = dve.tensor_copy(dstr, ps_b[0:64, pcol:pcol + 512])
            t_evac[pj] = k.inc(en, i)
            pj += 1
    i = pool.memset(vtok[:], 1.0)
    t_vset = k.inc("pool", i)
    for g in range(4):
        for tc in range(NTOK):
            pcol = (pj % 2) * 512
            prev = t_evac.get(pj - 2)
            if prev is not None:
                k.wait("pe", prev)
            for kc in range(NCH):
                inst = pe.matmul(ps_b[:, pcol:pcol + 64],
                                 y_bf[:, kc * T + tc * 128: kc * T + (tc + 1) * 128],
                                 kvw[:, kc * 512 + g * 128 + 64: kc * 512 + (g + 1) * 128],
                                 start=(kc == 0), stop=(kc == NCH - 1))
            t_mm = k.inc("pe", inst)
            en, eng = ("act", act) if pj % 2 == 0 else ("dve", dve)
            k.wait(en, t_mm, t_vset)
            base = (g * 16 + tc) * 64
            if en == "act":
                i = act.activation(vtok[:, base:base + 64], ps_b[:, pcol:pcol + 64], AF.Copy)
            else:
                i = dve.tensor_copy(vtok[:, base:base + 64], ps_b[:, pcol:pcol + 64])
            t_evac[pj] = k.inc(en, i)
            pj += 1
    t_proj_pe = k.tok("pe")
    t_proj = (k.tok("act"), k.tok("dve"))

    mcol = B["a_mcol"]
    sqt = B["xio"]
    def norm_stats(src_chunks, lhs_ones, outcol):
        t_sq = None
        t_entry_dve = k.tok("dve")
        t_entry_pe = k.tok("pe")
        for ci, ch in enumerate(src_chunks):
            en, eng = ("act", act) if ci % 2 == 0 else ("dve", dve)
            k.wait(en, t_sq, t_entry_pe)
            if en == "act":
                i = act.activation(sqt[:], ch, AF.Square)
            else:
                i = dve.tensor_mul(sqt[:], ch, ch)
            t_sqc = k.inc(en, i)
            k.wait("pe", t_sqc, t_entry_dve)
            for ts4 in range(4):
                tile = (ps_a if ts4 < 2 else ps_b)[:, (ts4 % 2) * 512:(ts4 % 2) * 512 + 512]
                inst = pe.matmul(tile, lhs_ones,
                                 sqt[:, ts4 * 512:(ts4 + 1) * 512],
                                 start=(ci == 0), stop=(ci == len(src_chunks) - 1))
            t_sq = k.inc("pe", inst)
        k.wait("dve", t_sq)
        for ts4 in range(4):
            tile = (ps_a if ts4 < 2 else ps_b)[:, (ts4 % 2) * 512:(ts4 % 2) * 512 + 512]
            i = dve.tensor_reduce(mcol[:, ts4:ts4 + 1], tile, AX.X, ALU.max)
            k.inc("dve", i)
        i = dve.tensor_reduce(mcol[:, outcol:outcol + 1], mcol[:, 0:4], AX.X, ALU.max)
        return k.inc("dve", i)
    k.wait("dve", *t_proj)
    k.wait("act", *t_proj)
    t_q2 = norm_stats([qT[:, c * T:(c + 1) * T] for c in range(8)],
                      cs["ones128"][:], 8)
    t_k2 = norm_stats([kT[:, c * T:(c + 1) * T] for c in range(2)],
                      B["a_blockones"][:], 9)
    rep = scr[0]
    k.wait("dve", t_q2, t_k2)
    i = dve.tensor_scalar(rep[:, 0:128], cs["ones128"][:], mcol[:, 8:9], None, ALU.mult)
    k.inc("dve", i)
    i = dve.tensor_scalar(rep[:, 128:256], cs["ones128"][:], mcol[:, 9:10], None, ALU.mult)
    t_rep = k.inc("dve", i)
    k.wait("pe", t_rep)
    inst = pe.transpose(ps_a[:, 0:128], rep[:, 0:128], cs["ident_f32"][:])
    k.inc("pe", inst)
    inst = pe.transpose(ps_a[:, 512:640], rep[:, 128:256], cs["ident_f32"][:])
    t_tr = k.inc("pe", inst)
    k.wait("dve", t_tr)
    i = dve.tensor_copy(rep[:, 0:128], ps_a[:, 0:128])
    k.inc("dve", i)
    i = dve.tensor_copy(rep[:, 128:256], ps_a[:, 512:640])
    k.inc("dve", i)
    i = dve.tensor_reduce(mcol[:, 10:11], rep[:, 0:128], AX.X, ALU.max)
    k.inc("dve", i)
    i = dve.tensor_reduce(mcol[:, 11:12], rep[:, 128:256], AX.X, ALU.max)
    k.inc("dve", i)
    i = dve.tensor_scalar(mcol[:, 12:13], mcol[:, 10:11], mcol[:, 11:12], None, ALU.mult)
    t_mm2 = k.inc("dve", i)
    k.wait("act", t_mm2)
    i = act.activation(mcol[:, 13:14], mcol[:, 12:13], AF.Sqrt)
    t_sqr = k.inc("act", i)
    k.wait("dve", t_sqr)
    i = dve.tensor_scalar(mcol[:, 14:15], mcol[:, 13:14], 1.25, None, ALU.mult)
    t_M = k.inc("dve", i)

    if dbg is not None:
        k.wait("pool", t_M, *t_proj)
        i = pool.dma_start(out=dbg["mcol"], in_=B["a_mcol"][:])
        k.inc("pool", i, 16)
        i = pool.dma_start(out=dbg["qT"], in_=qT[:])
        k.inc("pool", i, 16)
        i = pool.dma_start(out=dbg["kT"], in_=kT[:])
        k.inc("pool", i, 16)
        i = pool.dma_start(out=dbg["vtok"], in_=vtok[:])
        k.inc("pool", i, 16)

    attnout = y_bf
    kaug, qaug, qscr = B["a_kaug"], B["a_qaug"], B["a_qscr"]
    stexp = B["a_stexp"]
    recip = B["xio"]
    gbf = stexp
    t_head_dma = None
    t_ev_q = None
    t_ost = {}
    t_stread = {}
    st_war = {0: None, 1: None}
    t_wo_dma = None
    t_kscr_read = None
    for h in range(16):
        g = h // 4
        slope = 2.0 ** (-(h + 1) / 2.0)
        if g % 2 == 1 and h % 4 == 0:
            k.wait("dve", t_kscr_read, *t_proj)
            i = dve.tensor_copy(B["a_kscr"][:], kT[64:128, (g // 2) * T:(g // 2 + 1) * T])
            t_kscr = k.inc("dve", i)
        if g % 2 == 1:
            def krow(kt):
                return B["a_kscr"][:, kt * 128:(kt + 1) * 128]
        else:
            def krow(kt):
                return kT[0:64, (g // 2) * T + kt * 128:(g // 2) * T + (kt + 1) * 128]
        k.wait("sp", k.tok("dve"), k.tok("pe"))
        i = sp.dma_start(out=kaug[:], in_=W["aug_k"].ap()[h * 4:(h + 1) * 4, :])
        t_ka = k.inc("sp", i, 16)
        if h == 0:
            i = sp.dma_start(out=qaug[1:4, :], in_=W["ones3"].ap())
            k.inc("sp", i, 16)
        k.wait("dve", t_M, t_ka)
        i = dve.tensor_scalar(mcol[:, 16:17], mcol[:, 14:15], 1024.0 * slope, None, ALU.add)
        k.inc("dve", i)
        t_qa = None
        for gh in range(2):
            k.wait("dve", t_qa, t_kscr_read)
            i = dve.tensor_scalar(gbf[:, 0:1024], B["a_iota"][:],
                                  -slope, mcol[:, 14 + 2 * gh:15 + 2 * gh],
                                  ALU.mult, ALU.subtract)
            t_g = k.inc("dve", i)
            k.wait("sp", t_g)
            t_qa = k.inc("sp", sp.dma_start(
                out=qaug[0:1, gh * 1024:(gh + 1) * 1024], in_=gbf[0:1, 0:1024]), 16)
        qch = h // 2
        for qc in range(4):
            if h % 2 == 0:
                qrv = qT[0:64, qch * T + qc * 512: qch * T + qc * 512 + 512]
                t_ev_q = None
            else:
                k.wait("dve", t_stread.get("qscr"))
                i = dve.tensor_copy(qscr[:],
                                    qT[64:128, qch * T + qc * 512: qch * T + qc * 512 + 512])
                t_ev_q = k.inc("dve", i)
                qrv = qscr[:]
            ktmax = 4 * qc + 3
            st_i = 0
            o_ps = ps_up[:, (qc % 2) * 512:(qc % 2) * 512 + 512]
            r_ps = ps_dn[:, (qc % 2) * 512:(qc % 2) * 512 + 512]
            prev = t_ost.get((h * 4 + qc) - 2)
            if prev is not None:
                k.wait("pe", prev)
            for kt in range(ktmax + 1):
                st_ps = ps_a[:, (st_i % 2) * 512:(st_i % 2) * 512 + 512]
                k.wait("pe", st_war[st_i % 2])
                k.wait("pe", t_qa, (t_ev_q if h % 2 else None), t_proj_pe, t_M)
                if g % 2 == 1:
                    k.wait("pe", t_kscr)
                inst = pe.matmul(st_ps, krow(kt), qrv, start=True, stop=False)
                k.inc("pe", inst)
                delta = kt - 4 * qc
                last = not (0 <= delta <= 3)
                inst = pe.matmul(st_ps, kaug[:, kt * 128:(kt + 1) * 128],
                                 qaug[:, qc * 512:qc * 512 + 512],
                                 start=False, stop=last)
                k.inc("pe", inst)
                if not last:
                    inst = pe.matmul(st_ps, B["a_identbf"][:],
                                     B["a_cmask"][:, delta * 512:(delta + 1) * 512],
                                     start=False, stop=True)
                    k.inc("pe", inst)
                t_st = k.tok("pe")
                k.wait("act", t_st)
                i = act.activation(stexp[:, (st_i % 2) * 512:(st_i % 2) * 512 + 512],
                                   st_ps, AF.Exp)
                t_exp = k.inc("act", i)
                k.wait("pe", t_exp)
                vbase = (g * 16 + kt) * 64
                inst = pe.matmul(o_ps[0:64, :], vtok[:, vbase:vbase + 64],
                                 stexp[:, (st_i % 2) * 512:(st_i % 2) * 512 + 512],
                                 start=(kt == 0), stop=(kt == ktmax))
                k.inc("pe", inst)
                inst = pe.matmul(r_ps, B["a_onesbf"][:],
                                 stexp[:, (st_i % 2) * 512:(st_i % 2) * 512 + 512],
                                 start=(kt == 0), stop=(kt == ktmax))
                st_war[st_i % 2] = t_exp
                t_stread[st_i] = k.inc("pe", inst)
                st_i += 1
            t_pv = k.tok("pe")
            k.wait("dve", t_pv)
            i = dve.reciprocal(recip[:, 0:512], r_ps)
            k.inc("dve", i)
            i = dve.tensor_mul(scr[0][0:64, 0:512], o_ps[0:64, 0:512], recip[0:64, 0:512])
            k.inc("dve", i)
            dst = attnout[:, qch * T + qc * 512: qch * T + qc * 512 + 512]
            if h % 2 == 0:
                i = dve.tensor_copy(attnout[0:64, qch * T + qc * 512: qch * T + qc * 512 + 512],
                                    scr[0][0:64, 0:512])
            else:
                i = dve.tensor_copy(attnout[64:128, qch * T + qc * 512: qch * T + qc * 512 + 512],
                                    scr[0][0:64, 0:512])
            t_ost[h * 4 + qc] = k.inc("dve", i)
            t_stread["qscr"] = k.tok("pe")
            t_kscr_read = k.tok("pe")

    t_attn_done = (k.tok("dve"), k.tok("act"))
    if dbg is not None:
        k.wait("pool", *t_attn_done)
        i = pool.dma_start(out=dbg["ao"], in_=attnout[:])
        k.inc("pool", i, 16)
    k.wait("sp", k.tok("pe"))
    i = sp.dma_start(out=wbuf[:], in_=W["woT_c"].rearrange("(c p) m -> p c m", p=128))
    t_wo = k.inc("sp", i, 16)
    pj = 0
    t_woev = {}
    for fc in range(8):
        for tsub in range(4):
            pcol = (pj % 2) * 512
            prev = t_woev.get(pj - 2)
            if prev is not None:
                k.wait("pe", prev)
            k.wait("pe", t_wo, *t_attn_done)
            for kc in range(NCH):
                inst = pe.matmul(ps_b[:, pcol:pcol + 512],
                                 wbuf[:, kc * 1024 + fc * 128: kc * 1024 + (fc + 1) * 128],
                                 attnout[:, kc * T + tsub * 512: kc * T + tsub * 512 + 512],
                                 start=(kc == 0), stop=(kc == NCH - 1))
            t_mm = k.inc("pe", inst)
            k.wait("dve", t_mm)
            i = dve.tensor_add(x_fm[:, fc * T + tsub * 512: fc * T + tsub * 512 + 512],
                               x_fm[:, fc * T + tsub * 512: fc * T + tsub * 512 + 512],
                               ps_b[:, pcol:pcol + 512])
            t_woev[pj] = k.inc("dve", i)
            pj += 1
    return (k.tok("dve"),)


def declare_l0attn_weights(nc, pfx=""):
    W = {}
    W["wqT"] = nc.dram_tensor(pfx + "wqT", [D, D], F32, kind="ExternalInput").ap()
    W["wkvT"] = nc.dram_tensor(pfx + "wkvT", [D, 512], F32, kind="ExternalInput").ap()
    W["woT"] = nc.dram_tensor(pfx + "woT", [D, D], F32, kind="ExternalInput").ap()
    W["wqT_c"] = nc.dram_tensor(pfx + "wqT_c", [D, D], BF16, kind="Internal").ap()
    W["wkvT_c"] = nc.dram_tensor(pfx + "wkvT_c", [D, 512], BF16, kind="Internal").ap()
    W["woT_c"] = nc.dram_tensor(pfx + "woT_c", [D, D], BF16, kind="Internal").ap()
    return W


def l0attn_in_map(inputs, shard):
    return {
        "x_in": np.ascontiguousarray(shard),
        "wqT": np.ascontiguousarray(inputs["L0_attn_Wq"].T),
        "wkvT": np.ascontiguousarray(inputs["L0_attn_Wkv"].T),
        "woT": np.ascontiguousarray(inputs["L0_attn_Wo"].T),
    }


def build_l0attn_program(debug=False):
    nc = bass.Bass("TRN2", target_bir_lowering=False)
    x_in = nc.dram_tensor("x_in", [T, D], F32, kind="ExternalInput").ap()
    x_out = nc.dram_tensor("x_out", [T, D], F32, kind="ExternalOutput").ap()
    W = declare_l0attn_weights(nc)
    W.update(make_l0attn_consts(nc))
    k = Ctx(nc)
    B = build_common(nc, k)
    B["cast_b"] = [B["y_bf"][:, 0:2048], B["y_bf"][:, 2048:4096]]
    build_l0attn_bufs(nc, B)
    if debug:
        B["dbg"] = {
            "mcol": nc.dram_tensor("dbg_mcol", [128, 18], F32, kind="ExternalOutput").ap(),
            "qT": nc.dram_tensor("dbg_qT", [128, 8 * T], BF16, kind="ExternalOutput").ap(),
            "kT": nc.dram_tensor("dbg_kT", [128, 2 * T], BF16, kind="ExternalOutput").ap(),
            "vtok": nc.dram_tensor("dbg_vtok", [128, 4096], BF16, kind="ExternalOutput").ap(),
            "ao": nc.dram_tensor("dbg_ao", [128, 8 * T], BF16, kind="ExternalOutput").ap(),
        }
    t_cast = emit_cast_l0attn(k, B, W)
    t_load = emit_load_x(k, B, x_in)
    after = (t_load,) + tuple(t_cast)
    toks = emit_l0attn(k, B, W, after, dbg=B.get("dbg"))
    t_store = emit_store_x(k, B, x_out, toks)
    k.eng["sp"].wait_ge(k.sem["sp"], k.cnt["sp"])
    k.eng["pool"].wait_ge(k.sem["pool"], k.cnt["pool"])
    return nc


def make_l1attn_consts(nc):
    c = {}
    sel = np.zeros((32, 32 * 128), np.float32)
    for r in range(32):
        sel[r, r * 128:(r + 1) * 128] = 1.0
    c["sel32"] = nc.inline_tensor(_np_bf16(sel), "c_sel32")
    return c


def build_l1attn_bufs(nc, B):
    B["m_scr"] = [nc.alloc_sbuf_tensor(f"m_scr{i}", [128, T], F32) for i in range(5)]
    B["m_conv"] = nc.alloc_sbuf_tensor("m_conv", [128, T + 8], F32)
    B["ln2_R"] = B["m_scr"][0]
    B["m_bf"] = nc.alloc_sbuf_tensor("m_bf", [128, T], BF16)
    B["m_dtr"] = nc.alloc_sbuf_tensor("m_dtr", [128, T], BF16)
    B["m_BC"] = nc.alloc_sbuf_tensor("m_BC", [32, T], BF16)
    B["m_dtw"] = nc.alloc_sbuf_tensor("m_dtw", [128, T], BF16)
    B["m_xpw"] = nc.alloc_sbuf_tensor("m_xpw", [128, 16 * 160], BF16)
    B["m_wst"] = [nc.alloc_sbuf_tensor(f"m_wst{i}", [128, T], BF16) for i in range(2)]
    B["m_sel32r"] = nc.alloc_sbuf_tensor("m_sel32r", [32, 4096], BF16)
    B["m_ywin"] = nc.alloc_sbuf_tensor("m_ywin", [128, 16 * 128], BF16)
    B["m_cols"] = nc.alloc_sbuf_tensor("m_cols", [128, 256 + 64 + 16 * 4 + 16], F32)


def declare_l1attn_weights(nc, pfx=""):
    W = {}
    W["ln_gamma"] = nc.dram_tensor(pfx + "ln_gamma", [D], F32, kind="ExternalInput").ap()
    W["ln_beta"] = nc.dram_tensor(pfx + "ln_beta", [D], F32, kind="ExternalInput").ap()
    W["in_proj_w"] = nc.dram_tensor(pfx + "in_proj_w", [D, 4096], F32, kind="ExternalInput").ap()
    W["conv_w"] = nc.dram_tensor(pfx + "conv_w", [2048, 4], F32, kind="ExternalInput").ap()
    W["conv_b"] = nc.dram_tensor(pfx + "conv_b", [2048], F32, kind="ExternalInput").ap()
    W["x_proj_w"] = nc.dram_tensor(pfx + "x_proj_w", [2048, 160], F32, kind="ExternalInput").ap()
    W["dt_proj_w"] = nc.dram_tensor(pfx + "dt_proj_w", [128, 2048], F32, kind="ExternalInput").ap()
    W["dt_proj_b"] = nc.dram_tensor(pfx + "dt_proj_b", [2048], F32, kind="ExternalInput").ap()
    W["A_log"] = nc.dram_tensor(pfx + "A_log", [2048, 16], F32, kind="ExternalInput").ap()
    W["D_param"] = nc.dram_tensor(pfx + "D_param", [2048], F32, kind="ExternalInput").ap()
    W["out_proj_w"] = nc.dram_tensor(pfx + "out_proj_w", [2048, D], F32, kind="ExternalInput").ap()
    W["in_cache"] = nc.dram_tensor(pfx + "in_cache", [32, D, 128], BF16, kind="Internal").ap()
    W["xp_cache"] = nc.dram_tensor(pfx + "xp_cache", [2048, 160], BF16, kind="Internal").ap()
    W["dt_cache"] = nc.dram_tensor(pfx + "dt_cache", [128, 2048], BF16, kind="Internal").ap()
    W["out_cache"] = nc.dram_tensor(pfx + "out_cache", [8, 2048, 128], BF16, kind="Internal").ap()
    W["xb_dram"] = nc.dram_tensor(pfx + "xb_dram", [16, 128, T], BF16, kind="Internal").ap()
    W["yb_dram"] = nc.dram_tensor(pfx + "yb_dram", [16, 128, T], BF16, kind="Internal").ap()
    return W


def emit_cast_l1attn(k, B, W):
    pairs = []
    for kc in range(8):
        for half in range(2):
            pairs.append((W["in_proj_w"][kc * 128:(kc + 1) * 128, half * 2048:(half + 1) * 2048],
                          W["in_cache"][half * 16:(half + 1) * 16, kc * 128:(kc + 1) * 128, :]
                          .transpose([1, 0, 2])))
    for kc in range(16):
        pairs.append((W["x_proj_w"][kc * 128:(kc + 1) * 128, :],
                      W["xp_cache"][kc * 128:(kc + 1) * 128, :]))
    pairs.append((W["dt_proj_w"], W["dt_cache"]))
    for kc in range(16):
        pairs.append((W["out_proj_w"][kc * 128:(kc + 1) * 128, :],
                      W["out_cache"][:, kc * 128:(kc + 1) * 128, :].transpose([1, 0, 2])))
    return emit_cast_pairs(k, B, pairs)


def emit_l1attn(k, B, W, after):
    nc = k.nc
    pe = k.eng["pe"]
    act, dve, pool, sp = k.eng["act"], k.eng["dve"], k.eng["pool"], k.eng["sp"]
    cs = B["cs"]
    x_fm, y_bf = B["x_fm"], B["y_bf"]
    ps_a, ps_b, ps_up, ps_dn = B["ps_a"], B["ps_b"], B["ps_up"], B["ps_dn"]
    scr = B["m_scr"]
    cols = B["m_cols"]
    mbf, dtr, BC = B["m_bf"], B["m_dtr"], B["m_BC"]
    wst, selr = B["m_wst"], B["m_sel32r"]

    i = sp.dma_start(out=cols[:, 0:256], in_=W["A_log"].rearrange("(c p) n -> p c n", p=128),
                     allow_slow_non_contiguous=True)
    t_A = k.inc("sp", i, 16)
    i = sp.dma_start(out=cols[:, 256:320], in_=W["conv_w"].rearrange("(c p) n -> p c n", p=128),
                     allow_slow_non_contiguous=True)
    k.inc("sp", i, 16)
    i = sp.dma_start(out=cols[:, 320:336], in_=W["conv_b"].rearrange("(c p) -> p c", p=128),
                     allow_slow_non_contiguous=True)
    k.inc("sp", i, 16)
    i = sp.dma_start(out=cols[:, 336:352], in_=W["D_param"].rearrange("(c p) -> p c", p=128),
                     allow_slow_non_contiguous=True)
    k.inc("sp", i, 16)
    i = sp.dma_start(out=cols[:, 352:368], in_=W["dt_proj_b"].rearrange("(c p) -> p c", p=128),
                     allow_slow_non_contiguous=True)
    k.inc("sp", i, 16)
    i = sp.dma_start(out=cols[:, 368:376], in_=W["ln_gamma"].rearrange("(c p) -> p c", p=128),
                     allow_slow_non_contiguous=True)
    k.inc("sp", i, 16)
    i = sp.dma_start(out=cols[:, 376:384], in_=W["ln_beta"].rearrange("(c p) -> p c", p=128),
                     allow_slow_non_contiguous=True)
    t_cols = k.inc("sp", i, 16)
    k.wait("sp", W.get("bg_m1"))
    i = sp.dma_start(out=B["m_dtw"][:], in_=W["dt_cache"])
    t_dtw = k.inc("sp", i, 16)
    i = sp.dma_start(out=B["m_xpw"][:], in_=W["xp_cache"].rearrange("(c p) m -> p c m", p=128))
    t_xpw = k.inc("sp", i, 16)

    k.wait("act", t_cols)
    i = act.activation(cols[:, 0:256], cols[:, 0:256], AF.Exp)
    t_an = k.inc("act", i)
    k.wait("dve", t_an)
    i = dve.tensor_scalar(cols[:, 0:256], cols[:, 0:256], -1.0, None, ALU.mult)
    t_anneg = k.inc("dve", i)

    m_b, rstd_b, ln_toks = emit_ln(k, B, after, affine=(cols[:, 368:376], cols[:, 376:384]),
                                   fold_ln2=True)

    conv = B["m_conv"]
    i = pool.memset(conv[:, 0:8], 0.0)
    t_ms0 = k.inc("pool", i)
    t_xb_dma = {}
    for c in range(16):
        bi = c % 2
        prev = t_xb_dma.get(c - 2)
        if prev is not None:
            k.wait("sp", prev)
        i = sp.dma_start(out=wst[bi][:, 0:1024],
                         in_=W["in_cache"][c].rearrange("(q p) m -> p q m", p=128))
        t_w = k.inc("sp", i, 16)
        k.wait("pe", t_w, *ln_toks, t_anneg)
        if c >= 1:
            k.wait("pe", t_cv)
        for ts4 in range(4):
            tile = (ps_a if ts4 < 2 else ps_b)[:, (ts4 % 2) * 512:(ts4 % 2) * 512 + 512]
            for kc in range(NCH):
                inst = pe.matmul(tile, wst[bi][:, kc * 128:(kc + 1) * 128],
                                 y_bf[:, kc * T + ts4 * 512: kc * T + ts4 * 512 + 512],
                                 start=(kc == 0), stop=(kc == NCH - 1))
        t_mm = k.inc("pe", inst)
        k.wait("dve", t_mm, t_ms0)
        for ts4 in range(4):
            tile = (ps_a if ts4 < 2 else ps_b)[:, (ts4 % 2) * 512:(ts4 % 2) * 512 + 512]
            i = dve.tensor_copy(conv[:, 4 + ts4 * 512: 4 + (ts4 + 1) * 512], tile)
        t_cv = k.inc("dve", i)
        k.wait("dve", t_cols)
        i = dve.tensor_scalar(scr[0][:], conv[:, 4:4 + T], cols[:, 256 + c * 4 + 3:256 + c * 4 + 4],
                              None, ALU.mult)
        k.inc("dve", i)
        i = dve.tensor_scalar(scr[1][:], conv[:, 3:3 + T], cols[:, 256 + c * 4 + 2:256 + c * 4 + 3],
                              None, ALU.mult)
        k.inc("dve", i)
        i = dve.tensor_add(scr[0][:], scr[0][:], scr[1][:])
        k.inc("dve", i)
        i = dve.tensor_scalar(scr[1][:], conv[:, 2:2 + T], cols[:, 256 + c * 4 + 1:256 + c * 4 + 2],
                              None, ALU.mult)
        k.inc("dve", i)
        i = dve.tensor_add(scr[0][:], scr[0][:], scr[1][:])
        k.inc("dve", i)
        i = dve.tensor_scalar(scr[1][:], conv[:, 1:1 + T], cols[:, 256 + c * 4:256 + c * 4 + 1],
                              None, ALU.mult)
        k.inc("dve", i)
        i = dve.tensor_add(scr[0][:], scr[0][:], scr[1][:])
        t_acc = k.inc("dve", i)
        k.wait("act", t_acc)
        i = act.activation(mbf[:], scr[0][:], AF.Silu, bias=cols[:, 320 + c:321 + c])
        t_xb = k.inc("act", i)
        k.wait("sp", t_xb)
        i = sp.dma_start(out=W["xb_dram"][c], in_=mbf[:])
        t_xb_dma[c] = k.inc("sp", i, 16)

    t_rhs = {}
    rj = 0
    for pass_i, (m0, mw) in enumerate([(0, 128), (128, 32)]):
        for ts4 in range(4):
            for c in range(16):
                bi = c % 2
                prev = t_rhs.get(rj - 2)
                if prev is not None:
                    k.wait("sp", prev)
                i = sp.dma_start(out=mbf[:, bi * 512:(bi + 1) * 512],
                                 in_=W["xb_dram"][c][:, ts4 * 512:(ts4 + 1) * 512])
                t_l = k.inc("sp", i, 16)
                k.wait("pe", t_l, t_xpw, t_xb_dma.get(15))
                if pass_i == 0:
                    tile = (ps_a if ts4 < 2 else ps_b)[:, (ts4 % 2) * 512:(ts4 % 2) * 512 + 512]
                else:
                    tile = (ps_a if ts4 < 2 else ps_b)[0:32, (ts4 % 2) * 512:(ts4 % 2) * 512 + 512]
                inst = pe.matmul(tile, B["m_xpw"][:, c * 160 + m0: c * 160 + m0 + mw],
                                 mbf[:, bi * 512:(bi + 1) * 512],
                                 start=(c == 0), stop=(c == 15))
                t_rhs[rj] = k.inc("pe", inst)
                rj += 1
        t_xpj = k.tok("pe")
        for ts4 in range(4):
            if pass_i == 0:
                tile = (ps_a if ts4 < 2 else ps_b)[:, (ts4 % 2) * 512:(ts4 % 2) * 512 + 512]
                en, eng = ("act", act), None
                k.wait("act", t_xpj)
                i = act.activation(dtr[:, ts4 * 512:(ts4 + 1) * 512], tile, AF.Copy)
                t_ev = k.inc("act", i)
            else:
                tile = (ps_a if ts4 < 2 else ps_b)[0:32, (ts4 % 2) * 512:(ts4 % 2) * 512 + 512]
                k.wait("act", t_xpj)
                i = act.activation(BC[:, ts4 * 512:(ts4 + 1) * 512], tile, AF.Copy)
                t_ev = k.inc("act", i)
        k.wait("pe", t_ev)

    slotA, slotB, slotC = scr[0], scr[3], None
    t_xpj_all = k.tok("pe")
    k.wait("sp", t_xpj_all)
    i = sp.dma_start(out=selr[:], in_=W["sel32"].ap())
    t_selr = k.inc("sp", i, 16)
    t_u = None
    t_yb_dma = {}
    for c in range(16):
        k.wait("pe", t_dtw, k.tok("dve"), k.tok("act"), k.tok("pool"))
        for ts4 in range(4):
            tile = (ps_a if ts4 < 2 else ps_b)[:, (ts4 % 2) * 512:(ts4 % 2) * 512 + 512]
            inst = pe.matmul(tile, B["m_dtw"][:, c * 128:(c + 1) * 128],
                             dtr[:, ts4 * 512:(ts4 + 1) * 512],
                             start=True, stop=True)
        t_dt = k.inc("pe", inst)
        dtb = cols[:, 352 + c:353 + c]
        for ts4 in range(4):
            tile = (ps_a if ts4 < 2 else ps_b)[:, (ts4 % 2) * 512:(ts4 % 2) * 512 + 512]
            w0 = ts4 * 512
            k.wait("act", t_dt)
            i = act.activation(slotA[:, w0:w0 + 512], tile, AF.Abs, bias=dtb)
            k.inc("act", i)
            i = act.activation(slotA[:, w0:w0 + 512], slotA[:, w0:w0 + 512], AF.Exp, scale=-1.0)
            k.inc("act", i)
            i = act.activation(slotA[:, w0:w0 + 512], slotA[:, w0:w0 + 512], AF.Ln,
                               bias=cs["epscols"][:, 2:3])
            k.inc("act", i)
            i = act.activation(slotB[:, w0:w0 + 512], tile, AF.Relu, bias=dtb)
            t_sp2 = k.inc("act", i)
            k.wait("dve", t_sp2)
            i = dve.tensor_add(scr[1][:, w0:w0 + 512], slotA[:, w0:w0 + 512],
                               slotB[:, w0:w0 + 512])
            t_delta = k.inc("dve", i)
        k.wait("sp", t_xpj_all, t_u)
        i = sp.dma_start(out=mbf[:], in_=W["xb_dram"][c])
        t_xbl = k.inc("sp", i, 16)
        k.wait("dve", t_xbl, t_delta)
        i = dve.tensor_mul(scr[2][:], scr[1][:], mbf[:])
        k.inc("dve", i)
        i = dve.tensor_scalar(scr[4][:], mbf[:], cols[:, 336 + c:337 + c], None, ALU.mult)
        t_u = k.inc("dve", i)
        t_dbx = None
        t_cev = None
        t_ymul = None
        t_yacc = None
        t_scan_prev = None
        for n in range(16):
            k.wait("pe", t_selr, t_u, t_dbx, t_cev)
            for w4 in range(4):
                tile = (ps_up if w4 < 2 else ps_dn)[:, (w4 % 2) * 512:(w4 % 2) * 512 + 512]
                inst = pe.matmul(tile, selr[:, n * 128:(n + 1) * 128],
                                 BC[:, w4 * 512:(w4 + 1) * 512],
                                 start=True, stop=True, skip_group_check=True)
            t_bb = k.inc("pe", inst)
            for w4 in range(4):
                tile = (ps_a if w4 < 2 else ps_b)[:, (w4 % 2) * 512:(w4 % 2) * 512 + 512]
                inst = pe.matmul(tile, selr[:, (16 + n) * 128:(17 + n) * 128],
                                 BC[:, w4 * 512:(w4 + 1) * 512],
                                 start=True, stop=True, skip_group_check=True)
            t_cb = k.inc("pe", inst)
            k.wait("act", t_scan_prev, t_delta)
            i = act.activation(slotA[:], scr[1][:], AF.Exp, scale=cols[:, c * 16 + n:c * 16 + n + 1])
            t_da = k.inc("act", i)
            k.wait("dve", t_bb, t_yacc)
            i = dve.tensor_mul(slotB[:, 0:1024], scr[2][:, 0:1024], ps_up[:])
            k.inc("dve", i)
            i = dve.tensor_mul(slotB[:, 1024:2048], scr[2][:, 1024:2048], ps_dn[:])
            t_dbx = k.inc("dve", i)
            slotC = conv
            k.wait("dve", t_da)
            for w4 in range(4):
                w0 = w4 * 512
                init = 0.0 if w4 == 0 else slotC[:, w0 - 1:w0]
                i = dve.tensor_tensor_scan(slotC[:, w0:w0 + 512], slotA[:, w0:w0 + 512],
                                           slotB[:, w0:w0 + 512], init, ALU.mult, ALU.add)
                t_scan = k.inc("dve", i)
            t_scan_prev = t_scan
            k.wait("act", t_cb, t_yacc, t_u)
            i = act.activation(mbf[:, 0:1024], ps_a[:], AF.Copy)
            k.inc("act", i)
            i = act.activation(mbf[:, 1024:2048], ps_b[:], AF.Copy)
            t_cev = k.inc("act", i)
            k.wait("pool", t_cev, t_scan)
            i = pool.tensor_mul(slotB[:], slotC[:, 0:T], mbf[:])
            k.inc("pool", i)
            i = pool.tensor_add(scr[4][:], scr[4][:], slotB[:])
            t_ymul = t_yacc = k.inc("pool", i)
        bi = c % 2
        i = sp.dma_start(out=wst[bi][:, 0:1024],
                         in_=W["in_cache"][16 + c].rearrange("(q p) m -> p q m", p=128))
        t_wz = k.inc("sp", i, 16)
        k.wait("pe", t_wz, t_dbx, t_ymul, t_yacc)
        for ts4 in range(4):
            tile = (ps_up if ts4 < 2 else ps_dn)[:, (ts4 % 2) * 512:(ts4 % 2) * 512 + 512]
            for kc in range(NCH):
                inst = pe.matmul(tile, wst[bi][:, kc * 128:(kc + 1) * 128],
                                 y_bf[:, kc * T + ts4 * 512: kc * T + ts4 * 512 + 512],
                                 start=(kc == 0), stop=(kc == NCH - 1))
        t_z = k.inc("pe", inst)
        k.wait("act", t_z, t_ymul)
        i = act.activation(slotA[:, 0:1024], ps_up[:], AF.Silu)
        k.inc("act", i)
        i = act.activation(slotA[:, 1024:2048], ps_dn[:], AF.Silu)
        t_zs = k.inc("act", i)
        k.wait("dve", t_zs, t_yacc)
        i = dve.tensor_mul(mbf[:], scr[4][:], slotA[:])
        t_yf = k.inc("dve", i)
        k.wait("sp", t_yf)
        i = sp.dma_start(out=W["yb_dram"][c], in_=mbf[:])
        t_yb_dma[c] = k.inc("sp", i, 16)

    t_res = {}
    for ch in range(NCH):
        i = dve.tensor_sub(B["xio"][:], x_fm[:, ch * T:(ch + 1) * T], m_b[:])
        k.inc("dve", i)
        i = dve.tensor_mul(B["xio"][:], B["xio"][:], rstd_b[:])
        k.inc("dve", i)
        i = dve.tensor_add(x_fm[:, ch * T:(ch + 1) * T], x_fm[:, ch * T:(ch + 1) * T],
                           B["xio"][:])
        t_res[ch] = k.inc("dve", i)

    ywin = B["m_ywin"]
    t_oc_ev = {}
    for ts8 in range(16):
        w0 = ts8 * 128
        k.wait("sp", t_yb_dma[15], t_oc_ev.get(ts8 - 1))
        for c4 in range(4):
            i = sp.dma_start(out=ywin[:, c4 * 512:(c4 + 1) * 512]
                             .rearrange("p (c m) -> p c m", c=4),
                             in_=W["yb_dram"][c4 * 4:(c4 + 1) * 4, :, w0:w0 + 128]
                             .transpose([1, 0, 2]))
            t_yw = k.inc("sp", i, 16)
        for oc in range(8):
            bi = oc % 2
            prev = t_oc_ev.get((ts8, oc - 2))
            if prev is not None:
                k.wait("sp", prev)
            i = sp.dma_start(out=wst[bi][:],
                             in_=W["out_cache"][oc].rearrange("(c p) m -> p c m", p=128))
            t_wo = k.inc("sp", i, 16)
            psc = ps_a[:, (oc % 2) * 512:(oc % 2) * 512 + 128]
            prev = t_oc_ev.get((ts8, oc - 2))
            if prev is not None:
                k.wait("pe", prev)
            k.wait("pe", t_wo, t_yw)
            for c in range(16):
                inst = pe.matmul(psc, wst[bi][:, c * 128:(c + 1) * 128],
                                 ywin[:, c * 128:(c + 1) * 128],
                                 start=(c == 0), stop=(c == 15))
            t_mm = k.inc("pe", inst)
            k.wait("dve", t_mm, t_res.get(oc))
            i = dve.tensor_add(x_fm[:, oc * T + w0: oc * T + w0 + 128],
                               x_fm[:, oc * T + w0: oc * T + w0 + 128], psc)
            t_oc_ev[(ts8, oc)] = k.inc("dve", i)
        t_oc_ev[ts8] = k.tok("dve")
    return (k.tok("dve"),)


def l1attn_in_map(inputs, shard):
    return {
        "x_in": np.ascontiguousarray(shard),
        "ln_gamma": inputs["L1_attn_ln_gamma"],
        "ln_beta": inputs["L1_attn_ln_beta"],
        "in_proj_w": inputs["L1_attn_in_proj_w"],
        "conv_w": inputs["L1_attn_conv_w"],
        "conv_b": inputs["L1_attn_conv_b"],
        "x_proj_w": inputs["L1_attn_x_proj_w"],
        "dt_proj_w": inputs["L1_attn_dt_proj_w"],
        "dt_proj_b": inputs["L1_attn_dt_proj_b"],
        "A_log": inputs["L1_attn_A_log"],
        "D_param": inputs["L1_attn_D_param"],
        "out_proj_w": inputs["L1_attn_out_proj_w"],
    }


def build_l1attn_program(debug=False):
    nc = bass.Bass("TRN2", target_bir_lowering=False)
    x_in = nc.dram_tensor("x_in", [T, D], F32, kind="ExternalInput").ap()
    x_out = nc.dram_tensor("x_out", [T, D], F32, kind="ExternalOutput").ap()
    W = declare_l1attn_weights(nc)
    W.update(make_l1attn_consts(nc))
    k = Ctx(nc)
    B = build_common(nc, k)
    B["cast_b"] = [B["y_bf"][:, 0:2048], B["y_bf"][:, 2048:4096]]
    build_l1attn_bufs(nc, B)
    t_cast = emit_cast_l1attn(k, B, W)
    t_load = emit_load_x(k, B, x_in)
    after = (t_load,) + tuple(t_cast)
    toks = emit_l1attn(k, B, W, after)
    t_store = emit_store_x(k, B, x_out, toks)
    k.eng["sp"].wait_ge(k.sem["sp"], k.cnt["sp"])
    return nc


def make_l0ffn_consts(nc):
    c = {}
    t = np.arange(T, dtype=np.float64)
    dh = 64
    inv = 1.0 / (10000.0 ** (np.arange(0, dh, 2, dtype=np.float64) / dh))
    ang = t[None, :] * inv[:, None]
    cosr = np.zeros((128, T), np.float64)
    sinr = np.zeros((128, T), np.float64)
    for p in range(128):
        pair = (p % 64) // 2
        cosr[p] = np.cos(ang[pair])
        sinr[p] = (-1.0 if p % 2 == 0 else 1.0) * np.sin(ang[pair])
    c["ropecos"] = nc.inline_tensor(_np_bf16(cosr), "c_ropecos")
    c["ropesin"] = nc.inline_tensor(_np_bf16(sinr), "c_ropesin")
    aug = np.zeros((68, 128), np.float32)
    aug[0] = 1.0
    aug[64] = 1.0
    c["kaug1"] = nc.inline_tensor(_np_bf16(aug), "c_kaug1")
    sel8b = np.zeros((8, 8 * 128), np.float32)
    for e in range(8):
        sel8b[e, e * 128:(e + 1) * 128] = 1.0
    c["sel8bf"] = nc.inline_tensor(_np_bf16(sel8b), "c_sel8bf")
    ig = np.ones((128, 4), np.float32)
    c["ig4"] = nc.inline_tensor(ig, "c_ig4")
    return c


def build_l0ffn_bufs(nc, B):
    B["o_big"] = nc.alloc_sbuf_tensor("o_big", [128, NCH * T], BF16)
    B["o_kc"] = nc.alloc_sbuf_tensor("o_kc", [128, 2 * T], BF16)
    B["o_vc"] = nc.alloc_sbuf_tensor("o_vc", [128, 2 * T], BF16)
    B["o_q"] = nc.alloc_sbuf_tensor("o_q", [128, T], BF16)
    B["o_k"] = nc.alloc_sbuf_tensor("o_k", [128, T], BF16)
    B["o_qsw"] = nc.alloc_sbuf_tensor("o_qsw", [128, 1024], BF16)
    B["o_qtmp"] = nc.alloc_sbuf_tensor("o_qtmp", [128, 1024], BF16)
    B["o_vtok"] = nc.alloc_sbuf_tensor("o_vtok", [128, 16 * 64], BF16)
    B["o_cos"] = nc.alloc_sbuf_tensor("o_cos", [128, T], BF16)
    B["o_sin"] = nc.alloc_sbuf_tensor("o_sin", [128, T], BF16)
    B["o_stexp"] = nc.alloc_sbuf_tensor("o_stexp", [128, 1024], BF16)
    B["o_wst"] = [nc.alloc_sbuf_tensor(f"o_wst{i}", [128, T], BF16) for i in range(2)]
    B["o_blockones"] = nc.alloc_sbuf_tensor("o_blockones", [128, 128], F32)
    B["o_onesbf"] = nc.alloc_sbuf_tensor("o_onesbf", [128, 128], BF16)
    B["o_mcol"] = nc.alloc_sbuf_tensor("o_mcol", [128, 48], F32)
    B["o_gw"] = nc.alloc_sbuf_tensor("o_gw", [128, 8 * 10], F32)
    B["o_gsc"] = nc.alloc_sbuf_tensor("o_gsc", [128, 64], F32)
    B["o_w8"] = nc.alloc_sbuf_tensor("o_w8", [128, 16 * 8], F32)
    B["o_n12"] = nc.alloc_sbuf_tensor("o_n12", [128, 16], F32)
    B["o_ig4"] = nc.alloc_sbuf_tensor("o_ig4", [128, 4], F32)
    B["o_wvhead"] = nc.alloc_sbuf_tensor("o_wvhead", [128, 128], BF16)
    B["o_mbc2"] = nc.alloc_sbuf_tensor("o_mbc2", [128, 512], F32)


def declare_l0ffn_weights(nc, pfx=""):
    W = {}
    names = [("Wq", [D, D]), ("Wk_c", [D, 256]), ("Wv_c", [D, 256]),
             ("Wk", [256, D]), ("Wv", [256, D]), ("Wo", [D, D]),
             ("W_shared_in", [D, 4096]), ("W_shared_out", [2048, D]),
             ("W1_shared", [D, 4096]), ("W2_experts", [8, 2048, D]),
             ("Wg_gate", [D, 2]), ("We_gate", [D, 8]),
             ("group_bias", [2]), ("expert_bias", [8]),
             ("norm1_w", [D]), ("norm2_w", [D])]
    for n, s in names:
        W[n] = nc.dram_tensor(pfx + n, s, F32, kind="ExternalInput").ap()
    W["wq_c"] = nc.dram_tensor(pfx + "wq_c", [D, D], BF16, kind="Internal").ap()
    W["wkc_c"] = nc.dram_tensor(pfx + "wkc_c", [D, 256], BF16, kind="Internal").ap()
    W["wvc_c"] = nc.dram_tensor(pfx + "wvc_c", [D, 256], BF16, kind="Internal").ap()
    W["wk_c2"] = nc.dram_tensor(pfx + "wk_c2", [256, D], BF16, kind="Internal").ap()
    W["wv_c2"] = nc.dram_tensor(pfx + "wv_c2", [256, D], BF16, kind="Internal").ap()
    W["wo_c"] = nc.dram_tensor(pfx + "wo_c", [D, D], BF16, kind="Internal").ap()
    W["wsi_c"] = nc.dram_tensor(pfx + "wsi_c", [D, 4096], BF16, kind="Internal").ap()
    W["wso_c"] = nc.dram_tensor(pfx + "wso_c", [2048, D], BF16, kind="Internal").ap()
    W["w1s_c"] = nc.dram_tensor(pfx + "w1s_c", [D, 4096], BF16, kind="Internal").ap()
    W["w2e_c"] = nc.dram_tensor(pfx + "w2e_c", [8, 2048, D], BF16, kind="Internal").ap()
    W["x_spill"] = nc.dram_tensor(pfx + "x_spill", [128, NCH * T], F32, kind="Internal").ap()
    W["mask_scratch2"] = nc.dram_tensor(pfx + "mask_scratch2", [128, 128], F32, kind="Internal").ap()
    return W


def emit_cast_l0ffn(k, B, W):
    pairs = []
    for kc in range(8):
        r = slice(kc * 128, (kc + 1) * 128)
        pairs.append((W["Wq"][r, :], W["wq_c"][r, :], 0.125))
        pairs.append((W["Wk_c"][r, :], W["wkc_c"][r, :]))
        pairs.append((W["Wv_c"][r, :], W["wvc_c"][r, :]))
        pairs.append((W["Wo"][r, :], W["wo_c"][r, :]))
        for half in range(2):
            cc = slice(half * 2048, (half + 1) * 2048)
            pairs.append((W["W_shared_in"][r, cc], W["wsi_c"][r, cc]))
            pairs.append((W["W1_shared"][r, cc], W["w1s_c"][r, cc]))
    for kc in range(2):
        r = slice(kc * 128, (kc + 1) * 128)
        pairs.append((W["Wk"][r, :], W["wk_c2"][r, :]))
        pairs.append((W["Wv"][r, :], W["wv_c2"][r, :]))
    for kc in range(16):
        r = slice(kc * 128, (kc + 1) * 128)
        pairs.append((W["W_shared_out"][r, :], W["wso_c"][r, :]))
    for e in range(8):
        for kc in range(16):
            r = slice(kc * 128, (kc + 1) * 128)
            pairs.append((W["W2_experts"][e, r, :], W["w2e_c"][e, r, :]))
    return emit_cast_pairs(k, B, pairs)


def emit_l0ffn(k, B, W, after, dbg=None):
    nc = k.nc
    pe = k.eng["pe"]
    act, dve, pool, sp = k.eng["act"], k.eng["dve"], k.eng["pool"], k.eng["sp"]
    cs = B["cs"]
    x_fm, y_bf, scr = B["x_fm"], B["y_bf"], B["scr"]
    ps_a, ps_b, ps_up, ps_dn = B["ps_a"], B["ps_b"], B["ps_up"], B["ps_dn"]
    big = B["o_big"]
    wst = B["o_wst"]
    mcol = B["o_mcol"]

    k.wait("sp", *after)
    t_spill = {}
    for ch in range(NCH):
        i = sp.dma_start(out=W["x_spill"][:, ch * T:(ch + 1) * T],
                         in_=x_fm[:, ch * T:(ch + 1) * T])
        t_spill[ch] = k.inc("sp", i, 16)

    m_b, rstd_b, ln_toks = emit_ln(k, B, after)

    i = sp.dma_start(out=B["o_cos"][:], in_=W["ropecos"].ap())
    k.inc("sp", i, 16)
    i = sp.dma_start(out=B["o_sin"][:], in_=W["ropesin"].ap())
    k.inc("sp", i, 16)
    i = sp.dma_start(out=B["o_blockones"][:], in_=B["c_inline"]["blockones_sh"].ap())
    k.inc("sp", i, 16)
    i = sp.dma_start(out=B["o_onesbf"][:], in_=B["c_inline"]["ones_bf_sh"].ap())
    k.inc("sp", i, 16)
    i = sp.dma_start(out=B["o_ig4"][:], in_=W["ig4"].ap())
    k.inc("sp", i, 16)
    i = sp.dma_start(out=B["o_gw"][:].rearrange("p (c n) -> p c n", n=10)[:, :, 0:2],
                     in_=W["Wg_gate"].rearrange("(c p) n -> p c n", p=128),
                     allow_slow_non_contiguous=True)
    k.inc("sp", i, 16)
    i = sp.dma_start(out=B["o_gw"][:].rearrange("p (c n) -> p c n", n=10)[:, :, 2:10],
                     in_=W["We_gate"].rearrange("(c p) n -> p c n", p=128),
                     allow_slow_non_contiguous=True)
    k.inc("sp", i, 16)
    i = sp.dma_start(out=B["o_n12"][:, 0:8], in_=W["norm1_w"].rearrange("(c p) -> p c", p=128),
                     allow_slow_non_contiguous=True)
    k.inc("sp", i, 16)
    i = sp.dma_start(out=B["o_n12"][:, 8:16], in_=W["norm2_w"].rearrange("(c p) -> p c", p=128),
                     allow_slow_non_contiguous=True)
    k.inc("sp", i, 16)
    i = sp.dma_start(out=B["o_gsc"][:, 0:2], in_=W["group_bias"][None, :].partition_broadcast(128))
    k.inc("sp", i, 16)
    i = sp.dma_start(out=B["o_gsc"][:, 2:10], in_=W["expert_bias"][None, :].partition_broadcast(128))
    t_smalls = k.inc("sp", i, 16)

    t_ev = {}
    pj = 0
    li = 0
    t_wmm = {}
    k.wait("sp", W.get("bg_kcvc"))
    for (wap, dst) in ((W["wkc_c"], B["o_kc"]), (W["wvc_c"], B["o_vc"])):
        for fc in range(2):
            bi = li % 2
            prev = t_wmm.get(li - 2)
            if prev is not None:
                k.wait("sp", prev)
            i = sp.dma_start(out=wst[bi][:, 0:1024],
                             in_=wap[:, fc * 128:(fc + 1) * 128]
                             .rearrange("(c p) m -> p c m", p=128))
            t_w = k.inc("sp", i, 16)
            for tsub in range(4):
                pcol = (pj % 2) * 512
                prev = t_ev.get(pj - 2)
                if prev is not None:
                    k.wait("pe", prev)
                k.wait("pe", t_w, *ln_toks)
                for kc in range(NCH):
                    inst = pe.matmul(ps_b[:, pcol:pcol + 512],
                                     wst[bi][:, kc * 128:(kc + 1) * 128],
                                     y_bf[:, kc * T + tsub * 512: kc * T + tsub * 512 + 512],
                                     start=(kc == 0), stop=(kc == NCH - 1))
                t_mm = k.inc("pe", inst)
                en, eng = ("act", act) if pj % 2 == 0 else ("dve", dve)
                k.wait(en, t_mm)
                if en == "act":
                    i = act.activation(dst[:, fc * T + tsub * 512: fc * T + tsub * 512 + 512],
                                       ps_b[:, pcol:pcol + 512], AF.Copy)
                else:
                    i = dve.tensor_copy(dst[:, fc * T + tsub * 512: fc * T + tsub * 512 + 512],
                                        ps_b[:, pcol:pcol + 512])
                t_ev[pj] = k.inc(en, i)
                pj += 1
            t_wmm[li] = k.tok("pe")
            li += 1
    t_kcvc = (k.tok("act"), k.tok("dve"))

    attnout = big
    qch, kch, qsw, qtmp = B["o_q"], B["o_k"], B["o_qsw"], B["o_qtmp"]
    vtok = B["o_vtok"]
    stexp = B["o_stexp"]
    recip = B["xio"]
    t_attn_prev = None
    t_vread = None
    t_qkread = None
    st_war = {0: None, 1: None}
    t_ost = {}
    SQ = 0.125
    for hp in range(8):
        bi = hp % 2
        k.wait("sp", k.tok("pe"))
        i = sp.dma_start(out=wst[bi][:, 0:1024],
                         in_=W["wq_c"][:, hp * 128:(hp + 1) * 128]
                         .rearrange("(c p) m -> p c m", p=128))
        t_wq = k.inc("sp", i, 16)
        t_evq = {}
        for tsub in range(4):
            pcol = (tsub % 2) * 512
            prev = t_evq.get(tsub - 2)
            if prev is not None:
                k.wait("pe", prev)
            k.wait("pe", t_wq, *ln_toks, t_qkread)
            for kc in range(NCH):
                inst = pe.matmul(ps_a[:, pcol:pcol + 512],
                                 wst[bi][:, kc * 128:(kc + 1) * 128],
                                 y_bf[:, kc * T + tsub * 512: kc * T + tsub * 512 + 512],
                                 start=(kc == 0), stop=(kc == NCH - 1))
            t_mm = k.inc("pe", inst)
            k.wait("dve", t_mm)
            i = dve.tensor_copy(qch[:, tsub * 512:(tsub + 1) * 512], ps_a[:, pcol:pcol + 512])
            t_evq[tsub] = k.inc("dve", i)
        i = sp.dma_start(out=wst[bi][:, 1024:1280],
                         in_=W["wk_c2"][:, hp * 128:(hp + 1) * 128]
                         .rearrange("(c p) m -> p c m", p=128))
        t_wk = k.inc("sp", i, 16)
        for tsub in range(4):
            pcol = (tsub % 2) * 512
            prev = t_evq.get(4 + tsub - 2)
            if prev is not None:
                k.wait("pe", prev)
            k.wait("pe", t_wk, *t_kcvc)
            for c2 in range(2):
                inst = pe.matmul(ps_b[:, pcol:pcol + 512],
                                 wst[bi][:, 1024 + c2 * 128:1024 + (c2 + 1) * 128],
                                 B["o_kc"][:, c2 * T + tsub * 512: c2 * T + tsub * 512 + 512],
                                 start=(c2 == 0), stop=(c2 == 1))
            t_mm = k.inc("pe", inst)
            k.wait("dve", t_mm)
            i = dve.tensor_copy(kch[:, tsub * 512:(tsub + 1) * 512], ps_b[:, pcol:pcol + 512])
            t_evq[4 + tsub] = k.inc("dve", i)
        t_qk = k.tok("dve")
        mask = []
        for i2 in range(0, 32, 2):
            mask += [i2 + 1, i2]
        k.wait("dve", t_qk)
        for buf in (qch, kch):
            for hf in range(2):
                sl = slice(hf * 1024, (hf + 1) * 1024)
                i = dve.stream_shuffle(qsw[:], buf[:, sl], mask)
                k.inc("dve", i)
                i = dve.tensor_mul(qtmp[:], buf[:, sl], B["o_cos"][:, sl])
                k.inc("dve", i)
                i = dve.tensor_mul(qsw[:], qsw[:], B["o_sin"][:, sl])
                k.inc("dve", i)
                i = dve.tensor_add(buf[:, sl], qtmp[:], qsw[:])
                t_rope = k.inc("dve", i)

        def pairnorm(srcbuf, outc):
            t_entry = k.tok("dve")
            k.wait("act", t_rope, k.tok("pe"))
            i = act.activation(B["xio"][:], srcbuf[:, 0:2048], AF.Square)
            t_s = k.inc("act", i)
            k.wait("pe", t_s, t_entry)
            for ts4 in range(4):
                tile = (ps_a if ts4 < 2 else ps_b)[:, (ts4 % 2) * 512:(ts4 % 2) * 512 + 512]
                inst = pe.matmul(tile, B["o_blockones"][:],
                                 B["xio"][:, ts4 * 512:(ts4 + 1) * 512],
                                 start=True, stop=True)
            t_n = k.inc("pe", inst)
            k.wait("dve", t_n)
            for ts4 in range(4):
                tile = (ps_a if ts4 < 2 else ps_b)[:, (ts4 % 2) * 512:(ts4 % 2) * 512 + 512]
                i = dve.tensor_reduce(mcol[:, outc + ts4:outc + ts4 + 1], tile, AX.X, ALU.max)
                k.inc("dve", i)
            i = dve.tensor_reduce(mcol[:, outc + 4:outc + 5], mcol[:, outc:outc + 4],
                                  AX.X, ALU.max)
            return k.inc("dve", i)
        t_nq = pairnorm(qch, 0)
        t_nk = pairnorm(kch, 8)
        k.wait("dve", t_nq, t_nk)
        i = dve.tensor_scalar(B["xio"][:, 0:128], cs["ones128"][:], mcol[:, 4:5], None, ALU.mult)
        k.inc("dve", i)
        i = dve.tensor_scalar(B["xio"][:, 128:256], cs["ones128"][:], mcol[:, 12:13], None, ALU.mult)
        t_rep = k.inc("dve", i)
        k.wait("pe", t_rep)
        inst = pe.transpose(ps_a[:, 0:128], B["xio"][:, 0:128], cs["ident_f32"][:])
        k.inc("pe", inst)
        inst = pe.transpose(ps_a[:, 512:640], B["xio"][:, 128:256], cs["ident_f32"][:])
        t_tr = k.inc("pe", inst)
        k.wait("dve", t_tr)
        i = dve.tensor_copy(B["xio"][:, 0:128], ps_a[:, 0:128])
        k.inc("dve", i)
        i = dve.tensor_copy(B["xio"][:, 128:256], ps_a[:, 512:640])
        k.inc("dve", i)
        i = dve.tensor_reduce(mcol[:, 16:17], B["xio"][:, 0:128], AX.X, ALU.max)
        k.inc("dve", i)
        i = dve.tensor_reduce(mcol[:, 17:18], B["xio"][:, 128:256], AX.X, ALU.max)
        k.inc("dve", i)
        i = dve.tensor_scalar(mcol[:, 18:19], mcol[:, 16:17], mcol[:, 17:18], None, ALU.mult)
        t_m2 = k.inc("dve", i)
        k.wait("act", t_m2)
        i = act.activation(mcol[:, 19:20], mcol[:, 18:19], AF.Sqrt)
        t_sq = k.inc("act", i)
        k.wait("dve", t_sq)
        i = dve.tensor_scalar(mcol[:, 20:21], mcol[:, 19:20], -1.25, None, ALU.mult)
        t_M = k.inc("dve", i)
        t_qa = t_M
        if dbg is not None and hp == 0:
            k.wait("pool", t_M)
            i = pool.dma_start(out=dbg["q0"], in_=qch[:])
            k.inc("pool", i, 16)
            i = pool.dma_start(out=dbg["k0"], in_=kch[:])
            k.inc("pool", i, 16)
            i = pool.dma_start(out=dbg["m0"], in_=mcol[:])
            k.inc("pool", i, 16)

        for hh in range(2):
            h = hp * 2 + hh
            base = hh * 64
            k.wait("sp", k.tok("pe"))
            i = sp.dma_start(out=B["o_wvhead"][:],
                             in_=W["wv_c2"][:, h * 64:(h + 1) * 64]
                             .rearrange("(c p) m -> p c m", p=128))
            t_wv = k.inc("sp", i, 16)
            t_vev = {}
            for tc in range(NTOK):
                pcol = (tc % 2) * 512
                prev = t_vev.get(tc - 2)
                if prev is not None:
                    k.wait("pe", prev)
                k.wait("pe", *t_kcvc, t_vread, t_wv)
                for c2 in range(2):
                    inst = pe.matmul(ps_b[:, pcol:pcol + 64],
                                     B["o_vc"][:, c2 * T + tc * 128: c2 * T + (tc + 1) * 128],
                                     B["o_wvhead"][:, c2 * 64:(c2 + 1) * 64],
                                     start=(c2 == 0), stop=(c2 == 1))
                t_mm = k.inc("pe", inst)
                k.wait("dve", t_mm)
                i = dve.tensor_copy(vtok[:, tc * 64:(tc + 1) * 64], ps_b[:, pcol:pcol + 64])
                t_vev[tc] = k.inc("dve", i)
            t_v = k.tok("dve")
            if dbg is not None and h == 0:
                k.wait("pool", t_v)
                i = pool.dma_start(out=dbg["v0"], in_=vtok[:])
                k.inc("pool", i, 16)
            for qc in range(4):
                o_ps = ps_up[:, (qc % 2) * 512:(qc % 2) * 512 + 512]
                r_ps = ps_dn[:, (qc % 2) * 512:(qc % 2) * 512 + 512]
                prev = t_ost.get((h * 4 + qc) - 2)
                if prev is not None:
                    k.wait("pe", prev)
                for kt in range(NTOK):
                    sti = kt % 2
                    st_ps = ps_a[:, sti * 512:sti * 512 + 512]
                    k.wait("pe", st_war[sti], t_rope, t_qa)
                    inst = pe.matmul(st_ps, kch[base:base + 64, kt * 128:(kt + 1) * 128],
                                     qch[base:base + 64, qc * 512:(qc + 1) * 512],
                                     start=True, stop=True)
                    t_st = k.inc("pe", inst)
                    k.wait("act", t_st, t_qa)
                    i = act.activation(stexp[:, sti * 512:sti * 512 + 512], st_ps, AF.Exp,
                                       bias=mcol[:, 20:21])
                    t_exp = k.inc("act", i)
                    st_war[sti] = t_exp
                    k.wait("pe", t_exp, t_v)
                    inst = pe.matmul(o_ps[0:64, :], vtok[:, kt * 64:(kt + 1) * 64],
                                     stexp[:, sti * 512:sti * 512 + 512],
                                     start=(kt == 0), stop=(kt == NTOK - 1))
                    k.inc("pe", inst)
                    inst = pe.matmul(r_ps, B["o_onesbf"][:],
                                     stexp[:, sti * 512:sti * 512 + 512],
                                     start=(kt == 0), stop=(kt == NTOK - 1))
                    k.inc("pe", inst)
                t_pv = k.tok("pe")
                k.wait("dve", t_pv)
                i = dve.reciprocal(recip[:, 0:512], r_ps)
                k.inc("dve", i)
                i = dve.tensor_mul(B["xio"][0:64, 512:1024], o_ps[0:64, :], recip[0:64, 0:512])
                k.inc("dve", i)
                dst0 = hp * T + qc * 512
                if hh == 0:
                    i = dve.tensor_copy(attnout[0:64, dst0:dst0 + 512],
                                        B["xio"][0:64, 512:1024])
                else:
                    i = dve.tensor_copy(attnout[64:128, dst0:dst0 + 512],
                                        B["xio"][0:64, 512:1024])
                t_ost[h * 4 + qc] = k.inc("dve", i)
        t_vread = k.tok("pe")
        t_qkread = k.tok("pe")
    t_attn = (k.tok("dve"), k.tok("act"))
    if dbg is not None:
        k.wait("pool", *t_attn)
        i = pool.dma_start(out=dbg["ao"], in_=attnout[:])
        k.inc("pool", i, 16)

    t_woev = {}
    pj = 0
    for fc in range(8):
        bi = fc % 2
        prevw = t_woev.get(pj - 2)
        if prevw is not None:
            k.wait("sp", prevw)
        k.wait("sp", t_qkread, W.get("bg_wo"))
        i = sp.dma_start(out=wst[bi][:, 0:1024],
                         in_=W["wo_c"][:, fc * 128:(fc + 1) * 128]
                         .rearrange("(c p) m -> p c m", p=128))
        t_wo = k.inc("sp", i, 16)
        for tsub in range(4):
            pcol = (pj % 2) * 512
            prev = t_woev.get(pj - 2)
            if prev is not None:
                k.wait("pe", prev)
            k.wait("pe", t_wo, *t_attn)
            for kc in range(NCH):
                inst = pe.matmul(ps_b[:, pcol:pcol + 512],
                                 wst[bi][:, kc * 128:(kc + 1) * 128],
                                 attnout[:, kc * T + tsub * 512: kc * T + tsub * 512 + 512],
                                 start=(kc == 0), stop=(kc == NCH - 1))
            t_mm = k.inc("pe", inst)
            sl = slice(fc * T + tsub * 512, fc * T + tsub * 512 + 512)
            msl = slice(tsub * 512, tsub * 512 + 512)
            k.wait("dve", t_mm, t_spill[fc], *t_attn)
            i = dve.tensor_sub(B["xio"][:, msl], x_fm[:, sl], m_b[:, msl])
            k.inc("dve", i)
            i = dve.tensor_mul(B["xio"][:, msl], B["xio"][:, msl], rstd_b[:, msl])
            k.inc("dve", i)
            i = dve.tensor_add(x_fm[:, sl], B["xio"][:, msl], ps_b[:, pcol:pcol + 512])
            t_woev[pj] = k.inc("dve", i)
            pj += 1
    t_x1pre = k.tok("dve")
    if dbg is not None:
        k.wait("pool", t_x1pre)
        i = pool.dma_start(out=dbg["x1p"], in_=x_fm[:])
        k.inc("pool", i, 16)
        t_x1pre = (k.tok("pool"), t_x1pre)
        k.wait("dve", t_x1pre[0])
        t_x1pre = t_x1pre[1]

    def emit_rms(ncol, toks_in):
        t_s = None
        for c in range(NCH):
            en, eng = ("act", act) if c % 2 == 0 else ("dve", dve)
            k.wait(en, t_s, *toks_in)
            if en == "act":
                i = act.activation(B["xio"][:], x_fm[:, c * T:(c + 1) * T], AF.Square)
            else:
                i = dve.tensor_mul(B["xio"][:], x_fm[:, c * T:(c + 1) * T],
                                   x_fm[:, c * T:(c + 1) * T])
            t_sq = k.inc(en, i)
            k.wait("pe", t_sq)
            for ts4 in range(4):
                tile = (ps_a if ts4 < 2 else ps_b)[:, (ts4 % 2) * 512:(ts4 % 2) * 512 + 512]
                inst = pe.matmul(tile, cs["ones128"][:],
                                 B["xio"][:, ts4 * 512:(ts4 + 1) * 512],
                                 start=(c == 0), stop=(c == NCH - 1))
            t_s = k.inc("pe", inst)
        k.wait("dve", t_s)
        for ts4 in range(4):
            tile = (ps_a if ts4 < 2 else ps_b)[:, (ts4 % 2) * 512:(ts4 % 2) * 512 + 512]
            i = dve.tensor_scalar(scr[1][:, ts4 * 512:(ts4 + 1) * 512], tile,
                                  1.0 / D, None, ALU.mult)
            t_r = k.inc("dve", i)
        k.wait("act", t_r)
        i = act.activation(scr[1][:], scr[1][:], AF.Ln, bias=cs["epscols"][:, 1:2])
        k.inc("act", i)
        i = act.activation(scr[1][:], scr[1][:], AF.Exp, scale=-0.5)
        return k.inc("act", i)

    t_rms1 = emit_rms(0, (t_x1pre,))
    if dbg is not None:
        k.wait("pool", t_rms1)
        i = pool.dma_start(out=dbg["rrow"], in_=scr[1][:])
        k.inc("pool", i, 16)
        k.wait("dve", k.tok("pool"))
    t_apply = {}
    for c in range(NCH):
        k.wait("dve", t_rms1)
        i = dve.tensor_mul(x_fm[:, c * T:(c + 1) * T], x_fm[:, c * T:(c + 1) * T], scr[1][:])
        k.inc("dve", i)
        i = dve.tensor_scalar(x_fm[:, c * T:(c + 1) * T], x_fm[:, c * T:(c + 1) * T],
                              B["o_n12"][:, c:c + 1], None, ALU.mult)
        k.inc("dve", i)
        i = dve.tensor_copy(y_bf[:, c * T:(c + 1) * T], x_fm[:, c * T:(c + 1) * T])
        t_apply[c] = k.inc("dve", i)
    t_x1 = k.tok("dve")
    if dbg is not None:
        k.wait("pool", t_x1)
        i = pool.dma_start(out=dbg["x1"], in_=x_fm[:])
        k.inc("pool", i, 16)
    return t_x1, m_b, rstd_b, t_apply


def l0ffn_in_map(inputs, shard):
    return {
        "x_in": np.ascontiguousarray(shard),
        "Wq": inputs["L0_ffn_Wq"], "Wk_c": inputs["L0_ffn_Wk_c"],
        "Wv_c": inputs["L0_ffn_Wv_c"], "Wk": inputs["L0_ffn_Wk"],
        "Wv": inputs["L0_ffn_Wv"], "Wo": inputs["L0_ffn_Wo"],
        "W_shared_in": inputs["L0_ffn_W_shared_in"],
        "W_shared_out": inputs["L0_ffn_W_shared_out"],
        "W1_shared": inputs["L0_ffn_W1_shared"],
        "W2_experts": inputs["L0_ffn_W2_experts"],
        "Wg_gate": inputs["L0_ffn_Wg_gate"], "We_gate": inputs["L0_ffn_We_gate"],
        "group_bias": inputs["L0_ffn_group_bias"],
        "expert_bias": inputs["L0_ffn_expert_bias"],
        "norm1_w": inputs["L0_ffn_norm1_w"], "norm2_w": inputs["L0_ffn_norm2_w"],
    }


def emit_l0ffn_part2(k, B, W, t_x1, m_b, rstd_b, t_apply, dbg=None):
    nc = k.nc
    pe = k.eng["pe"]
    act, dve, pool, sp = k.eng["act"], k.eng["dve"], k.eng["pool"], k.eng["sp"]
    cs = B["cs"]
    x_fm, y_bf, scr = B["x_fm"], B["y_bf"], B["scr"]
    ps_a, ps_b, ps_up, ps_dn = B["ps_a"], B["ps_b"], B["ps_up"], B["ps_dn"]
    gbuf = B["o_big"]
    wst = B["o_wst"]
    gsc = B["o_gsc"]
    w8 = B["o_w8"]
    B["o_mbc"] = B["xio"][:, 1536:2048]

    gw = B["o_gw"].rearrange("p (c n) -> p c n", n=10)
    i = dve.tensor_copy(gsc[:, 10:64], cs["ones128"][:, 10:64])
    k.inc("dve", i)
    t_w8 = {}
    for ts in range(NTOK):
        if ts >= 1:
            k.wait("pe", t_w8[ts - 1])
        k.wait("pe", t_x1, t_apply.get(7))
        for c in range(NCH):
            inst = pe.matmul(ps_dn[:, 0:16][:, 0:10],
                             x_fm[:, c * T + ts * 128: c * T + (ts + 1) * 128],
                             gw[:, c, :], start=(c == 0), stop=(c == NCH - 1),
                             skip_group_check=True)
        t_A = k.inc("pe", inst)
        k.wait("dve", t_A)
        i = dve.tensor_add(gsc[:, 16:18], ps_dn[:, 0:2], gsc[:, 0:2])
        k.inc("dve", i)
        i = dve.tensor_add(gsc[:, 24:32], ps_dn[:, 2:10], gsc[:, 2:10])
        t_log = k.inc("dve", i)
        i = dve.tensor_reduce(gsc[:, 18:19], gsc[:, 16:18], AX.X, ALU.max, negate=True)
        t_n = k.inc("dve", i)
        k.wait("act", t_n)
        i = act.activation(gsc[:, 20:22], gsc[:, 16:18], AF.Exp, bias=gsc[:, 18:19])
        t_e2 = k.inc("act", i)
        k.wait("dve", t_e2)
        i = dve.tensor_reduce(gsc[:, 22:23], gsc[:, 20:22], AX.X, ALU.add)
        k.inc("dve", i)
        i = dve.tensor_reduce(gsc[:, 19:20], gsc[:, 20:22], AX.X, ALU.max)
        k.inc("dve", i)
        i = dve.tensor_scalar(gsc[:, 20:22], gsc[:, 20:22], gsc[:, 19:20], None, ALU.is_ge)
        k.inc("dve", i)
        i = dve.tensor_scalar(gsc[:, 32:36], B["o_ig4"][:], gsc[:, 20:21], None, ALU.mult)
        k.inc("dve", i)
        i = dve.tensor_scalar(gsc[:, 36:40], B["o_ig4"][:], gsc[:, 21:22], None, ALU.mult)
        k.inc("dve", i)
        i = dve.tensor_scalar(gsc[:, 32:40], gsc[:, 32:40], 1e9, -1e9, ALU.mult, ALU.add)
        k.inc("dve", i)
        i = dve.tensor_add(gsc[:, 24:32], gsc[:, 24:32], gsc[:, 32:40])
        k.inc("dve", i)
        i = dve.tensor_reduce(gsc[:, 40:41], gsc[:, 24:32], AX.X, ALU.max, negate=True)
        t_me = k.inc("dve", i)
        k.wait("act", t_me)
        i = act.activation(gsc[:, 24:32], gsc[:, 24:32], AF.Exp, bias=gsc[:, 40:41])
        t_ee = k.inc("act", i)
        k.wait("dve", t_ee)
        i = dve.tensor_reduce(gsc[:, 41:42], gsc[:, 24:32], AX.X, ALU.add)
        k.inc("dve", i)
        i = dve.tensor_reduce(gsc[:, 42:43], gsc[:, 24:32], AX.X, ALU.max)
        k.inc("dve", i)
        i = dve.tensor_scalar(gsc[:, 48:56], gsc[:, 24:32], gsc[:, 42:43], -1e9,
                              ALU.is_ge, ALU.mult)
        k.inc("dve", i)
        i = dve.tensor_add(gsc[:, 48:56], gsc[:, 48:56], gsc[:, 24:32])
        k.inc("dve", i)
        i = dve.tensor_reduce(gsc[:, 43:44], gsc[:, 48:56], AX.X, ALU.max)
        k.inc("dve", i)
        i = dve.tensor_scalar(gsc[:, 48:56], gsc[:, 24:32], gsc[:, 43:44], None, ALU.is_ge)
        k.inc("dve", i)
        i = dve.tensor_copy(gsc[:, 44:46], gsc[:, 22:24])
        k.inc("dve", i)
        i = dve.reciprocal(gsc[:, 40:48], gsc[:, 40:48])
        k.inc("dve", i)
        i = dve.tensor_mul(gsc[:, 24:32], gsc[:, 24:32], gsc[:, 48:56])
        k.inc("dve", i)
        i = dve.tensor_scalar(gsc[:, 24:32], gsc[:, 24:32], gsc[:, 19:20], None,
                              ALU.mult)
        k.inc("dve", i)
        i = dve.tensor_scalar(gsc[:, 24:32], gsc[:, 24:32], gsc[:, 41:42], None, ALU.mult)
        k.inc("dve", i)
        i = dve.tensor_scalar(w8[:, ts * 8:(ts + 1) * 8], gsc[:, 24:32], gsc[:, 44:45],
                              None, ALU.mult)
        t_w8[ts] = k.inc("dve", i)

    if dbg is not None:
        k.wait("pool", t_w8[NTOK - 1])
        i = pool.dma_start(out=dbg["w8"], in_=w8[:])
        k.inc("pool", i, 16)
    k.wait("pe", t_w8[NTOK - 1])
    inst = pe.transpose(ps_b[:, 0:128], w8[:], cs["ident_f32"][:])
    t_t = k.inc("pe", inst)
    k.wait("dve", t_t)
    i = dve.tensor_copy(B["xio"][:, 0:128], ps_b[:, 0:128])
    t_mT = k.inc("dve", i)
    k.wait("sp", t_mT)
    i = sp.dma_start(out=W["mask_scratch2"], in_=B["xio"][:, 0:128])
    t_wfm = k.inc("sp", i, 16)

    mbc = B["o_mbc2"]
    wrows = W["mask_scratch2"].rearrange("(ts e) p -> e ts p", e=8)
    t_gread = None
    t_dn_ev = {}
    dn_i = [0]
    t_mbc_read = None
    t_mbc_last = None
    for branch in range(2):
        win = W["wsi_c"] if branch == 0 else W["w1s_c"]
        if branch == 1 and dbg is not None:
            k.wait("pool", k.tok("dve"))
            i = pool.dma_start(out=dbg["sh0"], in_=x_fm[:])
            k.inc("pool", i, 16)
            k.wait("dve", k.tok("pool"))
        for tq in range(4):
            tok0 = tq * 512
            t_gc = {}
            for gc in range(16):
                bi = gc % 2
                prev = t_gc.get(gc - 2)
                if prev is not None:
                    k.wait("sp", prev)
                k.wait("sp", t_gread, t_x1, W.get("bg_ffn"))
                i = sp.dma_start(out=wst[bi][:, 0:1024],
                                 in_=win[:, gc * 128:(gc + 1) * 128]
                                 .rearrange("(c p) m -> p c m", p=128))
                t_wa = k.inc("sp", i, 16)
                k.wait("pool", t_gread, t_x1, W.get("bg_ffn"))
                if prev is not None:
                    k.wait("pool", prev)
                i = pool.dma_start(out=wst[bi][:, 1024:2048],
                                   in_=win[:, 2048 + gc * 128:2048 + (gc + 1) * 128]
                                   .rearrange("(c p) m -> p c m", p=128))
                t_w = k.inc("pool", i, 16)
                k.wait("pe", t_wa, t_w, t_x1, t_gread)
                if gc >= 2:
                    k.wait("pe", t_gc[gc - 2])
                for kc in range(NCH):
                    inst = pe.matmul(ps_up[:, (gc % 2) * 512:(gc % 2) * 512 + 512],
                                     wst[bi][:, kc * 128:(kc + 1) * 128],
                                     y_bf[:, kc * T + tok0: kc * T + tok0 + 512],
                                     start=(kc == 0), stop=(kc == NCH - 1))
                k.inc("pe", inst)
                for kc in range(NCH):
                    inst = pe.matmul(ps_dn[:, (gc % 2) * 512:(gc % 2) * 512 + 512],
                                     wst[bi][:, 1024 + kc * 128:1024 + (kc + 1) * 128],
                                     y_bf[:, kc * T + tok0: kc * T + tok0 + 512],
                                     start=(kc == 0), stop=(kc == NCH - 1))
                t_ab = k.inc("pe", inst)
                k.wait("act", t_ab, t_gc.get(gc - 1))
                i = act.activation(B["xio"][:, 0:512],
                                   ps_up[:, (gc % 2) * 512:(gc % 2) * 512 + 512], AF.Silu)
                t_sa = k.inc("act", i)
                k.wait("dve", t_sa)
                i = dve.tensor_mul(gbuf[:, gc * 512:(gc + 1) * 512], B["xio"][:, 0:512],
                                   ps_dn[:, (gc % 2) * 512:(gc % 2) * 512 + 512])
                t_gc[gc] = k.inc("dve", i)
            t_g = k.tok("dve")
            t_up_pe = k.tok("pe")
            if branch == 0 and tq == 0 and dbg is not None:
                k.wait("pool", t_g)
                i = pool.dma_start(out=dbg["gb0"], in_=gbuf[:, 0:16 * 512])
                k.inc("pool", i, 16)
                k.wait("pe", k.tok("pool"))
            if branch == 0:
                wouts = [(W["wso_c"], None)]
            else:
                wouts = [(W["w2e_c"][e], e) for e in range(8)]
            for wq_i, (wout, e) in enumerate(wouts):
                if e is not None:
                    k.wait("sp", t_wfm, t_mbc_read)
                    i = sp.dma_start(out=mbc[:].rearrange("p (a b) -> p a b", a=4),
                                     in_=wrows[e, 4 * tq:4 * tq + 4, :]
                                     .partition_broadcast(128))
                    t_mbc = t_mbc_last = k.inc("sp", i, 16)
                for fc in range(8):
                    bi = dn_i[0] % 2
                    prev = t_dn_ev.get(dn_i[0] - 2)
                    if prev is not None:
                        k.wait("sp", prev)
                    k.wait("sp", t_up_pe, W.get("bg_w2e"))
                    i = sp.dma_start(out=wst[bi][:, 0:2048],
                                     in_=wout[:, fc * 128:(fc + 1) * 128]
                                     .rearrange("(c p) m -> p c m", p=128))
                    t_w = k.inc("sp", i, 16)
                    pcol = (dn_i[0] % 2) * 512
                    prev = t_dn_ev.get(dn_i[0] - 2)
                    if prev is not None:
                        k.wait("pe", prev)
                    k.wait("pe", t_w, t_g)
                    for gc in range(16):
                        inst = pe.matmul(ps_a[:, pcol:pcol + 512],
                                         wst[bi][:, gc * 128:(gc + 1) * 128],
                                         gbuf[:, gc * 512:(gc + 1) * 512],
                                         start=(gc == 0), stop=(gc == 15))
                    t_mm = k.inc("pe", inst)
                    sl = slice(fc * T + tok0, fc * T + tok0 + 512)
                    if e is None:
                        k.wait("dve", t_mm)
                        i = dve.tensor_add(x_fm[:, sl], x_fm[:, sl], ps_a[:, pcol:pcol + 512])
                    else:
                        k.wait("dve", t_mm, t_mbc)
                        i = dve.tensor_mul(B["xio"][:, 512:1024], ps_a[:, pcol:pcol + 512],
                                           mbc[:])
                        k.inc("dve", i)
                        i = dve.tensor_add(x_fm[:, sl], x_fm[:, sl], B["xio"][:, 512:1024])
                    t_dn_ev[dn_i[0]] = k.inc("dve", i)
                    dn_i[0] += 1
                if e is not None:
                    t_mbc_read = k.tok("dve")
            t_gread = k.tok("pe")

    t_all = (k.tok("dve"), k.tok("act"), k.tok("pool"))
    if dbg is not None:
        k.wait("pool", k.tok("dve"))
        i = pool.dma_start(out=dbg["pre"], in_=x_fm[:])
        k.inc("pool", i, 16)
    t_s = None
    for c in range(NCH):
        en, eng = ("act", act) if c % 2 == 0 else ("dve", dve)
        k.wait(en, t_s, *t_all)
        if en == "act":
            i = act.activation(B["xio"][:], x_fm[:, c * T:(c + 1) * T], AF.Square)
        else:
            i = dve.tensor_mul(B["xio"][:], x_fm[:, c * T:(c + 1) * T],
                               x_fm[:, c * T:(c + 1) * T])
        t_sq = k.inc(en, i)
        k.wait("pe", t_sq)
        for ts4 in range(4):
            tile = (ps_a if ts4 < 2 else ps_b)[:, (ts4 % 2) * 512:(ts4 % 2) * 512 + 512]
            inst = pe.matmul(tile, cs["ones128"][:],
                             B["xio"][:, ts4 * 512:(ts4 + 1) * 512],
                             start=(c == 0), stop=(c == NCH - 1))
        t_s = k.inc("pe", inst)
    k.wait("dve", t_s)
    for ts4 in range(4):
        tile = (ps_a if ts4 < 2 else ps_b)[:, (ts4 % 2) * 512:(ts4 % 2) * 512 + 512]
        i = dve.tensor_scalar(scr[1][:, ts4 * 512:(ts4 + 1) * 512], tile,
                              1.0 / D, None, ALU.mult)
        t_r = k.inc("dve", i)
    k.wait("act", t_r)
    i = act.activation(scr[1][:], scr[1][:], AF.Ln, bias=cs["epscols"][:, 1:2])
    k.inc("act", i)
    i = act.activation(scr[1][:], scr[1][:], AF.Exp, scale=-0.5)
    t_rms2 = k.inc("act", i)
    if dbg is not None:
        k.wait("pool", t_rms2, k.tok("dve"))
        i = pool.dma_start(out=dbg["pf"], in_=x_fm[:])
        k.inc("pool", i, 16)
        i = pool.dma_start(out=dbg["r2"], in_=scr[1][:])
        k.inc("pool", i, 16)
        k.wait("dve", k.tok("pool"))
    toks = []
    k.wait("sp", t_s)
    for c in range(NCH):
        i = sp.dma_start(out=B["xio"][:], in_=W["x_spill"][:, c * T:(c + 1) * T])
        t_xl = k.inc("sp", i, 16)
        k.wait("dve", t_rms2, t_xl)
        i = dve.tensor_mul(x_fm[:, c * T:(c + 1) * T], x_fm[:, c * T:(c + 1) * T], scr[1][:])
        k.inc("dve", i)
        i = dve.tensor_scalar(x_fm[:, c * T:(c + 1) * T], x_fm[:, c * T:(c + 1) * T],
                              B["o_n12"][:, 8 + c:9 + c], 0.5, ALU.mult, ALU.mult)
        k.inc("dve", i)
        i = dve.tensor_add(x_fm[:, c * T:(c + 1) * T], x_fm[:, c * T:(c + 1) * T],
                           B["xio"][:])
        toks.append(k.inc("dve", i))
        k.wait("sp", toks[-1])
    if dbg is not None:
        k.wait("pool", *toks)
        i = pool.dma_start(out=dbg["xf2"], in_=x_fm[:])
        k.inc("pool", i, 16)
        toks.append(k.tok("pool"))
    return tuple(toks)


def build_l0ffn_program(debug=False):
    nc = bass.Bass("TRN2", target_bir_lowering=False)
    x_in = nc.dram_tensor("x_in", [T, D], F32, kind="ExternalInput").ap()
    x_out = nc.dram_tensor("x_out", [T, D], F32, kind="ExternalOutput").ap()
    W = declare_l0ffn_weights(nc)
    W.update(make_l0ffn_consts(nc))
    k = Ctx(nc)
    B = build_common(nc, k)
    B["cast_b"] = [B["y_bf"][:, 0:2048], B["y_bf"][:, 2048:4096]]
    build_l0ffn_bufs(nc, B)
    if debug:
        B["dbg"] = {
            "ao": nc.dram_tensor("dbg_ao", [128, NCH * T], BF16, kind="ExternalOutput").ap(),
            "x1": nc.dram_tensor("dbg_x1", [128, NCH * T], F32, kind="ExternalOutput").ap(),
            "w8": nc.dram_tensor("dbg_w8", [128, 128], F32, kind="ExternalOutput").ap(),
            "pre": nc.dram_tensor("dbg_pre", [128, NCH * T], F32, kind="ExternalOutput").ap(),
            "q0": nc.dram_tensor("dbg_q0", [128, T], BF16, kind="ExternalOutput").ap(),
            "k0": nc.dram_tensor("dbg_k0", [128, T], BF16, kind="ExternalOutput").ap(),
            "v0": nc.dram_tensor("dbg_v0", [128, 16 * 64], BF16, kind="ExternalOutput").ap(),
            "m0": nc.dram_tensor("dbg_m0", [128, 48], F32, kind="ExternalOutput").ap(),
            "x1p": nc.dram_tensor("dbg_x1p", [128, NCH * T], F32, kind="ExternalOutput").ap(),
            "sh0": nc.dram_tensor("dbg_sh0", [128, NCH * T], F32, kind="ExternalOutput").ap(),
            "gb0": nc.dram_tensor("dbg_gb0", [128, 16 * 512], BF16, kind="ExternalOutput").ap(),
            "pf": nc.dram_tensor("dbg_pf", [128, NCH * T], F32, kind="ExternalOutput").ap(),
            "r2": nc.dram_tensor("dbg_r2", [128, T], F32, kind="ExternalOutput").ap(),
            "xf2": nc.dram_tensor("dbg_xf2", [128, NCH * T], F32, kind="ExternalOutput").ap(),
            "rrow": nc.dram_tensor("dbg_rrow", [128, T], F32, kind="ExternalOutput").ap(),
        }
    t_cast = emit_cast_l0ffn(k, B, W)
    t_load = emit_load_x(k, B, x_in)
    after = (t_load,) + tuple(t_cast)
    t_x1, m_b, rstd_b, t_apply = emit_l0ffn(k, B, W, after, dbg=B.get("dbg"))
    toks = emit_l0ffn_part2(k, B, W, t_x1, m_b, rstd_b, t_apply, dbg=B.get("dbg"))
    t_store = emit_store_x(k, B, x_out, toks)
    k.eng["sp"].wait_ge(k.sem["sp"], k.cnt["sp"])
    k.eng["pool"].wait_ge(k.sem["pool"], k.cnt["pool"])
    return nc


ARENA_COLS = 22224


def phase_barrier(k):
    toks = {e: k.tok(e) for e in Ctx.ENGS}
    for e in Ctx.ENGS:
        k.wait(e, *[toks[p] for p in Ctx.ENGS if p != e])
    return tuple(toks.values())


class Carver:

    def __init__(self, nc, cols_f32=ARENA_COLS):
        self.t32 = nc.alloc_sbuf_tensor("arena", [128, cols_f32], F32)
        self.t16 = self.t32.bitcast(BF16)
        self.limit = cols_f32 * 4
        self.off = 0

    def reset(self):
        self.off = 0

    def take(self, shape, dt):
        self.off = (self.off + 31) & ~31
        bp = 4 if dt == F32 else 2
        col0 = self.off // bp
        self.off += shape[1] * bp
        assert self.off <= self.limit, f"arena overflow: {self.off} > {self.limit}"
        src = self.t32 if dt == F32 else self.t16
        return src[0:shape[0], col0:col0 + shape[1]]

    def view(self, off_bytes, shape, dt):
        bp = 4 if dt == F32 else 2
        src = self.t32 if dt == F32 else self.t16
        return src[0:shape[0], off_bytes // bp: off_bytes // bp + shape[1]]


def carve_l1ffn_bufs(cv, B):
    t = cv.take
    B["g_gwT"] = t([128, 64], F32)
    B["g_grow"] = t([128, 8], F32)
    B["g_biasbc"] = t([128, 16], F32)
    B["g_rstdtok"] = t([128, 128], F32)
    B["g_mask"] = t([128, 16 * 8], F32)
    B["g_maskfm"] = t([8, T], F32)
    B["g_l"] = t([128, 8], F32)
    B["g_mx"] = t([128, 1], F32)
    B["f_b1cols"] = t([128, 8 * 32], F32)
    B["f_b2cols"] = t([128, 8 * 8], F32)
    B["f_w1c"] = [t([128, 4096], BF16) for _ in range(2)]
    B["f_w2c"] = [t([128, 2 * 1024], BF16) for _ in range(2)]
    B["f_h"] = t([128, 32 * 512], BF16)
    B["f_mbc"] = t([128, 512], F32)
    B["g_stok"] = t([128, 128], F32)
    B["g_gterm"] = t([128, 8], F32)
    B["g_maskT"] = t([128, 128], F32)
    B["sel8_sb"] = t([8, 8 * 128], F32)
    B["cast_b"] = [B["y_bf"][:, 0:2048], B["y_bf"][:, 2048:4096]]


def carve_l0attn_bufs(cv, B):
    t = cv.take
    B["a_qT"] = t([128, 8 * T], BF16)
    B["a_kT"] = t([128, 2 * T], BF16)
    B["a_vtok"] = t([128, 4 * 16 * 64], BF16)
    B["a_wbuf"] = t([128, 8 * 1024], BF16)
    B["a_kaug"] = t([4, T], BF16)
    B["a_qaug"] = t([4, T], BF16)
    B["a_qscr"] = t([64, 512], BF16)
    B["a_kscr"] = t([64, T], BF16)
    B["a_stexp"] = t([128, 2 * 512], BF16)
    B["a_iota"] = t([128, 1024], BF16)
    B["a_onesbf"] = t([128, 128], BF16)
    B["a_blockones"] = t([128, 128], F32)
    B["a_cmask"] = t([128, 4 * 512], BF16)
    B["a_identbf"] = t([128, 128], BF16)
    B["a_mcol"] = t([128, 18], F32)
    B["cast_b"] = [B["y_bf"][:, 0:2048], B["y_bf"][:, 2048:4096]]


def carve_l1attn_bufs(cv, B):
    t = cv.take
    B["m_scr"] = [t([128, T], F32) for _ in range(5)]
    B["m_conv"] = t([128, T + 8], F32)
    B["ln2_R"] = B["m_scr"][0]
    B["m_bf"] = t([128, T], BF16)
    B["m_dtr"] = t([128, T], BF16)
    B["m_BC"] = t([32, T], BF16)
    B["m_dtw"] = t([128, T], BF16)
    off0 = (cv.off + 31) & ~31
    B["m_xpw"] = t([128, 16 * 160], BF16)
    t([128, 1536], BF16)
    B["m_sel32r"] = cv.view(off0, [32, 4096], BF16)
    B["m_wst"] = [t([128, T], BF16) for _ in range(2)]
    B["m_ywin"] = t([128, 16 * 128], BF16)
    B["m_cols"] = t([128, 256 + 64 + 16 * 4 + 16], F32)
    B["cast_b"] = [B["y_bf"][:, 0:2048], B["y_bf"][:, 2048:4096]]


def carve_l0ffn_bufs(cv, B):
    t = cv.take
    B["o_big"] = t([128, NCH * T], BF16)
    B["o_kc"] = t([128, 2 * T], BF16)
    B["o_vc"] = t([128, 2 * T], BF16)
    B["o_q"] = t([128, T], BF16)
    B["o_k"] = t([128, T], BF16)
    B["o_qsw"] = t([128, 1024], BF16)
    B["o_qtmp"] = t([128, 1024], BF16)
    B["o_vtok"] = t([128, 16 * 64], BF16)
    B["o_cos"] = t([128, T], BF16)
    B["o_sin"] = t([128, T], BF16)
    B["o_stexp"] = t([128, 1024], BF16)
    B["o_wst"] = [t([128, T], BF16) for _ in range(2)]
    B["o_blockones"] = t([128, 128], F32)
    B["o_onesbf"] = t([128, 128], BF16)
    B["o_mcol"] = t([128, 48], F32)
    B["o_gw"] = t([128, 8 * 10], F32)
    B["o_gsc"] = t([128, 64], F32)
    B["o_w8"] = t([128, 16 * 8], F32)
    B["o_n12"] = t([128, 16], F32)
    B["o_ig4"] = t([128, 4], F32)
    B["o_wvhead"] = t([128, 128], BF16)
    B["o_mbc2"] = t([128, 512], F32)
    B["cast_b"] = [B["y_bf"][:, 0:2048], B["y_bf"][:, 2048:4096]]


def bg_cast(k, pairs):
    raw = k.raw["pool"]
    for src, dst in pairs:
        if k.cnt["bgc"] > 0:
            raw.wait_ge(k.sem["bgc"], k.cnt["bgc"])
        inst = raw.dma_start(out=dst, in_=src)
        k.cnt["bgc"] += 16
        inst.then_inc(k.sem["bgc"], 16)
    return ("bgc", k.cnt["bgc"])


def emit_bg_casts(k, W0f, W0a, W1f, W1a):
    W0f["bg_kcvc"] = bg_cast(k, [
        (W0f["Wk_c"], W0f["wkc_c"]), (W0f["Wv_c"], W0f["wvc_c"]),
        (W0f["Wk"], W0f["wk_c2"]), (W0f["Wv"], W0f["wv_c2"])])
    W0f["bg_wo"] = bg_cast(k, [(W0f["Wo"], W0f["wo_c"])])
    W0f["bg_ffn"] = bg_cast(k, [
        (W0f["W_shared_in"], W0f["wsi_c"]), (W0f["W1_shared"], W0f["w1s_c"]),
        (W0f["W_shared_out"], W0f["wso_c"])])
    W0f["bg_w2e"] = bg_cast(k, [(W0f["W2_experts"], W0f["w2e_c"])])
    W0a["bg_a0"] = bg_cast(k, [(W0a["wkvT"], W0a["wkvT_c"]),
                               (W0a["woT"], W0a["woT_c"])])
    prs = []
    for e in range(8):
        for q in range(8):
            prs.append((W1f["w1"][e][:, q * 512:(q + 1) * 512],
                        W1f["w1cache"][e, q]))
        prs.append((W1f["w2"][e].rearrange("(hc p) m -> hc p m", p=128),
                    W1f["w2cache"][e]))
    W1f["bg_e1"] = bg_cast(k, prs)
    prs = [(W1a["x_proj_w"], W1a["xp_cache"]),
           (W1a["dt_proj_w"], W1a["dt_cache"])]
    for j in range(32):
        prs.append((W1a["in_proj_w"][:, j * 128:(j + 1) * 128], W1a["in_cache"][j]))
    for oc in range(8):
        prs.append((W1a["out_proj_w"][:, oc * 128:(oc + 1) * 128], W1a["out_cache"][oc]))
    W1a["bg_m1"] = bg_cast(k, prs)


def emit_cast_wq(k, B, W0f, W0a):
    pairs = []
    for kc in range(8):
        r = slice(kc * 128, (kc + 1) * 128)
        pairs.append((W0f["Wq"][r, :], W0f["wq_c"][r, :], 0.125))
        pairs.append((W0a["wqT"][r, :], W0a["wqT_c"][r, :], 0.125))
    return emit_cast_pairs(k, B, pairs, engines=("dve",))


def build_fused_program():
    nc = bass.Bass("TRN2", target_bir_lowering=False)
    x_in = nc.dram_tensor("x_in", [T, D], F32, kind="ExternalInput").ap()
    x_out = nc.dram_tensor("x_out", [T, D], F32, kind="ExternalOutput").ap()
    W0f = declare_l0ffn_weights(nc, "f0_")
    W0f.update(make_l0ffn_consts(nc))
    W0a = declare_l0attn_weights(nc, "a0_")
    W0a.update(make_l0attn_consts(nc))
    W1f = declare_l1ffn_weights(nc, "e1_")
    W1a = declare_l1attn_weights(nc, "m1_")
    W1a.update(make_l1attn_consts(nc))
    k = Ctx(nc)
    B = build_common(nc, k)
    B["cast_b"] = [B["y_bf"][:, 0:2048], B["y_bf"][:, 2048:4096]]
    cv = Carver(nc)
    Bf0 = dict(B); carve_l0ffn_bufs(cv, Bf0)
    Ba0 = dict(B); cv.reset(); carve_l0attn_bufs(cv, Ba0)
    Bm1 = dict(B); cv.reset(); carve_l1attn_bufs(cv, Bm1)
    Be1 = dict(B); cv.reset(); carve_l1ffn_bufs(cv, Be1)

    emit_bg_casts(k, W0f, W0a, W1f, W1a)
    t_cast = emit_cast_wq(k, B, W0f, W0a)
    t_load = emit_load_x(k, B, x_in)
    after = (t_load,) + tuple(t_cast)
    t_x1, m_b, rstd_b, t_apply = emit_l0ffn(k, Bf0, W0f, after)
    emit_l0ffn_part2(k, Bf0, W0f, t_x1, m_b, rstd_b, t_apply)
    after = phase_barrier(k)
    emit_l0attn(k, Ba0, W0a, after)
    after = phase_barrier(k)
    t_x1, m_b, rstd_b, t_apply = emit_l0ffn(k, Bf0, W0f, after)
    emit_l0ffn_part2(k, Bf0, W0f, t_x1, m_b, rstd_b, t_apply)
    after = phase_barrier(k)
    emit_l1ffn(k, Be1, W1f, after)
    after = phase_barrier(k)
    emit_l1attn(k, Bm1, W1a, after)
    after = phase_barrier(k)
    toks = emit_l1ffn(k, Be1, W1f, after)
    emit_store_x(k, B, x_out, toks)
    k.eng["sp"].wait_ge(k.sem["sp"], k.cnt["sp"])
    k.eng["pool"].wait_ge(k.sem["pool"], k.cnt["pool"])
    return nc


def fused_in_map(inputs, shard):
    d = {"x_in": np.ascontiguousarray(shard)}
    for pfx, im in (("f0_", l0ffn_in_map), ("a0_", l0attn_in_map),
                    ("m1_", l1attn_in_map), ("e1_", l1ffn_in_map)):
        sub = im(inputs, shard)
        sub.pop("x_in")
        d.update({pfx + name: v for name, v in sub.items()})
    return d


_PROGS = {}


def _get_prog(name):
    if name not in _PROGS:
        _PROGS[name] = {
            "l0ffn": build_l0ffn_program, "l0attn": build_l0attn_program,
            "l1ffn": build_l1ffn_program, "l1attn": build_l1attn_program,
            "fused": build_fused_program,
        }[name]()
    return _PROGS[name]


def kernel(**inputs):
    x = np.asarray(inputs["L0_attn_x"], dtype=np.float32)
    nc = _get_prog("fused")
    res = run_bass_kernel_spmd(nc, [fused_in_map(inputs, x[b]) for b in range(8)],
                               core_ids=list(range(8)))
    return np.stack([np.asarray(res.results[b]["x_out"]) for b in range(8)],
                    0).astype(np.float32)


# version 13 (final)
# speedup vs baseline: 1.1518x; 1.1518x over previous
import numpy as np
import concourse.bass as bass
import concourse.mybir as mybir
from concourse.bass_utils import run_bass_kernel_spmd

F32 = mybir.dt.float32
BF16 = mybir.dt.bfloat16
AF = mybir.ActivationFunctionType
ALU = mybir.AluOpType
AX = mybir.AxisListType

D = 1024
T = 2048
NCH = 8
NTOK = 16
LN_EPS = 1e-5


def _np_bf16(x):
    import ml_dtypes
    return np.asarray(np.asarray(x, np.float32), dtype=ml_dtypes.bfloat16)


class EngWrap:

    _NOWAIT = {"wait_ge", "wait_op", "sem_inc", "nop", "drain"}

    def __init__(self, k, name, raw):
        self._k = k
        self._name = name
        self._raw = raw

    def __getattr__(self, attr):
        fn = getattr(self._raw, attr)
        if attr in self._NOWAIT or not callable(fn):
            return fn
        k, name = self._k, self._name

        def wrapped(*a, **kw):
            v = k.cnt[name]
            if v > k.watermark[name][name]:
                self._raw.wait_ge(k.sem[name], v)
                k.watermark[name][name] = v
            return fn(*a, **kw)
        return wrapped


class DmaWrap:

    _DMA = {"dma_start", "dma_start_transpose", "indirect_dma_start"}

    def __init__(self, k, name, raw):
        self._k = k
        self._name = name
        self._raw = raw

    def __getattr__(self, attr):
        fn = getattr(self._raw, attr)
        if attr not in self._DMA or not callable(fn):
            return fn
        k, name = self._k, self._name

        def wrapped(*a, **kw):
            v = k.cnt[name]
            if v > k.watermark[name][name]:
                self._raw.wait_ge(k.sem[name], v)
                k.watermark[name][name] = v
            return fn(*a, **kw)
        return wrapped


class Ctx:
    ENGS = ("pe", "act", "dve", "pool", "sp")

    def __init__(self, nc):
        self.nc = nc
        self.sem = {e: nc.alloc_semaphore(f"sem_{e}") for e in self.ENGS}
        self.sem["bgc"] = nc.alloc_semaphore("sem_bgc")
        self.sem["pwd"] = nc.alloc_semaphore("sem_pwd")
        self.cnt = {e: 0 for e in self.ENGS}
        self.cnt["bgc"] = 0
        self.cnt["pwd"] = 0
        self.watermark = {e: {p: 0 for p in self.ENGS + ("bgc", "pwd")}
                          for e in self.ENGS}
        raw = {
            "pe": nc.tensor, "act": nc.scalar, "dve": nc.vector,
            "pool": nc.gpsimd, "sp": nc.sync,
        }
        self.raw = raw
        self.eng = {
            "pe": raw["pe"], "sp": DmaWrap(self, "sp", raw["sp"]),
            "act": EngWrap(self, "act", raw["act"]),
            "dve": EngWrap(self, "dve", raw["dve"]),
            "pool": EngWrap(self, "pool", raw["pool"]),
        }

    def inc(self, e, inst, amt=1):
        inst.then_inc(self.sem[e], amt)
        self.cnt[e] += amt
        return (e, self.cnt[e])

    def tok(self, e):
        return (e, self.cnt[e])

    def wait(self, consumer, *tokens):
        for tk in tokens:
            if tk is None:
                continue
            (p, v) = tk
            if v <= 0:
                continue
            if v > self.watermark[consumer][p]:
                self.eng[consumer].wait_ge(self.sem[p], v)
                self.watermark[consumer][p] = v


def make_consts(nc):
    c = {}
    c["ones128"] = nc.inline_tensor(np.ones((128, 128), np.float32), "c_ones128")
    c["ident_f32"] = nc.inline_tensor(np.eye(128, dtype=np.float32), "c_ident_f32")
    sel = np.zeros((8, 8 * 128), np.float32)
    for e in range(8):
        sel[e, e * 128:(e + 1) * 128] = 1.0
    c["sel8"] = nc.inline_tensor(sel, "c_sel8")
    c["epscols"] = nc.inline_tensor(
        np.array([[1e-5, 1e-6, 1.0]], np.float32) * np.ones((128, 1), np.float32),
        "c_epscols")
    bo = np.zeros((128, 128), np.float32)
    bo[0:64, 0:64] = 1.0
    bo[64:128, 64:128] = 1.0
    c["blockones_sh"] = nc.inline_tensor(bo, "c_blockones_sh")
    c["ones_bf_sh"] = nc.inline_tensor(_np_bf16(np.ones((128, 128))), "c_ones_bf_sh")
    return c


def load_consts(k, c):
    nc, sp = k.nc, k.eng["sp"]
    sb = {}
    shapes = {
        "ones128": ([128, 128], F32), "ident_f32": ([128, 128], F32),
        "epscols": ([128, 3], F32),
    }
    tok = None
    for name, (shape, dt) in shapes.items():
        sb[name] = nc.alloc_sbuf_tensor("cs_" + name, shape, dt)
        i = sp.dma_start(out=sb[name][:], in_=c[name].ap())
        tok = k.inc("sp", i, 16)
    sb["_tok"] = tok
    return sb


def emit_load_x(k, B, x_dram):
    cs, x_fm, ps_b, xio = B["cs"], B["x_fm"], B["ps_b"], B["xio"]
    pe, dve, sp = k.eng["pe"], k.eng["dve"], k.eng["sp"]
    t_evac = {}
    t_pe_done = {}
    for tt in range(NTOK):
        bi = tt % 2
        if tt >= 2:
            k.wait("sp", t_pe_done[tt - 2])
        i = sp.dma_start(out=xio[:, bi * D:(bi + 1) * D],
                         in_=x_dram[tt * 128:(tt + 1) * 128, :])
        t_dma = k.inc("sp", i, 16)
        k.wait("pe", t_dma, cs["_tok"])
        for c in range(NCH):
            j = tt * NCH + c
            pcol = (j % 2) * 512
            if j >= 2:
                k.wait("pe", t_evac[j - 2])
            inst = pe.transpose(ps_b[:, pcol:pcol + 128],
                                xio[:, bi * D + c * 128: bi * D + (c + 1) * 128],
                                cs["ident_f32"][:])
            t_t = k.inc("pe", inst)
            k.wait("dve", t_t)
            i = dve.tensor_copy(x_fm[:, c * T + tt * 128: c * T + (tt + 1) * 128],
                                ps_b[:, pcol:pcol + 128])
            t_evac[j] = k.inc("dve", i)
        t_pe_done[tt] = k.tok("pe")
    return k.tok("dve")


def emit_store_x(k, B, out_dram, after):
    cs, x_fm, ps_b, xio = B["cs"], B["x_fm"], B["ps_b"], B["xio"]
    pe, dve, sp = k.eng["pe"], k.eng["dve"], k.eng["sp"]
    k.wait("pe", *after)
    t_evac = {}
    t_dma = {}
    last = None
    for tt in range(NTOK):
        bi = tt % 2
        if tt >= 2:
            k.wait("dve", t_dma[tt - 2])
        for c in range(NCH):
            j = tt * NCH + c
            pcol = (j % 2) * 512
            if j >= 2:
                k.wait("pe", t_evac[j - 2])
            inst = pe.transpose(ps_b[:, pcol:pcol + 128],
                                x_fm[:, c * T + tt * 128: c * T + (tt + 1) * 128],
                                cs["ident_f32"][:])
            t_t = k.inc("pe", inst)
            k.wait("dve", t_t)
            i = dve.tensor_copy(xio[:, bi * D + c * 128: bi * D + (c + 1) * 128],
                                ps_b[:, pcol:pcol + 128])
            t_evac[j] = k.inc("dve", i)
        k.wait("sp", k.tok("dve"))
        i = sp.dma_start(out=out_dram[tt * 128:(tt + 1) * 128, :],
                         in_=xio[:, bi * D:(bi + 1) * D])
        t_dma[tt] = last = k.inc("sp", i, 16)
    return last


def emit_ln(k, B, after=(), affine=None, fold_ln2=False):
    cs = B["cs"]
    x_fm, y_bf = B["x_fm"], B["y_bf"]
    scr0, m_b, rstd_b = B["xio"], B["scr"][0], B["scr"][1]
    ps_sum = [B["ps_a"], B["ps_b"]]
    ps_sqm = [B["ps_up"], B["ps_dn"]]
    pe, act, dve, pool = k.eng["pe"], k.eng["act"], k.eng["dve"], k.eng["pool"]

    def sum_tile(idx, ts):
        return ps_sum[ts // 2][:, (ts % 2) * 512:(ts % 2) * 512 + 512] if idx == 0 else \
            ps_sqm[ts // 2][:, (ts % 2) * 512:(ts % 2) * 512 + 512]

    k.wait("pe", *after, cs["_tok"])
    k.wait("act", *after)
    t_sq = {}
    for c in range(NCH):
        if c > 0:
            k.wait("act", t_mm_sq)
        i = act.activation(scr0[:], x_fm[:, c * T:(c + 1) * T], AF.Square)
        t_sqc = k.inc("act", i)
        k.wait("pe", t_sqc)
        for ts in range(4):
            inst = pe.matmul(sum_tile(0, ts), cs["ones128"][:],
                             x_fm[:, c * T + ts * 512: c * T + ts * 512 + 512],
                             start=(c == 0), stop=(c == NCH - 1))
        for ts in range(4):
            inst2 = pe.matmul(sum_tile(1, ts), cs["ones128"][:],
                              scr0[:, ts * 512:(ts + 1) * 512],
                              start=(c == 0), stop=(c == NCH - 1))
        t_mm_sq = k.inc("pe", inst2)
    t_stats = t_mm_sq

    k.wait("dve", t_stats)
    for ts in range(4):
        i = dve.tensor_scalar(m_b[:, ts * 512:(ts + 1) * 512], sum_tile(0, ts),
                              1.0 / D, None, ALU.mult)
        k.inc("dve", i)
        i = dve.tensor_scalar(rstd_b[:, ts * 512:(ts + 1) * 512], sum_tile(1, ts),
                              1.0 / D, None, ALU.mult)
        k.inc("dve", i)
    t_msum = k.tok("dve")
    k.wait("act", t_msum)
    i = act.activation(scr0[:], m_b[:], AF.Square)
    t_msq = k.inc("act", i)
    k.wait("dve", t_msq)
    i = dve.tensor_sub(rstd_b[:], rstd_b[:], scr0[:])
    t_var = k.inc("dve", i)
    k.wait("act", t_var)
    if fold_ln2:
        Rt = B["ln2_R"]
        k.wait("dve", t_var)
        i = dve.tensor_copy(Rt[:], rstd_b[:])
        t_varc = k.inc("dve", i)
        k.wait("act", t_varc)
    i = act.activation(rstd_b[:], rstd_b[:], AF.Ln, bias=cs["epscols"][:, 0:1])
    k.inc("act", i)
    i = act.activation(rstd_b[:], rstd_b[:], AF.Exp, scale=-0.5)
    t_rstd = k.inc("act", i)
    applyR = rstd_b
    if fold_ln2:
        k.wait("dve", t_rstd)
        i = dve.tensor_mul(Rt[:], Rt[:], rstd_b[:])
        k.inc("dve", i)
        i = dve.tensor_mul(Rt[:], Rt[:], rstd_b[:])
        t_v2 = k.inc("dve", i)
        k.wait("act", t_v2)
        i = act.activation(Rt[:], Rt[:], AF.Ln, bias=cs["epscols"][:, 0:1])
        k.inc("act", i)
        i = act.activation(Rt[:], Rt[:], AF.Exp, scale=-0.5)
        t_r2 = k.inc("act", i)
        k.wait("dve", t_r2)
        i = dve.tensor_mul(Rt[:], Rt[:], rstd_b[:])
        t_rstd = k.inc("dve", i)
        applyR = Rt

    toks = []
    k.wait("dve", t_rstd)
    for c in range(NCH):
        i = dve.tensor_sub(scr0[:], x_fm[:, c * T:(c + 1) * T], m_b[:])
        k.inc("dve", i)
        if affine is None:
            i = dve.tensor_mul(y_bf[:, c * T:(c + 1) * T], scr0[:], applyR[:])
        else:
            gcol, bcol = affine
            i = dve.tensor_mul(scr0[:], scr0[:], applyR[:])
            k.inc("dve", i)
            i = dve.tensor_scalar(y_bf[:, c * T:(c + 1) * T], scr0[:],
                                  gcol[:, c:c + 1], bcol[:, c:c + 1],
                                  ALU.mult, ALU.add)
        toks.append(k.inc("dve", i))
    return m_b, rstd_b, toks


def emit_l1ffn(k, B, W, after, dbg=None, parts="full"):
    nc = k.nc
    pe, act, dve, pool, sp = (k.eng[e] for e in ("pe", "act", "dve", "pool", "sp"))
    cs = B["cs"]
    x_fm, y_bf, scr = B["x_fm"], B["y_bf"], B["scr"]
    ps_a, ps_b, ps_up, ps_dn = B["ps_a"], B["ps_b"], B["ps_up"], B["ps_dn"]

    m_b, rstd_b, ln_toks = emit_ln(k, B, after)
    if dbg is not None:
        k.wait("pool", *ln_toks)
        i = pool.dma_start(out=dbg["m"], in_=m_b[:])
        k.inc("pool", i, 16)
        i = pool.dma_start(out=dbg["y"], in_=y_bf[:])
        k.inc("pool", i, 16)
        i = pool.dma_start(out=dbg["rstd"], in_=rstd_b[:])
        k.inc("pool", i, 16)

    gwT = B["g_gwT"]
    i = sp.dma_start(out=gwT[:], in_=W["gate_wT"].rearrange("(c p) e -> p c e", p=128))
    t_gw = k.inc("sp", i, 16)
    bias_bc = B["g_biasbc"]
    i = sp.dma_start(out=bias_bc[:, 0:8], in_=W["gate_b"][None, :].partition_broadcast(128))
    k.inc("sp", i, 16)
    i = sp.dma_start(out=bias_bc[:, 8:16], in_=W["expert_bias"][None, :].partition_broadcast(128))
    t_brow = k.inc("sp", i, 16)
    b1cols = B["f_b1cols"]
    i = sp.dma_start(out=b1cols[:], in_=W["b1"].rearrange("e (c p) -> p e c", p=128),
                     allow_slow_non_contiguous=True)
    t_b1 = k.inc("sp", i, 16)
    b2cols = B["f_b2cols"]
    i = sp.dma_start(out=b2cols[:], in_=W["b2"].rearrange("e (c p) -> p e c", p=128),
                     allow_slow_non_contiguous=True)
    t_b2 = k.inc("sp", i, 16)
    k.wait("dve", t_brow)
    i = dve.tensor_add(bias_bc[:, 0:8], bias_bc[:, 0:8], bias_bc[:, 8:16])
    t_biasbc = k.inc("dve", i)

    G_bc = B["g_grow"]
    k.wait("pe", t_gw, cs["_tok"], k.tok("dve"))
    for c in range(NCH):
        inst = pe.matmul(ps_a[:, 512:520], cs["ones128"][:], gwT[:, c * 8:(c + 1) * 8],
                         start=(c == 0), stop=(c == NCH - 1), skip_group_check=True)
    t_G = k.inc("pe", inst)
    k.wait("dve", t_G)
    i = dve.tensor_copy(G_bc[:], ps_a[:, 512:520])
    t_Gbc = k.inc("dve", i)

    rstd_tok = B["g_rstdtok"]
    s_tok = B["g_stok"]
    mblk = B["g_gterm"]
    t_prev = None
    for ts in range(NTOK):
        if t_prev is not None:
            k.wait("pe", t_prev)
        inst = pe.transpose(ps_b[:, 0:128],
                            m_b[:, ts * 128:(ts + 1) * 128], cs["ident_f32"][:])
        k.inc("pe", inst)
        inst = pe.transpose(ps_b[:, 512:640],
                            rstd_b[:, ts * 128:(ts + 1) * 128], cs["ident_f32"][:])
        t_t = k.inc("pe", inst)
        k.wait("dve", t_t)
        i = dve.tensor_copy(mblk[:], ps_b[:, 0:8])
        k.inc("dve", i)
        i = dve.tensor_copy(rstd_tok[:, ts * 8:(ts + 1) * 8], ps_b[:, 512:520])
        t_prev = k.inc("dve", i)
        i = dve.tensor_mul(s_tok[:, ts * 8:(ts + 1) * 8], mblk[:],
                           rstd_tok[:, ts * 8:(ts + 1) * 8])
        k.inc("dve", i)
    t_stok = k.tok("dve")

    if dbg is not None:
        k.wait("pool", t_stok)
        i = pool.dma_start(out=dbg["rstdtok"], in_=rstd_tok[:, 0:16])
        k.inc("pool", i, 16)

    mask05, lv, mx = B["g_mask"], B["g_l"], B["g_mx"]
    gterm = B["g_gterm"]
    k.wait("pe", t_biasbc, t_stok, *ln_toks)
    t_mask_ts = {}
    for ts in range(NTOK):
        if ts >= 1:
            k.wait("pe", t_mask_ts[ts - 1])
        for c in range(NCH):
            inst = pe.matmul(ps_a[:, 0:8],
                             x_fm[:, c * T + ts * 128: c * T + (ts + 1) * 128],
                             gwT[:, c * 8:(c + 1) * 8],
                             start=(c == 0), stop=(c == NCH - 1), skip_group_check=True)
        t_A = k.inc("pe", inst)
        k.wait("dve", t_A)
        i = dve.scalar_tensor_tensor(lv[:], ps_a[:, 0:8], rstd_tok[:, ts * 8:ts * 8 + 1],
                                     bias_bc[:, 0:8], ALU.mult, ALU.add)
        k.inc("dve", i)
        i = dve.tensor_scalar(gterm[:], G_bc[:], s_tok[:, ts * 8:ts * 8 + 1], None, ALU.mult)
        k.inc("dve", i)
        i = dve.tensor_sub(lv[:], lv[:], gterm[:])
        t_lv = k.inc("dve", i)
        i = dve.tensor_reduce(mx[:], lv[:], AX.X, ALU.max)
        k.inc("dve", i)
        i = dve.tensor_scalar(mask05[:, ts * 8:(ts + 1) * 8], lv[:], mx[:], 0.5,
                              ALU.is_ge, ALU.mult)
        t_mask_ts[ts] = k.inc("dve", i)
        if dbg is not None:
            i = dve.tensor_copy(B["g_lvdump"][:, ts * 8:(ts + 1) * 8], lv[:])
            k.inc("dve", i)

    if dbg is not None:
        k.wait("pool", t_mask_ts[NTOK - 1], k.tok("dve"))
        i = pool.dma_start(out=dbg["mask"], in_=mask05[:])
        k.inc("pool", i, 16)
        i = pool.dma_start(out=dbg["lv"], in_=B["g_lvdump"][:])
        k.inc("pool", i, 16)

    mask_fm = B["g_maskfm"]
    maskT = B["g_maskT"]
    k.wait("pe", t_mask_ts[NTOK - 1], t_prev)
    inst = pe.transpose(ps_b[:, 0:128], mask05[:], cs["ident_f32"][:])
    t_t = k.inc("pe", inst)
    k.wait("dve", t_t)
    i = dve.tensor_copy(maskT[:], ps_b[:, 0:128])
    t_mT = k.inc("dve", i)
    k.wait("sp", t_mT)
    i = sp.dma_start(out=W["mask_scratch"], in_=maskT[:])
    t_ms = k.inc("sp", i, 16)
    k.wait("sp", t_ms)
    i = sp.dma_start(out=mask_fm[:],
                     in_=W["mask_scratch"].rearrange("(ts e) p -> e ts p", e=8))
    t_maskfm = k.inc("sp", i, 16)

    if dbg is not None:
        k.wait("pool", t_maskfm)
        i = pool.dma_start(out=dbg["mt"], in_=mask_fm[:])
        k.inc("pool", i, 16)
    if parts == "gate":
        return (k.tok("dve"), k.tok("sp"))
    w1c = B["f_w1c"]
    w2c = B["f_w2c"]
    hbuf = B["f_h"]
    mbc = B["f_mbc"]
    w1cache, w2cache = W["w1cache"], W["w2cache"]
    tmp = B["xio"]
    k.wait("sp", W.get("bg_e1"))
    i = sp.dma_start(out=B["sel8_sb"][:], in_=B["c_inline"]["sel8"].ap())
    t_sel = k.inc("sp", i, 16)

    def bank(fc):
        t = (ps_a, ps_b, ps_up, ps_dn)[fc // 2]
        return t[:, (fc % 2) * 512:(fc % 2) * 512 + 512]

    t_w1_used = {}
    t_w2_used = {}
    t_up_done = {}
    t_evac = {}
    t_slot_read = {0: None, 1: None}
    t_mbc_read = ()
    t_mbc_last = None
    t_h_read = None
    w1_step = 0
    w2_step = 0
    for tq in range(4):
        tok0 = tq * 512
        for e in range(8):
            for hcq in range(8):
                bi = w1_step % 2
                prev = t_w1_used.get(w1_step - 2)
                if prev is not None:
                    k.wait("sp", prev)
                i = sp.dma_start(out=w1c[bi][:],
                                 in_=w1cache[e, hcq].rearrange("(c p) m -> p c m", p=128))
                t_w1 = k.inc("sp", i, 16)
                k.wait("pe", t_w1, *ln_toks)
                for hcl in range(4):
                    hc = hcq * 4 + hcl
                    up_ps = ps_up[:, (hc % 2) * 512:(hc % 2) * 512 + 512]
                    prev = t_up_done.get(hc - 2)
                    if prev is not None:
                        k.wait("pe", prev)
                    if hc < 2:
                        k.wait("pe", t_evac.get(4 + hc))
                    for kc in range(NCH):
                        inst = pe.matmul(
                            up_ps,
                            w1c[bi][:, kc * 512 + hcl * 128: kc * 512 + (hcl + 1) * 128],
                            y_bf[:, kc * T + tok0: kc * T + tok0 + 512],
                            start=(kc == 0), stop=(kc == NCH - 1))
                    t_up = k.inc("pe", inst)
                    eng, en = (act, "act") if hc % 2 == 0 else (dve, "dve")
                    k.wait(en, t_up, t_b1, t_h_read)
                    if en == "act":
                        i = act.activation(hbuf[:, hc * 512:(hc + 1) * 512], up_ps, AF.Relu,
                                           bias=b1cols[:, e * 32 + hc: e * 32 + hc + 1])
                    else:
                        i = dve.tensor_scalar(hbuf[:, hc * 512:(hc + 1) * 512], up_ps,
                                              b1cols[:, e * 32 + hc: e * 32 + hc + 1], 0.0,
                                              ALU.add, ALU.max)
                    t_up_done[hc] = k.inc(en, i)
                t_w1_used[w1_step] = k.tok("pe")
                w1_step += 1
            t_h_act, t_h_dve = k.tok("act"), k.tok("dve")

            k.wait("pe", t_maskfm, t_sel, t_evac.get(1), t_mbc_last)
            inst = pe.matmul(ps_a[:, 512:1024],
                             B["sel8_sb"][:, e * 128:(e + 1) * 128],
                             mask_fm[:, tok0:tok0 + 512], start=True, stop=True,
                             skip_group_check=True)
            t_mm = k.inc("pe", inst)
            k.wait("act", t_mm, *t_mbc_read)
            i = act.activation(mbc[:], ps_a[:, 512:1024][:, 0:512], AF.Copy)
            t_mbc = t_mbc_last = k.inc("act", i)
            if dbg is not None and e == 0 and tq == 0:
                k.wait("pool", t_h_act, t_h_dve)
                i = pool.dma_start(out=dbg["h"], in_=hbuf[:])
                k.inc("pool", i, 16)

            for j2 in range(16):
                bi = w2_step % 2
                prev = t_w2_used.get(w2_step - 2)
                if prev is not None:
                    k.wait("sp", prev)
                i = sp.dma_start(out=w2c[bi][:].rearrange("p (a m) -> p a m", a=2),
                                 in_=w2cache[e, j2 * 2:j2 * 2 + 2].transpose([1, 0, 2]))
                t_w2 = k.inc("sp", i, 16)
                k.wait("pe", t_w2, t_h_act, t_h_dve, t_mbc)
                if j2 == 0:
                    k.wait("pe", *[t_evac.get(fc) for fc in range(8)])
                for sub in range(2):
                    hc = j2 * 2 + sub
                    for fc in range(8):
                        inst = pe.matmul(bank(fc),
                                         w2c[bi][:, sub * 1024 + fc * 128:
                                                 sub * 1024 + (fc + 1) * 128],
                                         hbuf[:, hc * 512:(hc + 1) * 512],
                                         start=(hc == 0), stop=(hc == 31))
                        k.inc("pe", inst)
                t_w2_used[w2_step] = k.tok("pe")
                w2_step += 1
            t_dn = t_h_read = k.tok("pe")
            for fc in range(8):
                xsl = slice(fc * T + tok0, fc * T + tok0 + 512)
                if fc % 2 == 0:
                    k.wait("dve", t_dn, t_b2, t_mbc)
                    i = dve.scalar_tensor_tensor(tmp[:, 0:512], bank(fc),
                                                 b2cols[:, e * 8 + fc: e * 8 + fc + 1],
                                                 mbc[:], ALU.add, ALU.mult)
                    t_evac[fc] = k.inc("dve", i)
                    i = dve.tensor_add(x_fm[:, xsl], x_fm[:, xsl], tmp[:, 0:512])
                    k.inc("dve", i)
                else:
                    s = (fc // 2) % 2
                    off = 512 + s * 512
                    k.wait("dve", t_dn, t_b2, t_mbc, t_slot_read[s])
                    i = dve.scalar_tensor_tensor(tmp[:, off:off + 512], bank(fc),
                                                 b2cols[:, e * 8 + fc: e * 8 + fc + 1],
                                                 mbc[:], ALU.add, ALU.mult)
                    t_evac[fc] = k.inc("dve", i)
                    k.wait("pool", t_evac[fc])
                    i = pool.tensor_add(x_fm[:, xsl], x_fm[:, xsl], tmp[:, off:off + 512])
                    t_slot_read[s] = k.inc("pool", i)
            t_mbc_read = (k.tok("dve"),)
        t_up_done = {}
    return (k.tok("dve"), k.tok("pool"))


def emit_cast_l1ffn(k, B, W):
    nc = k.nc
    dve, pool, act, sp = k.eng["dve"], k.eng["pool"], k.eng["act"], k.eng["sp"]
    fbuf = [B["scr"][0], B["scr"][1]]
    bbuf = B["cast_b"]
    w1, w2 = W["w1"], W["w2"]
    w1cache, w2cache = W["w1cache"], W["w2cache"]
    engs = [("dve", dve), ("pool", pool)]
    t_store = {}
    t_cast = {}
    step = 0

    def do(load_in, store_out):
        nonlocal step
        bi = step % 2
        prev = t_cast.get(step - 2)
        if prev is not None:
            k.wait("sp", prev)
        i = sp.dma_start(out=fbuf[bi][:], in_=load_in)
        t_l = k.inc("sp", i, 16)
        en, eng = engs[step % 2]
        k.wait(en, t_l, t_store.get(step - 2))
        i = eng.tensor_copy(bbuf[bi], fbuf[bi][:])
        t_cast[step] = k.inc(en, i)
        k.wait("sp", t_cast[step])
        i = sp.dma_start(out=store_out, in_=bbuf[bi])
        t_store[step] = k.inc("sp", i, 16)
        step += 1

    for e in range(8):
        for kc in range(8):
            for half in range(2):
                do(w1[e, kc * 128:(kc + 1) * 128, half * 2048:(half + 1) * 2048],
                   w1cache[e, half * 4:(half + 1) * 4, kc * 128:(kc + 1) * 128, :]
                   .transpose([1, 0, 2]))
        for h2 in range(16):
            do(w2[e, h2 * 256:(h2 + 1) * 256, :].rearrange("(c p) m -> p c m", p=128),
               w2cache[e, h2 * 2:(h2 + 1) * 2].transpose([1, 0, 2]))
    return t_store[step - 1], t_cast[step - 1], k.tok("dve"), k.tok("pool")


def build_common(nc, k):
    B = {}
    c_inline = make_consts(nc)
    B["c_inline"] = c_inline
    B["cs"] = load_consts(k, c_inline)
    B["x_fm"] = nc.alloc_sbuf_tensor("x_fm", [128, NCH * T], F32)
    B["y_bf"] = nc.alloc_sbuf_tensor("y_bf", [128, NCH * T], BF16)
    B["scr"] = [nc.alloc_sbuf_tensor(f"scr{i}", [128, T], F32) for i in range(2)]
    B["xio"] = nc.alloc_sbuf_tensor("xio", [128, 2 * D], F32)
    B["ps_a"] = nc.alloc_psum_tensor("ps_a", [128, 1024], F32)
    B["ps_b"] = nc.alloc_psum_tensor("ps_b", [128, 1024], F32)
    B["ps_up"] = nc.alloc_psum_tensor("ps_up", [128, 1024], F32)
    B["ps_dn"] = nc.alloc_psum_tensor("ps_dn", [128, 1024], F32)
    return B


def build_l1ffn_bufs(nc, B):
    B["g_gwT"] = nc.alloc_sbuf_tensor("g_gwT", [128, 64], F32)
    B["g_brow"] = nc.alloc_sbuf_tensor("g_brow", [1, 24], F32)
    B["g_grow"] = nc.alloc_sbuf_tensor("g_grow", [128, 8], F32)
    B["g_biasbc"] = nc.alloc_sbuf_tensor("g_biasbc", [128, 16], F32)
    B["g_rstdtok"] = nc.alloc_sbuf_tensor("g_rstdtok", [128, 128], F32)
    B["g_mask"] = nc.alloc_sbuf_tensor("g_mask", [128, 16 * 8], F32)
    B["g_maskfm"] = nc.alloc_sbuf_tensor("g_maskfm", [8, T], F32)
    B["g_l"] = nc.alloc_sbuf_tensor("g_l", [128, 8], F32)
    B["g_mx"] = nc.alloc_sbuf_tensor("g_mx", [128, 1], F32)
    B["f_b1cols"] = nc.alloc_sbuf_tensor("f_b1cols", [128, 8 * 32], F32)
    B["f_b2cols"] = nc.alloc_sbuf_tensor("f_b2cols", [128, 8 * 8], F32)
    B["f_w1c"] = [nc.alloc_sbuf_tensor(f"f_w1c{i}", [128, 4096], BF16) for i in range(2)]
    B["f_w2c"] = [nc.alloc_sbuf_tensor(f"f_w2c{i}", [128, 2 * 1024], BF16) for i in range(2)]
    B["f_h"] = nc.alloc_sbuf_tensor("f_h", [128, 32 * 512], BF16)
    B["f_mbc"] = nc.alloc_sbuf_tensor("f_mbc", [128, 512], F32)
    B["g_lvdump"] = nc.alloc_sbuf_tensor("g_lvdump", [128, 128], F32)
    B["g_stok"] = nc.alloc_sbuf_tensor("g_stok", [128, 128], F32)
    B["g_gterm"] = nc.alloc_sbuf_tensor("g_gterm", [128, 8], F32)
    B["g_maskT"] = nc.alloc_sbuf_tensor("g_maskT", [128, 128], F32)
    B["sel8_sb"] = nc.alloc_sbuf_tensor("sel8_sb", [8, 8 * 128], F32)
    B["cast_b"] = [B["y_bf"][:, 0:2048], B["y_bf"][:, 2048:4096]]


def declare_l1ffn_weights(nc, pfx=""):
    W = {}
    W["gate_wT"] = nc.dram_tensor(pfx + "gate_wT", [D, 8], F32, kind="ExternalInput").ap()
    W["gate_b"] = nc.dram_tensor(pfx + "gate_b", [8], F32, kind="ExternalInput").ap()
    W["expert_bias"] = nc.dram_tensor(pfx + "expert_bias", [8], F32, kind="ExternalInput").ap()
    W["w1"] = nc.dram_tensor(pfx + "w1", [8, D, 4096], F32, kind="ExternalInput").ap()
    W["b1"] = nc.dram_tensor(pfx + "b1", [8, 4096], F32, kind="ExternalInput").ap()
    W["w2"] = nc.dram_tensor(pfx + "w2", [8, 4096, D], F32, kind="ExternalInput").ap()
    W["b2"] = nc.dram_tensor(pfx + "b2", [8, D], F32, kind="ExternalInput").ap()
    W["w1cache"] = nc.dram_tensor(pfx + "w1cache", [8, 8, 1024, 512], BF16, kind="Internal").ap()
    W["w2cache"] = nc.dram_tensor(pfx + "w2cache", [8, 32, 128, 1024], BF16, kind="Internal").ap()
    W["mask_scratch"] = nc.dram_tensor(pfx + "mask_scratch", [128, 128], F32, kind="Internal").ap()
    return W


def l1ffn_in_map(inputs, shard):
    return {
        "x_in": np.ascontiguousarray(shard),
        "gate_wT": np.ascontiguousarray(inputs["L1_ffn_gate_w"].T),
        "gate_b": inputs["L1_ffn_gate_b"],
        "expert_bias": inputs["L1_ffn_expert_bias"],
        "w1": inputs["L1_ffn_w1"],
        "b1": inputs["L1_ffn_b1"],
        "w2": inputs["L1_ffn_w2"],
        "b2": inputs["L1_ffn_b2"],
    }


def build_l1ffn_program(debug=False, parts="full"):
    nc = bass.Bass("TRN2", target_bir_lowering=False)
    x_in = nc.dram_tensor("x_in", [T, D], F32, kind="ExternalInput").ap()
    x_out = nc.dram_tensor("x_out", [T, D], F32, kind="ExternalOutput").ap()
    W = declare_l1ffn_weights(nc)
    k = Ctx(nc)
    B = build_common(nc, k)
    build_l1ffn_bufs(nc, B)
    if debug:
        B["dbg"] = {
            "x": nc.dram_tensor("dbg_x", [128, NCH * T], F32, kind="ExternalOutput").ap(),
            "y": nc.dram_tensor("dbg_y", [128, NCH * T], BF16, kind="ExternalOutput").ap(),
            "mask": nc.dram_tensor("dbg_mask", [128, 128], F32, kind="ExternalOutput").ap(),
            "h": nc.dram_tensor("dbg_h", [128, 32 * 512], BF16, kind="ExternalOutput").ap(),
            "m": nc.dram_tensor("dbg_m", [128, T], F32, kind="ExternalOutput").ap(),
            "rstd": nc.dram_tensor("dbg_rstd", [128, T], F32, kind="ExternalOutput").ap(),
            "lv": nc.dram_tensor("dbg_lv", [128, 16 * 8], F32, kind="ExternalOutput").ap(),

            "rstdtok": nc.dram_tensor("dbg_rstdtok", [128, 16], F32, kind="ExternalOutput").ap(),
            "mt": nc.dram_tensor("dbg_mt", [8, T], F32, kind="ExternalOutput").ap(),
            "mx": nc.dram_tensor("dbg_mx", [128, 16], F32, kind="ExternalOutput").ap(),
        }

    if parts in ("full", "castln", "castgate"):
        t_cast = emit_cast_l1ffn(k, B, W)
    else:
        t_cast = ()
    t_load = emit_load_x(k, B, x_in)
    if debug:
        k.wait("pool", t_load)
        i = k.eng["pool"].dma_start(out=B["dbg"]["x"], in_=B["x_fm"][:])
        k.inc("pool", i, 16)
    after = (t_load,) + tuple(t_cast)
    if parts in ("ln", "castln"):
        m_b, rstd_b, ln_toks = emit_ln(k, B, after)
        if debug:
            k.wait("pool", *ln_toks)
            i = k.eng["pool"].dma_start(out=B["dbg"]["y"], in_=B["y_bf"][:])
            k.inc("pool", i, 16)
            i = k.eng["pool"].dma_start(out=B["dbg"]["m"], in_=m_b[:])
            k.inc("pool", i, 16)
            i = k.eng["pool"].dma_start(out=B["dbg"]["rstd"], in_=rstd_b[:])
            k.inc("pool", i, 16)
        toks = tuple(ln_toks)
    else:
        toks = emit_l1ffn(k, B, W, after, dbg=B.get("dbg"),
                          parts="gate" if parts in ("gate", "gate1", "castgate") else "full")
    t_store = emit_store_x(k, B, x_out, toks)
    k.eng["sp"].wait_ge(k.sem["sp"], k.cnt["sp"])
    k.eng["pool"].wait_ge(k.sem["pool"], k.cnt["pool"])
    return nc



def make_l0attn_consts(nc):
    import ml_dtypes
    c = {}
    j = np.arange(T, dtype=np.float32)
    c["iota_rep"] = nc.inline_tensor(_np_bf16(np.tile(j[None, :1024], (128, 1))), "c_iota_rep")
    aug = np.zeros((16, 4, T), np.float32)
    for h in range(16):
        slope = 2.0 ** (-(h + 1) / 2.0)
        v = (slope * j).astype(np.float64)
        h1 = v.astype(ml_dtypes.bfloat16).astype(np.float64)
        h2 = (v - h1).astype(np.float32).astype(ml_dtypes.bfloat16).astype(np.float64)
        h3 = (v - h1 - h2).astype(np.float32)
        aug[h, 0] = 1.0
        aug[h, 1] = h1.astype(np.float32)
        aug[h, 2] = h2.astype(np.float32)
        aug[h, 3] = h3
    c["aug_k"] = nc.inline_tensor(_np_bf16(aug.reshape(64, T)), "c_aug_k")
    mask = np.zeros((4, 128, 512), np.float32)
    for d in range(4):
        jl = np.arange(128)[:, None] + 128 * d
        ql = np.arange(512)[None, :]
        mask[d] = np.where(jl > ql, -1e9, 0.0)
    c["cmask"] = nc.inline_tensor(_np_bf16(mask.reshape(512, 512)), "c_cmask")
    c["ident_bf"] = nc.inline_tensor(_np_bf16(np.eye(128)), "c_identbf_a")
    c["ones_bf"] = nc.inline_tensor(_np_bf16(np.ones((128, 128))), "c_ones_bf")
    bo = np.zeros((128, 128), np.float32)
    bo[0:64, 0:64] = 1.0
    bo[64:128, 64:128] = 1.0
    c["blockones"] = nc.inline_tensor(bo, "c_blockones")
    c["ones3"] = nc.inline_tensor(_np_bf16(np.ones((3, T))), "c_ones3")
    return c


def build_l0attn_bufs(nc, B):
    B["a_qT"] = nc.alloc_sbuf_tensor("a_qT", [128, 8 * T], BF16)
    B["a_kT"] = nc.alloc_sbuf_tensor("a_kT", [128, 2 * T], BF16)
    B["a_vtok"] = nc.alloc_sbuf_tensor("a_vtok", [128, 4 * 16 * 64], BF16)
    B["a_wbuf"] = nc.alloc_sbuf_tensor("a_wbuf", [128, 8 * 1024], BF16)
    B["a_kaug"] = nc.alloc_sbuf_tensor("a_kaug", [4, T], BF16)
    B["a_qaug"] = nc.alloc_sbuf_tensor("a_qaug", [4, T], BF16)
    B["a_qscr"] = nc.alloc_sbuf_tensor("a_qscr", [64, 512], BF16)
    B["a_kscr"] = nc.alloc_sbuf_tensor("a_kscr", [64, T], BF16)
    B["a_stexp"] = nc.alloc_sbuf_tensor("a_stexp", [128, 2 * 512], BF16)
    B["a_iota"] = nc.alloc_sbuf_tensor("a_iota", [128, 1024], BF16)
    B["a_onesbf"] = nc.alloc_sbuf_tensor("a_onesbf", [128, 128], BF16)
    B["a_blockones"] = nc.alloc_sbuf_tensor("a_blockones", [128, 128], F32)
    B["a_cmask"] = nc.alloc_sbuf_tensor("a_cmask", [128, 4 * 512], BF16)
    B["a_identbf"] = nc.alloc_sbuf_tensor("a_identbf", [128, 128], BF16)
    B["a_mcol"] = nc.alloc_sbuf_tensor("a_mcol", [128, 18], F32)


def emit_cast_pairs(k, B, pairs, scale=None, engines=("dve", "pool")):
    sp = k.eng["sp"]
    engs = [(e, k.eng[e]) for e in engines]
    fbuf = [B["scr"][0], B["scr"][1]]
    bbuf = B["cast_b"]
    t_store = {}
    t_cast = {}
    for step, pr in enumerate(pairs):
        src, dst = pr[0], pr[1]
        sc = pr[2] if len(pr) > 2 else None
        ncols = src.shape[-1]
        bi = step % 2
        prev = t_cast.get(step - 2)
        if prev is not None:
            k.wait("sp", prev)
        i = sp.dma_start(out=fbuf[bi][:, 0:ncols], in_=src)
        t_l = k.inc("sp", i, 16)
        en, eng = engs[step % len(engs)]
        k.wait(en, t_l, t_store.get(step - 2))
        if sc is None:
            i = eng.tensor_copy(bbuf[bi][:, 0:ncols], fbuf[bi][:, 0:ncols])
        else:
            i = eng.tensor_scalar(bbuf[bi][:, 0:ncols], fbuf[bi][:, 0:ncols],
                                  float(sc), None, ALU.mult)
        t_cast[step] = k.inc(en, i)
        k.wait("sp", t_cast[step])
        i = sp.dma_start(out=dst, in_=bbuf[bi][:, 0:ncols])
        t_store[step] = k.inc("sp", i, 16)
    return (t_store[len(pairs) - 1], t_cast[len(pairs) - 1],
            k.tok("dve"), k.tok("pool"))


def emit_cast_l0attn(k, B, W):
    pairs = []
    for kc in range(8):
        pairs.append((W["wqT"][kc * 128:(kc + 1) * 128, :],
                      W["wqT_c"][kc * 128:(kc + 1) * 128, :], 0.125))
        pairs.append((W["wkvT"][kc * 128:(kc + 1) * 128, :],
                      W["wkvT_c"][kc * 128:(kc + 1) * 128, :]))
        pairs.append((W["woT"][kc * 128:(kc + 1) * 128, :],
                      W["woT_c"][kc * 128:(kc + 1) * 128, :]))
    return emit_cast_pairs(k, B, pairs)


def emit_l0attn(k, B, W, after, dbg=None):
    nc = k.nc
    pe = k.eng["pe"]
    act, dve, pool, sp = k.eng["act"], k.eng["dve"], k.eng["pool"], k.eng["sp"]
    cs = B["cs"]
    x_fm, y_bf, scr = B["x_fm"], B["y_bf"], B["scr"]
    ps_a, ps_b, ps_up, ps_dn = B["ps_a"], B["ps_b"], B["ps_up"], B["ps_dn"]
    qT, kT, vtok = B["a_qT"], B["a_kT"], B["a_vtok"]
    wbuf = B["a_wbuf"]
    kvw = wbuf

    m_b, rstd_b, ln_toks = emit_ln(k, B, after)

    i = sp.dma_start(out=B["a_iota"][:], in_=W["iota_rep"].ap())
    k.inc("sp", i, 16)
    i = sp.dma_start(out=B["a_onesbf"][:], in_=W["ones_bf"].ap())
    k.inc("sp", i, 16)
    i = sp.dma_start(out=B["a_blockones"][:], in_=W["blockones"].ap())
    k.inc("sp", i, 16)
    i = sp.dma_start(out=B["a_cmask"][:],
                     in_=W["cmask"].ap().rearrange("(d p) m -> p d m", p=128))
    k.inc("sp", i, 16)
    i = sp.dma_start(out=B["a_identbf"][:], in_=W["ident_bf"].ap())
    t_csts = k.inc("sp", i, 16)

    i = sp.dma_start(out=wbuf[:], in_=W["wqT_c"].rearrange("(c p) m -> p c m", p=128))
    t_wq = k.inc("sp", i, 16)

    t_evac = {}
    k.wait("pe", t_wq, t_csts, *ln_toks)
    pj = 0
    for fc in range(8):
        for tsub in range(4):
            pcol = (pj % 2) * 512
            prev = t_evac.get(pj - 2)
            if prev is not None:
                k.wait("pe", prev)
            for kc in range(NCH):
                inst = pe.matmul(ps_b[:, pcol:pcol + 512],
                                 wbuf[:, kc * 1024 + fc * 128: kc * 1024 + (fc + 1) * 128],
                                 y_bf[:, kc * T + tsub * 512: kc * T + tsub * 512 + 512],
                                 start=(kc == 0), stop=(kc == NCH - 1))
            t_mm = k.inc("pe", inst)
            en, eng = ("act", act) if pj % 2 == 0 else ("dve", dve)
            k.wait(en, t_mm)
            if en == "act":
                i = act.activation(qT[:, fc * T + tsub * 512: fc * T + tsub * 512 + 512],
                                   ps_b[:, pcol:pcol + 512], AF.Copy)
            else:
                i = dve.tensor_copy(qT[:, fc * T + tsub * 512: fc * T + tsub * 512 + 512],
                                    ps_b[:, pcol:pcol + 512])
            t_evac[pj] = k.inc(en, i)
            pj += 1
    k.wait("sp", k.tok("pe"), W.get("bg_a0"))
    i = sp.dma_start(out=wbuf[:, 0:4096], in_=W["wkvT_c"].rearrange("(c p) m -> p c m", p=128))
    t_wkv = k.inc("sp", i, 16)
    k.wait("pe", t_wkv)
    for g in range(4):
        for tsub in range(4):
            pcol = (pj % 2) * 512
            prev = t_evac.get(pj - 2)
            if prev is not None:
                k.wait("pe", prev)
            for kc in range(NCH):
                inst = pe.matmul(ps_b[0:64, pcol:pcol + 512],
                                 kvw[:, kc * 512 + g * 128: kc * 512 + g * 128 + 64],
                                 y_bf[:, kc * T + tsub * 512: kc * T + tsub * 512 + 512],
                                 start=(kc == 0), stop=(kc == NCH - 1))
            t_mm = k.inc("pe", inst)
            en, eng = ("act", act) if pj % 2 == 0 else ("dve", dve)
            k.wait(en, t_mm)
            dstr = kT[(g % 2) * 64:(g % 2) * 64 + 64,
                      (g // 2) * T + tsub * 512:(g // 2) * T + tsub * 512 + 512]
            if en == "act":
                i = act.activation(dstr, ps_b[0:64, pcol:pcol + 512], AF.Copy)
            else:
                i = dve.tensor_copy(dstr, ps_b[0:64, pcol:pcol + 512])
            t_evac[pj] = k.inc(en, i)
            pj += 1
    i = pool.memset(vtok[:], 1.0)
    t_vset = k.inc("pool", i)
    for g in range(4):
        for tc in range(NTOK):
            pcol = (pj % 2) * 512
            prev = t_evac.get(pj - 2)
            if prev is not None:
                k.wait("pe", prev)
            for kc in range(NCH):
                inst = pe.matmul(ps_b[:, pcol:pcol + 64],
                                 y_bf[:, kc * T + tc * 128: kc * T + (tc + 1) * 128],
                                 kvw[:, kc * 512 + g * 128 + 64: kc * 512 + (g + 1) * 128],
                                 start=(kc == 0), stop=(kc == NCH - 1))
            t_mm = k.inc("pe", inst)
            en, eng = ("act", act) if pj % 2 == 0 else ("dve", dve)
            k.wait(en, t_mm, t_vset)
            base = (g * 16 + tc) * 64
            if en == "act":
                i = act.activation(vtok[:, base:base + 64], ps_b[:, pcol:pcol + 64], AF.Copy)
            else:
                i = dve.tensor_copy(vtok[:, base:base + 64], ps_b[:, pcol:pcol + 64])
            t_evac[pj] = k.inc(en, i)
            pj += 1
    t_proj_pe = k.tok("pe")
    t_proj = (k.tok("act"), k.tok("dve"))

    mcol = B["a_mcol"]
    sqt = B["xio"]
    def norm_stats(src_chunks, lhs_ones, outcol):
        t_sq = None
        t_entry_dve = k.tok("dve")
        t_entry_pe = k.tok("pe")
        for ci, ch in enumerate(src_chunks):
            en, eng = ("act", act) if ci % 2 == 0 else ("dve", dve)
            k.wait(en, t_sq, t_entry_pe)
            if en == "act":
                i = act.activation(sqt[:], ch, AF.Square)
            else:
                i = dve.tensor_mul(sqt[:], ch, ch)
            t_sqc = k.inc(en, i)
            k.wait("pe", t_sqc, t_entry_dve)
            for ts4 in range(4):
                tile = (ps_a if ts4 < 2 else ps_b)[:, (ts4 % 2) * 512:(ts4 % 2) * 512 + 512]
                inst = pe.matmul(tile, lhs_ones,
                                 sqt[:, ts4 * 512:(ts4 + 1) * 512],
                                 start=(ci == 0), stop=(ci == len(src_chunks) - 1))
            t_sq = k.inc("pe", inst)
        k.wait("dve", t_sq)
        for ts4 in range(4):
            tile = (ps_a if ts4 < 2 else ps_b)[:, (ts4 % 2) * 512:(ts4 % 2) * 512 + 512]
            i = dve.tensor_reduce(mcol[:, ts4:ts4 + 1], tile, AX.X, ALU.max)
            k.inc("dve", i)
        i = dve.tensor_reduce(mcol[:, outcol:outcol + 1], mcol[:, 0:4], AX.X, ALU.max)
        return k.inc("dve", i)
    k.wait("dve", *t_proj)
    k.wait("act", *t_proj)
    t_q2 = norm_stats([qT[:, c * T:(c + 1) * T] for c in range(8)],
                      cs["ones128"][:], 8)
    t_k2 = norm_stats([kT[:, c * T:(c + 1) * T] for c in range(2)],
                      B["a_blockones"][:], 9)
    rep = scr[0]
    k.wait("dve", t_q2, t_k2)
    i = dve.tensor_scalar(rep[:, 0:128], cs["ones128"][:], mcol[:, 8:9], None, ALU.mult)
    k.inc("dve", i)
    i = dve.tensor_scalar(rep[:, 128:256], cs["ones128"][:], mcol[:, 9:10], None, ALU.mult)
    t_rep = k.inc("dve", i)
    k.wait("pe", t_rep)
    inst = pe.transpose(ps_a[:, 0:128], rep[:, 0:128], cs["ident_f32"][:])
    k.inc("pe", inst)
    inst = pe.transpose(ps_a[:, 512:640], rep[:, 128:256], cs["ident_f32"][:])
    t_tr = k.inc("pe", inst)
    k.wait("dve", t_tr)
    i = dve.tensor_copy(rep[:, 0:128], ps_a[:, 0:128])
    k.inc("dve", i)
    i = dve.tensor_copy(rep[:, 128:256], ps_a[:, 512:640])
    k.inc("dve", i)
    i = dve.tensor_reduce(mcol[:, 10:11], rep[:, 0:128], AX.X, ALU.max)
    k.inc("dve", i)
    i = dve.tensor_reduce(mcol[:, 11:12], rep[:, 128:256], AX.X, ALU.max)
    k.inc("dve", i)
    i = dve.tensor_scalar(mcol[:, 12:13], mcol[:, 10:11], mcol[:, 11:12], None, ALU.mult)
    t_mm2 = k.inc("dve", i)
    k.wait("act", t_mm2)
    i = act.activation(mcol[:, 13:14], mcol[:, 12:13], AF.Sqrt)
    t_sqr = k.inc("act", i)
    k.wait("dve", t_sqr)
    i = dve.tensor_scalar(mcol[:, 14:15], mcol[:, 13:14], 1.25, None, ALU.mult)
    t_M = k.inc("dve", i)

    if dbg is not None:
        k.wait("pool", t_M, *t_proj)
        i = pool.dma_start(out=dbg["mcol"], in_=B["a_mcol"][:])
        k.inc("pool", i, 16)
        i = pool.dma_start(out=dbg["qT"], in_=qT[:])
        k.inc("pool", i, 16)
        i = pool.dma_start(out=dbg["kT"], in_=kT[:])
        k.inc("pool", i, 16)
        i = pool.dma_start(out=dbg["vtok"], in_=vtok[:])
        k.inc("pool", i, 16)

    attnout = y_bf
    kaug, qaug, qscr = B["a_kaug"], B["a_qaug"], B["a_qscr"]
    stexp = B["a_stexp"]
    recip = B["xio"]
    gbf = stexp
    t_head_dma = None
    t_ev_q = None
    t_ost = {}
    t_stread = {}
    st_war = {0: None, 1: None}
    t_wo_dma = None
    t_kscr_read = None
    for h in range(16):
        g = h // 4
        slope = 2.0 ** (-(h + 1) / 2.0)
        if g % 2 == 1 and h % 4 == 0:
            k.wait("dve", t_kscr_read, *t_proj)
            i = dve.tensor_copy(B["a_kscr"][:], kT[64:128, (g // 2) * T:(g // 2 + 1) * T])
            t_kscr = k.inc("dve", i)
        if g % 2 == 1:
            def krow(kt):
                return B["a_kscr"][:, kt * 128:(kt + 1) * 128]
        else:
            def krow(kt):
                return kT[0:64, (g // 2) * T + kt * 128:(g // 2) * T + (kt + 1) * 128]
        k.wait("sp", k.tok("dve"), k.tok("pe"))
        i = sp.dma_start(out=kaug[:], in_=W["aug_k"].ap()[h * 4:(h + 1) * 4, :])
        t_ka = k.inc("sp", i, 16)
        if h == 0:
            i = sp.dma_start(out=qaug[1:4, :], in_=W["ones3"].ap())
            k.inc("sp", i, 16)
        k.wait("dve", t_M, t_ka)
        i = dve.tensor_scalar(mcol[:, 16:17], mcol[:, 14:15], 1024.0 * slope, None, ALU.add)
        k.inc("dve", i)
        t_qa = None
        for gh in range(2):
            k.wait("dve", t_qa, t_kscr_read)
            i = dve.tensor_scalar(gbf[:, 0:1024], B["a_iota"][:],
                                  -slope, mcol[:, 14 + 2 * gh:15 + 2 * gh],
                                  ALU.mult, ALU.subtract)
            t_g = k.inc("dve", i)
            k.wait("sp", t_g)
            t_qa = k.inc("sp", sp.dma_start(
                out=qaug[0:1, gh * 1024:(gh + 1) * 1024], in_=gbf[0:1, 0:1024]), 16)
        qch = h // 2
        for qc in range(4):
            if h % 2 == 0:
                qrv = qT[0:64, qch * T + qc * 512: qch * T + qc * 512 + 512]
                t_ev_q = None
            else:
                k.wait("dve", t_stread.get("qscr"))
                i = dve.tensor_copy(qscr[:],
                                    qT[64:128, qch * T + qc * 512: qch * T + qc * 512 + 512])
                t_ev_q = k.inc("dve", i)
                qrv = qscr[:]
            ktmax = 4 * qc + 3
            st_i = 0
            o_ps = ps_up[:, (qc % 2) * 512:(qc % 2) * 512 + 512]
            r_ps = ps_dn[:, (qc % 2) * 512:(qc % 2) * 512 + 512]
            prev = t_ost.get((h * 4 + qc) - 2)
            if prev is not None:
                k.wait("pe", prev)
            for kt in range(ktmax + 1):
                st_ps = ps_a[:, (st_i % 2) * 512:(st_i % 2) * 512 + 512]
                k.wait("pe", st_war[st_i % 2])
                k.wait("pe", t_qa, (t_ev_q if h % 2 else None), t_proj_pe, t_M)
                if g % 2 == 1:
                    k.wait("pe", t_kscr)
                inst = pe.matmul(st_ps, krow(kt), qrv, start=True, stop=False)
                k.inc("pe", inst)
                delta = kt - 4 * qc
                last = not (0 <= delta <= 3)
                inst = pe.matmul(st_ps, kaug[:, kt * 128:(kt + 1) * 128],
                                 qaug[:, qc * 512:qc * 512 + 512],
                                 start=False, stop=last)
                k.inc("pe", inst)
                if not last:
                    inst = pe.matmul(st_ps, B["a_identbf"][:],
                                     B["a_cmask"][:, delta * 512:(delta + 1) * 512],
                                     start=False, stop=True)
                    k.inc("pe", inst)
                t_st = k.tok("pe")
                k.wait("act", t_st)
                i = act.activation(stexp[:, (st_i % 2) * 512:(st_i % 2) * 512 + 512],
                                   st_ps, AF.Exp)
                t_exp = k.inc("act", i)
                k.wait("pe", t_exp)
                vbase = (g * 16 + kt) * 64
                inst = pe.matmul(o_ps[0:64, :], vtok[:, vbase:vbase + 64],
                                 stexp[:, (st_i % 2) * 512:(st_i % 2) * 512 + 512],
                                 start=(kt == 0), stop=(kt == ktmax))
                k.inc("pe", inst)
                inst = pe.matmul(r_ps, B["a_onesbf"][:],
                                 stexp[:, (st_i % 2) * 512:(st_i % 2) * 512 + 512],
                                 start=(kt == 0), stop=(kt == ktmax))
                st_war[st_i % 2] = t_exp
                t_stread[st_i] = k.inc("pe", inst)
                st_i += 1
            t_pv = k.tok("pe")
            k.wait("dve", t_pv)
            i = dve.reciprocal(recip[:, 0:512], r_ps)
            k.inc("dve", i)
            i = dve.tensor_mul(scr[0][0:64, 0:512], o_ps[0:64, 0:512], recip[0:64, 0:512])
            k.inc("dve", i)
            dst = attnout[:, qch * T + qc * 512: qch * T + qc * 512 + 512]
            if h % 2 == 0:
                i = dve.tensor_copy(attnout[0:64, qch * T + qc * 512: qch * T + qc * 512 + 512],
                                    scr[0][0:64, 0:512])
            else:
                i = dve.tensor_copy(attnout[64:128, qch * T + qc * 512: qch * T + qc * 512 + 512],
                                    scr[0][0:64, 0:512])
            t_ost[h * 4 + qc] = k.inc("dve", i)
            t_stread["qscr"] = k.tok("pe")
            t_kscr_read = k.tok("pe")

    t_attn_done = (k.tok("dve"), k.tok("act"))
    if dbg is not None:
        k.wait("pool", *t_attn_done)
        i = pool.dma_start(out=dbg["ao"], in_=attnout[:])
        k.inc("pool", i, 16)
    k.wait("sp", k.tok("pe"))
    i = sp.dma_start(out=wbuf[:], in_=W["woT_c"].rearrange("(c p) m -> p c m", p=128))
    t_wo = k.inc("sp", i, 16)
    pj = 0
    t_woev = {}
    for fc in range(8):
        for tsub in range(4):
            pcol = (pj % 2) * 512
            prev = t_woev.get(pj - 2)
            if prev is not None:
                k.wait("pe", prev)
            k.wait("pe", t_wo, *t_attn_done)
            for kc in range(NCH):
                inst = pe.matmul(ps_b[:, pcol:pcol + 512],
                                 wbuf[:, kc * 1024 + fc * 128: kc * 1024 + (fc + 1) * 128],
                                 attnout[:, kc * T + tsub * 512: kc * T + tsub * 512 + 512],
                                 start=(kc == 0), stop=(kc == NCH - 1))
            t_mm = k.inc("pe", inst)
            k.wait("dve", t_mm)
            i = dve.tensor_add(x_fm[:, fc * T + tsub * 512: fc * T + tsub * 512 + 512],
                               x_fm[:, fc * T + tsub * 512: fc * T + tsub * 512 + 512],
                               ps_b[:, pcol:pcol + 512])
            t_woev[pj] = k.inc("dve", i)
            pj += 1
    return (k.tok("dve"),)


def declare_l0attn_weights(nc, pfx=""):
    W = {}
    W["wqT"] = nc.dram_tensor(pfx + "wqT", [D, D], F32, kind="ExternalInput").ap()
    W["wkvT"] = nc.dram_tensor(pfx + "wkvT", [D, 512], F32, kind="ExternalInput").ap()
    W["woT"] = nc.dram_tensor(pfx + "woT", [D, D], F32, kind="ExternalInput").ap()
    W["wqT_c"] = nc.dram_tensor(pfx + "wqT_c", [D, D], BF16, kind="Internal").ap()
    W["wkvT_c"] = nc.dram_tensor(pfx + "wkvT_c", [D, 512], BF16, kind="Internal").ap()
    W["woT_c"] = nc.dram_tensor(pfx + "woT_c", [D, D], BF16, kind="Internal").ap()
    return W


def l0attn_in_map(inputs, shard):
    return {
        "x_in": np.ascontiguousarray(shard),
        "wqT": np.ascontiguousarray(inputs["L0_attn_Wq"].T),
        "wkvT": np.ascontiguousarray(inputs["L0_attn_Wkv"].T),
        "woT": np.ascontiguousarray(inputs["L0_attn_Wo"].T),
    }


def build_l0attn_program(debug=False):
    nc = bass.Bass("TRN2", target_bir_lowering=False)
    x_in = nc.dram_tensor("x_in", [T, D], F32, kind="ExternalInput").ap()
    x_out = nc.dram_tensor("x_out", [T, D], F32, kind="ExternalOutput").ap()
    W = declare_l0attn_weights(nc)
    W.update(make_l0attn_consts(nc))
    k = Ctx(nc)
    B = build_common(nc, k)
    B["cast_b"] = [B["y_bf"][:, 0:2048], B["y_bf"][:, 2048:4096]]
    build_l0attn_bufs(nc, B)
    if debug:
        B["dbg"] = {
            "mcol": nc.dram_tensor("dbg_mcol", [128, 18], F32, kind="ExternalOutput").ap(),
            "qT": nc.dram_tensor("dbg_qT", [128, 8 * T], BF16, kind="ExternalOutput").ap(),
            "kT": nc.dram_tensor("dbg_kT", [128, 2 * T], BF16, kind="ExternalOutput").ap(),
            "vtok": nc.dram_tensor("dbg_vtok", [128, 4096], BF16, kind="ExternalOutput").ap(),
            "ao": nc.dram_tensor("dbg_ao", [128, 8 * T], BF16, kind="ExternalOutput").ap(),
        }
    t_cast = emit_cast_l0attn(k, B, W)
    t_load = emit_load_x(k, B, x_in)
    after = (t_load,) + tuple(t_cast)
    toks = emit_l0attn(k, B, W, after, dbg=B.get("dbg"))
    t_store = emit_store_x(k, B, x_out, toks)
    k.eng["sp"].wait_ge(k.sem["sp"], k.cnt["sp"])
    k.eng["pool"].wait_ge(k.sem["pool"], k.cnt["pool"])
    return nc


def make_l1attn_consts(nc):
    c = {}
    sel = np.zeros((32, 32 * 128), np.float32)
    for r in range(32):
        sel[r, r * 128:(r + 1) * 128] = 1.0
    c["sel32"] = nc.inline_tensor(_np_bf16(sel), "c_sel32")
    return c


def build_l1attn_bufs(nc, B):
    B["m_scr"] = [nc.alloc_sbuf_tensor(f"m_scr{i}", [128, T], F32) for i in range(5)]
    B["m_conv"] = nc.alloc_sbuf_tensor("m_conv", [128, T + 8], F32)
    B["ln2_R"] = B["m_scr"][0]
    B["m_bf"] = nc.alloc_sbuf_tensor("m_bf", [128, T], BF16)
    B["m_dtr"] = nc.alloc_sbuf_tensor("m_dtr", [128, T], BF16)
    B["m_BC"] = nc.alloc_sbuf_tensor("m_BC", [32, T], BF16)
    B["m_dtw"] = nc.alloc_sbuf_tensor("m_dtw", [128, T], BF16)
    B["m_xpw"] = nc.alloc_sbuf_tensor("m_xpw", [128, 16 * 160], BF16)
    B["m_wst"] = [nc.alloc_sbuf_tensor(f"m_wst{i}", [128, T], BF16) for i in range(2)]
    B["m_sel32r"] = nc.alloc_sbuf_tensor("m_sel32r", [32, 4096], BF16)
    B["m_ywin"] = nc.alloc_sbuf_tensor("m_ywin", [128, 16 * 128], BF16)
    B["m_cols"] = nc.alloc_sbuf_tensor("m_cols", [128, 256 + 64 + 16 * 4 + 16], F32)


def declare_l1attn_weights(nc, pfx=""):
    W = {}
    W["ln_gamma"] = nc.dram_tensor(pfx + "ln_gamma", [D], F32, kind="ExternalInput").ap()
    W["ln_beta"] = nc.dram_tensor(pfx + "ln_beta", [D], F32, kind="ExternalInput").ap()
    W["in_proj_w"] = nc.dram_tensor(pfx + "in_proj_w", [D, 4096], F32, kind="ExternalInput").ap()
    W["conv_w"] = nc.dram_tensor(pfx + "conv_w", [2048, 4], F32, kind="ExternalInput").ap()
    W["conv_b"] = nc.dram_tensor(pfx + "conv_b", [2048], F32, kind="ExternalInput").ap()
    W["x_proj_w"] = nc.dram_tensor(pfx + "x_proj_w", [2048, 160], F32, kind="ExternalInput").ap()
    W["dt_proj_w"] = nc.dram_tensor(pfx + "dt_proj_w", [128, 2048], F32, kind="ExternalInput").ap()
    W["dt_proj_b"] = nc.dram_tensor(pfx + "dt_proj_b", [2048], F32, kind="ExternalInput").ap()
    W["A_log"] = nc.dram_tensor(pfx + "A_log", [2048, 16], F32, kind="ExternalInput").ap()
    W["D_param"] = nc.dram_tensor(pfx + "D_param", [2048], F32, kind="ExternalInput").ap()
    W["out_proj_w"] = nc.dram_tensor(pfx + "out_proj_w", [2048, D], F32, kind="ExternalInput").ap()
    W["in_cache"] = nc.dram_tensor(pfx + "in_cache", [32, D, 128], BF16, kind="Internal").ap()
    W["xp_cache"] = nc.dram_tensor(pfx + "xp_cache", [2048, 160], BF16, kind="Internal").ap()
    W["dt_cache"] = nc.dram_tensor(pfx + "dt_cache", [128, 2048], BF16, kind="Internal").ap()
    W["out_cache"] = nc.dram_tensor(pfx + "out_cache", [8, 2048, 128], BF16, kind="Internal").ap()
    W["xb_dram"] = nc.dram_tensor(pfx + "xb_dram", [16, 128, T], BF16, kind="Internal").ap()
    W["yb_dram"] = nc.dram_tensor(pfx + "yb_dram", [16, 128, T], BF16, kind="Internal").ap()
    return W


def emit_cast_l1attn(k, B, W):
    pairs = []
    for kc in range(8):
        for half in range(2):
            pairs.append((W["in_proj_w"][kc * 128:(kc + 1) * 128, half * 2048:(half + 1) * 2048],
                          W["in_cache"][half * 16:(half + 1) * 16, kc * 128:(kc + 1) * 128, :]
                          .transpose([1, 0, 2])))
    for kc in range(16):
        pairs.append((W["x_proj_w"][kc * 128:(kc + 1) * 128, :],
                      W["xp_cache"][kc * 128:(kc + 1) * 128, :]))
    pairs.append((W["dt_proj_w"], W["dt_cache"]))
    for kc in range(16):
        pairs.append((W["out_proj_w"][kc * 128:(kc + 1) * 128, :],
                      W["out_cache"][:, kc * 128:(kc + 1) * 128, :].transpose([1, 0, 2])))
    return emit_cast_pairs(k, B, pairs)


def emit_l1attn(k, B, W, after):
    nc = k.nc
    pe = k.eng["pe"]
    act, dve, pool, sp = k.eng["act"], k.eng["dve"], k.eng["pool"], k.eng["sp"]
    cs = B["cs"]
    x_fm, y_bf = B["x_fm"], B["y_bf"]
    ps_a, ps_b, ps_up, ps_dn = B["ps_a"], B["ps_b"], B["ps_up"], B["ps_dn"]
    scr = B["m_scr"]
    cols = B["m_cols"]
    mbf, dtr, BC = B["m_bf"], B["m_dtr"], B["m_BC"]
    wst, selr = B["m_wst"], B["m_sel32r"]

    i = sp.dma_start(out=cols[:, 0:256], in_=W["A_log"].rearrange("(c p) n -> p c n", p=128),
                     allow_slow_non_contiguous=True)
    t_A = k.inc("sp", i, 16)
    i = sp.dma_start(out=cols[:, 256:320], in_=W["conv_w"].rearrange("(c p) n -> p c n", p=128),
                     allow_slow_non_contiguous=True)
    k.inc("sp", i, 16)
    i = sp.dma_start(out=cols[:, 320:336], in_=W["conv_b"].rearrange("(c p) -> p c", p=128),
                     allow_slow_non_contiguous=True)
    k.inc("sp", i, 16)
    i = sp.dma_start(out=cols[:, 336:352], in_=W["D_param"].rearrange("(c p) -> p c", p=128),
                     allow_slow_non_contiguous=True)
    k.inc("sp", i, 16)
    i = sp.dma_start(out=cols[:, 352:368], in_=W["dt_proj_b"].rearrange("(c p) -> p c", p=128),
                     allow_slow_non_contiguous=True)
    k.inc("sp", i, 16)
    i = sp.dma_start(out=cols[:, 368:376], in_=W["ln_gamma"].rearrange("(c p) -> p c", p=128),
                     allow_slow_non_contiguous=True)
    k.inc("sp", i, 16)
    i = sp.dma_start(out=cols[:, 376:384], in_=W["ln_beta"].rearrange("(c p) -> p c", p=128),
                     allow_slow_non_contiguous=True)
    t_cols = k.inc("sp", i, 16)
    k.wait("sp", W.get("bg_m1"))
    i = sp.dma_start(out=B["m_dtw"][:], in_=W["dt_cache"])
    t_dtw = k.inc("sp", i, 16)
    i = sp.dma_start(out=B["m_xpw"][:], in_=W["xp_cache"].rearrange("(c p) m -> p c m", p=128))
    t_xpw = k.inc("sp", i, 16)

    k.wait("act", t_cols)
    i = act.activation(cols[:, 0:256], cols[:, 0:256], AF.Exp)
    t_an = k.inc("act", i)
    k.wait("dve", t_an)
    i = dve.tensor_scalar(cols[:, 0:256], cols[:, 0:256], -1.0, None, ALU.mult)
    t_anneg = k.inc("dve", i)

    m_b, rstd_b, ln_toks = emit_ln(k, B, after, affine=(cols[:, 368:376], cols[:, 376:384]),
                                   fold_ln2=True)

    conv = B["m_conv"]
    i = pool.memset(conv[:, 0:8], 0.0)
    t_ms0 = k.inc("pool", i)
    t_xb_dma = {}
    for c in range(16):
        bi = c % 2
        prev = t_xb_dma.get(c - 2)
        if prev is not None:
            k.wait("sp", prev)
        i = sp.dma_start(out=wst[bi][:, 0:1024],
                         in_=W["in_cache"][c].rearrange("(q p) m -> p q m", p=128))
        t_w = k.inc("sp", i, 16)
        k.wait("pe", t_w, *ln_toks, t_anneg)
        if c >= 1:
            k.wait("pe", t_cv)
        for ts4 in range(4):
            tile = (ps_a if ts4 < 2 else ps_b)[:, (ts4 % 2) * 512:(ts4 % 2) * 512 + 512]
            for kc in range(NCH):
                inst = pe.matmul(tile, wst[bi][:, kc * 128:(kc + 1) * 128],
                                 y_bf[:, kc * T + ts4 * 512: kc * T + ts4 * 512 + 512],
                                 start=(kc == 0), stop=(kc == NCH - 1))
        t_mm = k.inc("pe", inst)
        k.wait("dve", t_mm, t_ms0)
        for ts4 in range(4):
            tile = (ps_a if ts4 < 2 else ps_b)[:, (ts4 % 2) * 512:(ts4 % 2) * 512 + 512]
            i = dve.tensor_copy(conv[:, 4 + ts4 * 512: 4 + (ts4 + 1) * 512], tile)
        t_cv = k.inc("dve", i)
        k.wait("dve", t_cols)
        i = dve.tensor_scalar(scr[0][:], conv[:, 4:4 + T], cols[:, 256 + c * 4 + 3:256 + c * 4 + 4],
                              None, ALU.mult)
        k.inc("dve", i)
        i = dve.tensor_scalar(scr[1][:], conv[:, 3:3 + T], cols[:, 256 + c * 4 + 2:256 + c * 4 + 3],
                              None, ALU.mult)
        k.inc("dve", i)
        i = dve.tensor_add(scr[0][:], scr[0][:], scr[1][:])
        k.inc("dve", i)
        i = dve.tensor_scalar(scr[1][:], conv[:, 2:2 + T], cols[:, 256 + c * 4 + 1:256 + c * 4 + 2],
                              None, ALU.mult)
        k.inc("dve", i)
        i = dve.tensor_add(scr[0][:], scr[0][:], scr[1][:])
        k.inc("dve", i)
        i = dve.tensor_scalar(scr[1][:], conv[:, 1:1 + T], cols[:, 256 + c * 4:256 + c * 4 + 1],
                              None, ALU.mult)
        k.inc("dve", i)
        i = dve.tensor_add(scr[0][:], scr[0][:], scr[1][:])
        t_acc = k.inc("dve", i)
        k.wait("act", t_acc)
        i = act.activation(mbf[:], scr[0][:], AF.Silu, bias=cols[:, 320 + c:321 + c])
        t_xb = k.inc("act", i)
        k.wait("sp", t_xb)
        i = sp.dma_start(out=W["xb_dram"][c], in_=mbf[:])
        t_xb_dma[c] = k.inc("sp", i, 16)

    t_rhs = {}
    rj = 0
    for pass_i, (m0, mw) in enumerate([(0, 128), (128, 32)]):
        for ts4 in range(4):
            for c in range(16):
                bi = c % 2
                prev = t_rhs.get(rj - 2)
                if prev is not None:
                    k.wait("sp", prev)
                i = sp.dma_start(out=mbf[:, bi * 512:(bi + 1) * 512],
                                 in_=W["xb_dram"][c][:, ts4 * 512:(ts4 + 1) * 512])
                t_l = k.inc("sp", i, 16)
                k.wait("pe", t_l, t_xpw, t_xb_dma.get(15))
                if pass_i == 0:
                    tile = (ps_a if ts4 < 2 else ps_b)[:, (ts4 % 2) * 512:(ts4 % 2) * 512 + 512]
                else:
                    tile = (ps_a if ts4 < 2 else ps_b)[0:32, (ts4 % 2) * 512:(ts4 % 2) * 512 + 512]
                inst = pe.matmul(tile, B["m_xpw"][:, c * 160 + m0: c * 160 + m0 + mw],
                                 mbf[:, bi * 512:(bi + 1) * 512],
                                 start=(c == 0), stop=(c == 15))
                t_rhs[rj] = k.inc("pe", inst)
                rj += 1
        t_xpj = k.tok("pe")
        for ts4 in range(4):
            if pass_i == 0:
                tile = (ps_a if ts4 < 2 else ps_b)[:, (ts4 % 2) * 512:(ts4 % 2) * 512 + 512]
                en, eng = ("act", act), None
                k.wait("act", t_xpj)
                i = act.activation(dtr[:, ts4 * 512:(ts4 + 1) * 512], tile, AF.Copy)
                t_ev = k.inc("act", i)
            else:
                tile = (ps_a if ts4 < 2 else ps_b)[0:32, (ts4 % 2) * 512:(ts4 % 2) * 512 + 512]
                k.wait("act", t_xpj)
                i = act.activation(BC[:, ts4 * 512:(ts4 + 1) * 512], tile, AF.Copy)
                t_ev = k.inc("act", i)
        k.wait("pe", t_ev)

    slotA, slotB, slotC = scr[0], scr[3], None
    t_xpj_all = k.tok("pe")
    k.wait("sp", t_xpj_all)
    i = sp.dma_start(out=selr[:], in_=W["sel32"].ap())
    t_selr = k.inc("sp", i, 16)
    t_u = None
    t_yb_dma = {}
    for c in range(16):
        k.wait("pe", t_dtw, k.tok("dve"), k.tok("act"), k.tok("pool"))
        for ts4 in range(4):
            tile = (ps_a if ts4 < 2 else ps_b)[:, (ts4 % 2) * 512:(ts4 % 2) * 512 + 512]
            inst = pe.matmul(tile, B["m_dtw"][:, c * 128:(c + 1) * 128],
                             dtr[:, ts4 * 512:(ts4 + 1) * 512],
                             start=True, stop=True)
        t_dt = k.inc("pe", inst)
        dtb = cols[:, 352 + c:353 + c]
        for ts4 in range(4):
            tile = (ps_a if ts4 < 2 else ps_b)[:, (ts4 % 2) * 512:(ts4 % 2) * 512 + 512]
            w0 = ts4 * 512
            k.wait("act", t_dt)
            i = act.activation(slotA[:, w0:w0 + 512], tile, AF.Abs, bias=dtb)
            k.inc("act", i)
            i = act.activation(slotA[:, w0:w0 + 512], slotA[:, w0:w0 + 512], AF.Exp, scale=-1.0)
            k.inc("act", i)
            i = act.activation(slotA[:, w0:w0 + 512], slotA[:, w0:w0 + 512], AF.Ln,
                               bias=cs["epscols"][:, 2:3])
            k.inc("act", i)
            i = act.activation(slotB[:, w0:w0 + 512], tile, AF.Relu, bias=dtb)
            t_sp2 = k.inc("act", i)
            k.wait("dve", t_sp2)
            i = dve.tensor_add(scr[1][:, w0:w0 + 512], slotA[:, w0:w0 + 512],
                               slotB[:, w0:w0 + 512])
            t_delta = k.inc("dve", i)
        k.wait("sp", t_xpj_all, t_u)
        i = sp.dma_start(out=mbf[:], in_=W["xb_dram"][c])
        t_xbl = k.inc("sp", i, 16)
        k.wait("dve", t_xbl, t_delta)
        i = dve.tensor_mul(scr[2][:], scr[1][:], mbf[:])
        k.inc("dve", i)
        i = dve.tensor_scalar(scr[4][:], mbf[:], cols[:, 336 + c:337 + c], None, ALU.mult)
        t_u = k.inc("dve", i)
        t_dbx = None
        t_cev = None
        t_ymul = None
        t_yacc = None
        t_scan_prev = None
        for n in range(16):
            k.wait("pe", t_selr, t_u, t_dbx, t_cev)
            for w4 in range(4):
                tile = (ps_up if w4 < 2 else ps_dn)[:, (w4 % 2) * 512:(w4 % 2) * 512 + 512]
                inst = pe.matmul(tile, selr[:, n * 128:(n + 1) * 128],
                                 BC[:, w4 * 512:(w4 + 1) * 512],
                                 start=True, stop=True, skip_group_check=True)
            t_bb = k.inc("pe", inst)
            for w4 in range(4):
                tile = (ps_a if w4 < 2 else ps_b)[:, (w4 % 2) * 512:(w4 % 2) * 512 + 512]
                inst = pe.matmul(tile, selr[:, (16 + n) * 128:(17 + n) * 128],
                                 BC[:, w4 * 512:(w4 + 1) * 512],
                                 start=True, stop=True, skip_group_check=True)
            t_cb = k.inc("pe", inst)
            k.wait("act", t_scan_prev, t_delta)
            i = act.activation(slotA[:], scr[1][:], AF.Exp, scale=cols[:, c * 16 + n:c * 16 + n + 1])
            t_da = k.inc("act", i)
            k.wait("dve", t_bb, t_yacc)
            i = dve.tensor_mul(slotB[:, 0:1024], scr[2][:, 0:1024], ps_up[:])
            k.inc("dve", i)
            i = dve.tensor_mul(slotB[:, 1024:2048], scr[2][:, 1024:2048], ps_dn[:])
            t_dbx = k.inc("dve", i)
            slotC = conv
            k.wait("dve", t_da)
            for w4 in range(4):
                w0 = w4 * 512
                init = 0.0 if w4 == 0 else slotC[:, w0 - 1:w0]
                i = dve.tensor_tensor_scan(slotC[:, w0:w0 + 512], slotA[:, w0:w0 + 512],
                                           slotB[:, w0:w0 + 512], init, ALU.mult, ALU.add)
                t_scan = k.inc("dve", i)
            t_scan_prev = t_scan
            k.wait("act", t_cb, t_yacc, t_u)
            i = act.activation(mbf[:, 0:1024], ps_a[:], AF.Copy)
            k.inc("act", i)
            i = act.activation(mbf[:, 1024:2048], ps_b[:], AF.Copy)
            t_cev = k.inc("act", i)
            k.wait("pool", t_cev, t_scan)
            i = pool.tensor_mul(slotB[:], slotC[:, 0:T], mbf[:])
            k.inc("pool", i)
            i = pool.tensor_add(scr[4][:], scr[4][:], slotB[:])
            t_ymul = t_yacc = k.inc("pool", i)
        bi = c % 2
        i = sp.dma_start(out=wst[bi][:, 0:1024],
                         in_=W["in_cache"][16 + c].rearrange("(q p) m -> p q m", p=128))
        t_wz = k.inc("sp", i, 16)
        k.wait("pe", t_wz, t_dbx, t_ymul, t_yacc)
        for ts4 in range(4):
            tile = (ps_up if ts4 < 2 else ps_dn)[:, (ts4 % 2) * 512:(ts4 % 2) * 512 + 512]
            for kc in range(NCH):
                inst = pe.matmul(tile, wst[bi][:, kc * 128:(kc + 1) * 128],
                                 y_bf[:, kc * T + ts4 * 512: kc * T + ts4 * 512 + 512],
                                 start=(kc == 0), stop=(kc == NCH - 1))
        t_z = k.inc("pe", inst)
        k.wait("act", t_z, t_ymul)
        i = act.activation(slotA[:, 0:1024], ps_up[:], AF.Silu)
        k.inc("act", i)
        i = act.activation(slotA[:, 1024:2048], ps_dn[:], AF.Silu)
        t_zs = k.inc("act", i)
        k.wait("dve", t_zs, t_yacc)
        i = dve.tensor_mul(mbf[:], scr[4][:], slotA[:])
        t_yf = k.inc("dve", i)
        k.wait("sp", t_yf)
        i = sp.dma_start(out=W["yb_dram"][c], in_=mbf[:])
        t_yb_dma[c] = k.inc("sp", i, 16)

    t_res = {}
    for ch in range(NCH):
        i = dve.tensor_sub(B["xio"][:], x_fm[:, ch * T:(ch + 1) * T], m_b[:])
        k.inc("dve", i)
        i = dve.tensor_mul(B["xio"][:], B["xio"][:], rstd_b[:])
        k.inc("dve", i)
        i = dve.tensor_add(x_fm[:, ch * T:(ch + 1) * T], x_fm[:, ch * T:(ch + 1) * T],
                           B["xio"][:])
        t_res[ch] = k.inc("dve", i)

    ywin = B["m_ywin"]
    t_oc_ev = {}
    for ts8 in range(16):
        w0 = ts8 * 128
        k.wait("sp", t_yb_dma[15], t_oc_ev.get(ts8 - 1))
        for c4 in range(4):
            i = sp.dma_start(out=ywin[:, c4 * 512:(c4 + 1) * 512]
                             .rearrange("p (c m) -> p c m", c=4),
                             in_=W["yb_dram"][c4 * 4:(c4 + 1) * 4, :, w0:w0 + 128]
                             .transpose([1, 0, 2]))
            t_yw = k.inc("sp", i, 16)
        for oc in range(8):
            bi = oc % 2
            prev = t_oc_ev.get((ts8, oc - 2))
            if prev is not None:
                k.wait("sp", prev)
            i = sp.dma_start(out=wst[bi][:],
                             in_=W["out_cache"][oc].rearrange("(c p) m -> p c m", p=128))
            t_wo = k.inc("sp", i, 16)
            psc = ps_a[:, (oc % 2) * 512:(oc % 2) * 512 + 128]
            prev = t_oc_ev.get((ts8, oc - 2))
            if prev is not None:
                k.wait("pe", prev)
            k.wait("pe", t_wo, t_yw)
            for c in range(16):
                inst = pe.matmul(psc, wst[bi][:, c * 128:(c + 1) * 128],
                                 ywin[:, c * 128:(c + 1) * 128],
                                 start=(c == 0), stop=(c == 15))
            t_mm = k.inc("pe", inst)
            k.wait("dve", t_mm, t_res.get(oc))
            i = dve.tensor_add(x_fm[:, oc * T + w0: oc * T + w0 + 128],
                               x_fm[:, oc * T + w0: oc * T + w0 + 128], psc)
            t_oc_ev[(ts8, oc)] = k.inc("dve", i)
        t_oc_ev[ts8] = k.tok("dve")
    return (k.tok("dve"),)


def l1attn_in_map(inputs, shard):
    return {
        "x_in": np.ascontiguousarray(shard),
        "ln_gamma": inputs["L1_attn_ln_gamma"],
        "ln_beta": inputs["L1_attn_ln_beta"],
        "in_proj_w": inputs["L1_attn_in_proj_w"],
        "conv_w": inputs["L1_attn_conv_w"],
        "conv_b": inputs["L1_attn_conv_b"],
        "x_proj_w": inputs["L1_attn_x_proj_w"],
        "dt_proj_w": inputs["L1_attn_dt_proj_w"],
        "dt_proj_b": inputs["L1_attn_dt_proj_b"],
        "A_log": inputs["L1_attn_A_log"],
        "D_param": inputs["L1_attn_D_param"],
        "out_proj_w": inputs["L1_attn_out_proj_w"],
    }


def build_l1attn_program(debug=False):
    nc = bass.Bass("TRN2", target_bir_lowering=False)
    x_in = nc.dram_tensor("x_in", [T, D], F32, kind="ExternalInput").ap()
    x_out = nc.dram_tensor("x_out", [T, D], F32, kind="ExternalOutput").ap()
    W = declare_l1attn_weights(nc)
    W.update(make_l1attn_consts(nc))
    k = Ctx(nc)
    B = build_common(nc, k)
    B["cast_b"] = [B["y_bf"][:, 0:2048], B["y_bf"][:, 2048:4096]]
    build_l1attn_bufs(nc, B)
    t_cast = emit_cast_l1attn(k, B, W)
    t_load = emit_load_x(k, B, x_in)
    after = (t_load,) + tuple(t_cast)
    toks = emit_l1attn(k, B, W, after)
    t_store = emit_store_x(k, B, x_out, toks)
    k.eng["sp"].wait_ge(k.sem["sp"], k.cnt["sp"])
    return nc


def make_l0ffn_consts(nc):
    c = {}
    t = np.arange(T, dtype=np.float64)
    dh = 64
    inv = 1.0 / (10000.0 ** (np.arange(0, dh, 2, dtype=np.float64) / dh))
    ang = t[None, :] * inv[:, None]
    cosr = np.zeros((128, T), np.float64)
    sinr = np.zeros((128, T), np.float64)
    for p in range(128):
        pair = (p % 64) // 2
        cosr[p] = np.cos(ang[pair])
        sinr[p] = (-1.0 if p % 2 == 0 else 1.0) * np.sin(ang[pair])
    c["ropecos"] = nc.inline_tensor(_np_bf16(cosr), "c_ropecos")
    c["ropesin"] = nc.inline_tensor(_np_bf16(sinr), "c_ropesin")
    aug = np.zeros((68, 128), np.float32)
    aug[0] = 1.0
    aug[64] = 1.0
    c["kaug1"] = nc.inline_tensor(_np_bf16(aug), "c_kaug1")
    sel8b = np.zeros((8, 8 * 128), np.float32)
    for e in range(8):
        sel8b[e, e * 128:(e + 1) * 128] = 1.0
    c["sel8bf"] = nc.inline_tensor(_np_bf16(sel8b), "c_sel8bf")
    ig = np.ones((128, 4), np.float32)
    c["ig4"] = nc.inline_tensor(ig, "c_ig4")
    return c


def build_l0ffn_bufs(nc, B):
    B["o_big"] = nc.alloc_sbuf_tensor("o_big", [128, NCH * T], BF16)
    B["o_kc"] = nc.alloc_sbuf_tensor("o_kc", [128, 2 * T], BF16)
    B["o_vc"] = nc.alloc_sbuf_tensor("o_vc", [128, 2 * T], BF16)
    B["o_q"] = nc.alloc_sbuf_tensor("o_q", [128, T], BF16)
    B["o_k"] = nc.alloc_sbuf_tensor("o_k", [128, T], BF16)
    B["o_qsw"] = nc.alloc_sbuf_tensor("o_qsw", [128, 1024], BF16)
    B["o_qtmp"] = nc.alloc_sbuf_tensor("o_qtmp", [128, 1024], BF16)
    B["o_vtok"] = nc.alloc_sbuf_tensor("o_vtok", [128, 16 * 64], BF16)
    B["o_cos"] = nc.alloc_sbuf_tensor("o_cos", [128, T], BF16)
    B["o_sin"] = nc.alloc_sbuf_tensor("o_sin", [128, T], BF16)
    B["o_stexp"] = nc.alloc_sbuf_tensor("o_stexp", [128, 1024], BF16)
    B["o_wst"] = [nc.alloc_sbuf_tensor(f"o_wst{i}", [128, T], BF16) for i in range(2)]
    B["o_blockones"] = nc.alloc_sbuf_tensor("o_blockones", [128, 128], F32)
    B["o_onesbf"] = nc.alloc_sbuf_tensor("o_onesbf", [128, 128], BF16)
    B["o_mcol"] = nc.alloc_sbuf_tensor("o_mcol", [128, 48], F32)
    B["o_gw"] = nc.alloc_sbuf_tensor("o_gw", [128, 8 * 10], F32)
    B["o_gsc"] = nc.alloc_sbuf_tensor("o_gsc", [128, 64], F32)
    B["o_w8"] = nc.alloc_sbuf_tensor("o_w8", [128, 16 * 8], F32)
    B["o_n12"] = nc.alloc_sbuf_tensor("o_n12", [128, 16], F32)
    B["o_ig4"] = nc.alloc_sbuf_tensor("o_ig4", [128, 4], F32)
    B["o_wvhead"] = nc.alloc_sbuf_tensor("o_wvhead", [128, 128], BF16)
    B["o_mbc2"] = nc.alloc_sbuf_tensor("o_mbc2", [128, 512], F32)


def declare_l0ffn_weights(nc, pfx=""):
    W = {}
    names = [("Wq", [D, D]), ("Wk_c", [D, 256]), ("Wv_c", [D, 256]),
             ("Wk", [256, D]), ("Wv", [256, D]), ("Wo", [D, D]),
             ("W_shared_in", [D, 4096]), ("W_shared_out", [2048, D]),
             ("W1_shared", [D, 4096]), ("W2_experts", [8, 2048, D]),
             ("Wg_gate", [D, 2]), ("We_gate", [D, 8]),
             ("group_bias", [2]), ("expert_bias", [8]),
             ("norm1_w", [D]), ("norm2_w", [D])]
    for n, s in names:
        W[n] = nc.dram_tensor(pfx + n, s, F32, kind="ExternalInput").ap()
    W["wq_c"] = nc.dram_tensor(pfx + "wq_c", [D, D], BF16, kind="Internal").ap()
    W["wkc_c"] = nc.dram_tensor(pfx + "wkc_c", [D, 256], BF16, kind="Internal").ap()
    W["wvc_c"] = nc.dram_tensor(pfx + "wvc_c", [D, 256], BF16, kind="Internal").ap()
    W["wk_c2"] = nc.dram_tensor(pfx + "wk_c2", [256, D], BF16, kind="Internal").ap()
    W["wv_c2"] = nc.dram_tensor(pfx + "wv_c2", [256, D], BF16, kind="Internal").ap()
    W["wo_c"] = nc.dram_tensor(pfx + "wo_c", [D, D], BF16, kind="Internal").ap()
    W["wsi_c"] = nc.dram_tensor(pfx + "wsi_c", [D, 4096], BF16, kind="Internal").ap()
    W["wso_c"] = nc.dram_tensor(pfx + "wso_c", [2048, D], BF16, kind="Internal").ap()
    W["w1s_c"] = nc.dram_tensor(pfx + "w1s_c", [D, 4096], BF16, kind="Internal").ap()
    W["w2e_c"] = nc.dram_tensor(pfx + "w2e_c", [8, 2048, D], BF16, kind="Internal").ap()
    W["x_spill"] = nc.dram_tensor(pfx + "x_spill", [128, NCH * T], F32, kind="Internal").ap()
    W["mask_scratch2"] = nc.dram_tensor(pfx + "mask_scratch2", [128, 128], F32, kind="Internal").ap()
    return W


def emit_cast_l0ffn(k, B, W):
    pairs = []
    for kc in range(8):
        r = slice(kc * 128, (kc + 1) * 128)
        pairs.append((W["Wq"][r, :], W["wq_c"][r, :], 0.125))
        pairs.append((W["Wk_c"][r, :], W["wkc_c"][r, :]))
        pairs.append((W["Wv_c"][r, :], W["wvc_c"][r, :]))
        pairs.append((W["Wo"][r, :], W["wo_c"][r, :]))
        for half in range(2):
            cc = slice(half * 2048, (half + 1) * 2048)
            pairs.append((W["W_shared_in"][r, cc], W["wsi_c"][r, cc]))
            pairs.append((W["W1_shared"][r, cc], W["w1s_c"][r, cc]))
    for kc in range(2):
        r = slice(kc * 128, (kc + 1) * 128)
        pairs.append((W["Wk"][r, :], W["wk_c2"][r, :]))
        pairs.append((W["Wv"][r, :], W["wv_c2"][r, :]))
    for kc in range(16):
        r = slice(kc * 128, (kc + 1) * 128)
        pairs.append((W["W_shared_out"][r, :], W["wso_c"][r, :]))
    for e in range(8):
        for kc in range(16):
            r = slice(kc * 128, (kc + 1) * 128)
            pairs.append((W["W2_experts"][e, r, :], W["w2e_c"][e, r, :]))
    return emit_cast_pairs(k, B, pairs)


def emit_l0ffn(k, B, W, after, dbg=None):
    nc = k.nc
    pe = k.eng["pe"]
    act, dve, pool, sp = k.eng["act"], k.eng["dve"], k.eng["pool"], k.eng["sp"]
    cs = B["cs"]
    x_fm, y_bf, scr = B["x_fm"], B["y_bf"], B["scr"]
    ps_a, ps_b, ps_up, ps_dn = B["ps_a"], B["ps_b"], B["ps_up"], B["ps_dn"]
    big = B["o_big"]
    wst = B["o_wst"]
    mcol = B["o_mcol"]

    k.wait("sp", *after)
    t_spill = {}
    for ch in range(NCH):
        i = sp.dma_start(out=W["x_spill"][:, ch * T:(ch + 1) * T],
                         in_=x_fm[:, ch * T:(ch + 1) * T])
        t_spill[ch] = k.inc("sp", i, 16)

    m_b, rstd_b, ln_toks = emit_ln(k, B, after)

    i = sp.dma_start(out=B["o_cos"][:], in_=W["ropecos"].ap())
    k.inc("sp", i, 16)
    i = sp.dma_start(out=B["o_sin"][:], in_=W["ropesin"].ap())
    k.inc("sp", i, 16)
    i = sp.dma_start(out=B["o_blockones"][:], in_=B["c_inline"]["blockones_sh"].ap())
    k.inc("sp", i, 16)
    i = sp.dma_start(out=B["o_onesbf"][:], in_=B["c_inline"]["ones_bf_sh"].ap())
    k.inc("sp", i, 16)
    i = sp.dma_start(out=B["o_ig4"][:], in_=W["ig4"].ap())
    k.inc("sp", i, 16)
    i = sp.dma_start(out=B["o_gw"][:].rearrange("p (c n) -> p c n", n=10)[:, :, 0:2],
                     in_=W["Wg_gate"].rearrange("(c p) n -> p c n", p=128),
                     allow_slow_non_contiguous=True)
    k.inc("sp", i, 16)
    i = sp.dma_start(out=B["o_gw"][:].rearrange("p (c n) -> p c n", n=10)[:, :, 2:10],
                     in_=W["We_gate"].rearrange("(c p) n -> p c n", p=128),
                     allow_slow_non_contiguous=True)
    k.inc("sp", i, 16)
    i = sp.dma_start(out=B["o_n12"][:, 0:8], in_=W["norm1_w"].rearrange("(c p) -> p c", p=128),
                     allow_slow_non_contiguous=True)
    k.inc("sp", i, 16)
    i = sp.dma_start(out=B["o_n12"][:, 8:16], in_=W["norm2_w"].rearrange("(c p) -> p c", p=128),
                     allow_slow_non_contiguous=True)
    k.inc("sp", i, 16)
    i = sp.dma_start(out=B["o_gsc"][:, 0:2], in_=W["group_bias"][None, :].partition_broadcast(128))
    k.inc("sp", i, 16)
    i = sp.dma_start(out=B["o_gsc"][:, 2:10], in_=W["expert_bias"][None, :].partition_broadcast(128))
    t_smalls = k.inc("sp", i, 16)

    t_ev = {}
    pj = 0
    li = 0
    t_wmm = {}
    k.wait("sp", W.get("bg_kcvc"))
    for (wap, dst) in ((W["wkc_c"], B["o_kc"]), (W["wvc_c"], B["o_vc"])):
        for fc in range(2):
            bi = li % 2
            prev = t_wmm.get(li - 2)
            if prev is not None:
                k.wait("sp", prev)
            i = sp.dma_start(out=wst[bi][:, 0:1024],
                             in_=wap[:, fc * 128:(fc + 1) * 128]
                             .rearrange("(c p) m -> p c m", p=128))
            t_w = k.inc("sp", i, 16)
            for tsub in range(4):
                pcol = (pj % 2) * 512
                prev = t_ev.get(pj - 2)
                if prev is not None:
                    k.wait("pe", prev)
                k.wait("pe", t_w, *ln_toks)
                for kc in range(NCH):
                    inst = pe.matmul(ps_b[:, pcol:pcol + 512],
                                     wst[bi][:, kc * 128:(kc + 1) * 128],
                                     y_bf[:, kc * T + tsub * 512: kc * T + tsub * 512 + 512],
                                     start=(kc == 0), stop=(kc == NCH - 1))
                t_mm = k.inc("pe", inst)
                en, eng = ("act", act) if pj % 2 == 0 else ("dve", dve)
                k.wait(en, t_mm)
                if en == "act":
                    i = act.activation(dst[:, fc * T + tsub * 512: fc * T + tsub * 512 + 512],
                                       ps_b[:, pcol:pcol + 512], AF.Copy)
                else:
                    i = dve.tensor_copy(dst[:, fc * T + tsub * 512: fc * T + tsub * 512 + 512],
                                        ps_b[:, pcol:pcol + 512])
                t_ev[pj] = k.inc(en, i)
                pj += 1
            t_wmm[li] = k.tok("pe")
            li += 1
    t_kcvc = (k.tok("act"), k.tok("dve"))

    attnout = big
    qch, kch, qsw, qtmp = B["o_q"], B["o_k"], B["o_qsw"], B["o_qtmp"]
    vtok = B["o_vtok"]
    stexp = B["o_stexp"]
    recip = B["xio"]
    t_attn_prev = None
    t_vread = None
    t_qkread = None
    st_war = {0: None, 1: None}
    t_ost = {}
    SQ = 0.125
    for hp in range(8):
        bi = hp % 2
        k.wait("sp", k.tok("pe"))
        i = sp.dma_start(out=wst[bi][:, 0:1024],
                         in_=W["wq_c"][:, hp * 128:(hp + 1) * 128]
                         .rearrange("(c p) m -> p c m", p=128))
        t_wq = k.inc("sp", i, 16)
        t_evq = {}
        for tsub in range(4):
            pcol = (tsub % 2) * 512
            prev = t_evq.get(tsub - 2)
            if prev is not None:
                k.wait("pe", prev)
            k.wait("pe", t_wq, *ln_toks, t_qkread)
            for kc in range(NCH):
                inst = pe.matmul(ps_a[:, pcol:pcol + 512],
                                 wst[bi][:, kc * 128:(kc + 1) * 128],
                                 y_bf[:, kc * T + tsub * 512: kc * T + tsub * 512 + 512],
                                 start=(kc == 0), stop=(kc == NCH - 1))
            t_mm = k.inc("pe", inst)
            k.wait("dve", t_mm)
            i = dve.tensor_copy(qch[:, tsub * 512:(tsub + 1) * 512], ps_a[:, pcol:pcol + 512])
            t_evq[tsub] = k.inc("dve", i)
        i = sp.dma_start(out=wst[bi][:, 1024:1280],
                         in_=W["wk_c2"][:, hp * 128:(hp + 1) * 128]
                         .rearrange("(c p) m -> p c m", p=128))
        t_wk = k.inc("sp", i, 16)
        for tsub in range(4):
            pcol = (tsub % 2) * 512
            prev = t_evq.get(4 + tsub - 2)
            if prev is not None:
                k.wait("pe", prev)
            k.wait("pe", t_wk, *t_kcvc)
            for c2 in range(2):
                inst = pe.matmul(ps_b[:, pcol:pcol + 512],
                                 wst[bi][:, 1024 + c2 * 128:1024 + (c2 + 1) * 128],
                                 B["o_kc"][:, c2 * T + tsub * 512: c2 * T + tsub * 512 + 512],
                                 start=(c2 == 0), stop=(c2 == 1))
            t_mm = k.inc("pe", inst)
            k.wait("dve", t_mm)
            i = dve.tensor_copy(kch[:, tsub * 512:(tsub + 1) * 512], ps_b[:, pcol:pcol + 512])
            t_evq[4 + tsub] = k.inc("dve", i)
        t_qk = k.tok("dve")
        mask = []
        for i2 in range(0, 32, 2):
            mask += [i2 + 1, i2]
        k.wait("dve", t_qk)
        for buf in (qch, kch):
            for hf in range(2):
                sl = slice(hf * 1024, (hf + 1) * 1024)
                i = dve.stream_shuffle(qsw[:], buf[:, sl], mask)
                k.inc("dve", i)
                i = dve.tensor_mul(qtmp[:], buf[:, sl], B["o_cos"][:, sl])
                k.inc("dve", i)
                i = dve.tensor_mul(qsw[:], qsw[:], B["o_sin"][:, sl])
                k.inc("dve", i)
                i = dve.tensor_add(buf[:, sl], qtmp[:], qsw[:])
                t_rope = k.inc("dve", i)

        def pairnorm(srcbuf, outc):
            t_entry = k.tok("dve")
            k.wait("act", t_rope, k.tok("pe"))
            i = act.activation(B["xio"][:], srcbuf[:, 0:2048], AF.Square)
            t_s = k.inc("act", i)
            k.wait("pe", t_s, t_entry)
            for ts4 in range(4):
                tile = (ps_a if ts4 < 2 else ps_b)[:, (ts4 % 2) * 512:(ts4 % 2) * 512 + 512]
                inst = pe.matmul(tile, B["o_blockones"][:],
                                 B["xio"][:, ts4 * 512:(ts4 + 1) * 512],
                                 start=True, stop=True)
            t_n = k.inc("pe", inst)
            k.wait("dve", t_n)
            for ts4 in range(4):
                tile = (ps_a if ts4 < 2 else ps_b)[:, (ts4 % 2) * 512:(ts4 % 2) * 512 + 512]
                i = dve.tensor_reduce(mcol[:, outc + ts4:outc + ts4 + 1], tile, AX.X, ALU.max)
                k.inc("dve", i)
            i = dve.tensor_reduce(mcol[:, outc + 4:outc + 5], mcol[:, outc:outc + 4],
                                  AX.X, ALU.max)
            return k.inc("dve", i)
        t_nq = pairnorm(qch, 0)
        t_nk = pairnorm(kch, 8)
        k.wait("dve", t_nq, t_nk)
        i = dve.tensor_scalar(B["xio"][:, 0:128], cs["ones128"][:], mcol[:, 4:5], None, ALU.mult)
        k.inc("dve", i)
        i = dve.tensor_scalar(B["xio"][:, 128:256], cs["ones128"][:], mcol[:, 12:13], None, ALU.mult)
        t_rep = k.inc("dve", i)
        k.wait("pe", t_rep)
        inst = pe.transpose(ps_a[:, 0:128], B["xio"][:, 0:128], cs["ident_f32"][:])
        k.inc("pe", inst)
        inst = pe.transpose(ps_a[:, 512:640], B["xio"][:, 128:256], cs["ident_f32"][:])
        t_tr = k.inc("pe", inst)
        k.wait("dve", t_tr)
        i = dve.tensor_copy(B["xio"][:, 0:128], ps_a[:, 0:128])
        k.inc("dve", i)
        i = dve.tensor_copy(B["xio"][:, 128:256], ps_a[:, 512:640])
        k.inc("dve", i)
        i = dve.tensor_reduce(mcol[:, 16:17], B["xio"][:, 0:128], AX.X, ALU.max)
        k.inc("dve", i)
        i = dve.tensor_reduce(mcol[:, 17:18], B["xio"][:, 128:256], AX.X, ALU.max)
        k.inc("dve", i)
        i = dve.tensor_scalar(mcol[:, 18:19], mcol[:, 16:17], mcol[:, 17:18], None, ALU.mult)
        t_m2 = k.inc("dve", i)
        k.wait("act", t_m2)
        i = act.activation(mcol[:, 19:20], mcol[:, 18:19], AF.Sqrt)
        t_sq = k.inc("act", i)
        k.wait("dve", t_sq)
        i = dve.tensor_scalar(mcol[:, 20:21], mcol[:, 19:20], -1.25, None, ALU.mult)
        t_M = k.inc("dve", i)
        t_qa = t_M
        if dbg is not None and hp == 0:
            k.wait("pool", t_M)
            i = pool.dma_start(out=dbg["q0"], in_=qch[:])
            k.inc("pool", i, 16)
            i = pool.dma_start(out=dbg["k0"], in_=kch[:])
            k.inc("pool", i, 16)
            i = pool.dma_start(out=dbg["m0"], in_=mcol[:])
            k.inc("pool", i, 16)

        for hh in range(2):
            h = hp * 2 + hh
            base = hh * 64
            k.wait("sp", k.tok("pe"))
            i = sp.dma_start(out=B["o_wvhead"][:],
                             in_=W["wv_c2"][:, h * 64:(h + 1) * 64]
                             .rearrange("(c p) m -> p c m", p=128))
            t_wv = k.inc("sp", i, 16)
            t_vev = {}
            for tc in range(NTOK):
                pcol = (tc % 2) * 512
                prev = t_vev.get(tc - 2)
                if prev is not None:
                    k.wait("pe", prev)
                k.wait("pe", *t_kcvc, t_vread, t_wv)
                for c2 in range(2):
                    inst = pe.matmul(ps_b[:, pcol:pcol + 64],
                                     B["o_vc"][:, c2 * T + tc * 128: c2 * T + (tc + 1) * 128],
                                     B["o_wvhead"][:, c2 * 64:(c2 + 1) * 64],
                                     start=(c2 == 0), stop=(c2 == 1))
                t_mm = k.inc("pe", inst)
                k.wait("dve", t_mm)
                i = dve.tensor_copy(vtok[:, tc * 64:(tc + 1) * 64], ps_b[:, pcol:pcol + 64])
                t_vev[tc] = k.inc("dve", i)
            t_v = k.tok("dve")
            if dbg is not None and h == 0:
                k.wait("pool", t_v)
                i = pool.dma_start(out=dbg["v0"], in_=vtok[:])
                k.inc("pool", i, 16)
            for qc in range(4):
                o_ps = ps_up[:, (qc % 2) * 512:(qc % 2) * 512 + 512]
                r_ps = ps_dn[:, (qc % 2) * 512:(qc % 2) * 512 + 512]
                prev = t_ost.get((h * 4 + qc) - 2)
                if prev is not None:
                    k.wait("pe", prev)
                for kt in range(NTOK):
                    sti = kt % 2
                    st_ps = ps_a[:, sti * 512:sti * 512 + 512]
                    k.wait("pe", st_war[sti], t_rope, t_qa)
                    inst = pe.matmul(st_ps, kch[base:base + 64, kt * 128:(kt + 1) * 128],
                                     qch[base:base + 64, qc * 512:(qc + 1) * 512],
                                     start=True, stop=True)
                    t_st = k.inc("pe", inst)
                    k.wait("act", t_st, t_qa)
                    i = act.activation(stexp[:, sti * 512:sti * 512 + 512], st_ps, AF.Exp,
                                       bias=mcol[:, 20:21])
                    t_exp = k.inc("act", i)
                    st_war[sti] = t_exp
                    k.wait("pe", t_exp, t_v)
                    inst = pe.matmul(o_ps[0:64, :], vtok[:, kt * 64:(kt + 1) * 64],
                                     stexp[:, sti * 512:sti * 512 + 512],
                                     start=(kt == 0), stop=(kt == NTOK - 1))
                    k.inc("pe", inst)
                    inst = pe.matmul(r_ps, B["o_onesbf"][:],
                                     stexp[:, sti * 512:sti * 512 + 512],
                                     start=(kt == 0), stop=(kt == NTOK - 1))
                    k.inc("pe", inst)
                t_pv = k.tok("pe")
                k.wait("dve", t_pv)
                i = dve.reciprocal(recip[:, 0:512], r_ps)
                k.inc("dve", i)
                i = dve.tensor_mul(B["xio"][0:64, 512:1024], o_ps[0:64, :], recip[0:64, 0:512])
                k.inc("dve", i)
                dst0 = hp * T + qc * 512
                if hh == 0:
                    i = dve.tensor_copy(attnout[0:64, dst0:dst0 + 512],
                                        B["xio"][0:64, 512:1024])
                else:
                    i = dve.tensor_copy(attnout[64:128, dst0:dst0 + 512],
                                        B["xio"][0:64, 512:1024])
                t_ost[h * 4 + qc] = k.inc("dve", i)
        t_vread = k.tok("pe")
        t_qkread = k.tok("pe")
    t_attn = (k.tok("dve"), k.tok("act"))
    if dbg is not None:
        k.wait("pool", *t_attn)
        i = pool.dma_start(out=dbg["ao"], in_=attnout[:])
        k.inc("pool", i, 16)

    t_woev = {}
    pj = 0
    for fc in range(8):
        bi = fc % 2
        prevw = t_woev.get(pj - 2)
        if prevw is not None:
            k.wait("sp", prevw)
        k.wait("sp", t_qkread, W.get("bg_wo"))
        i = sp.dma_start(out=wst[bi][:, 0:1024],
                         in_=W["wo_c"][:, fc * 128:(fc + 1) * 128]
                         .rearrange("(c p) m -> p c m", p=128))
        t_wo = k.inc("sp", i, 16)
        for tsub in range(4):
            pcol = (pj % 2) * 512
            prev = t_woev.get(pj - 2)
            if prev is not None:
                k.wait("pe", prev)
            k.wait("pe", t_wo, *t_attn)
            for kc in range(NCH):
                inst = pe.matmul(ps_b[:, pcol:pcol + 512],
                                 wst[bi][:, kc * 128:(kc + 1) * 128],
                                 attnout[:, kc * T + tsub * 512: kc * T + tsub * 512 + 512],
                                 start=(kc == 0), stop=(kc == NCH - 1))
            t_mm = k.inc("pe", inst)
            sl = slice(fc * T + tsub * 512, fc * T + tsub * 512 + 512)
            msl = slice(tsub * 512, tsub * 512 + 512)
            k.wait("dve", t_mm, t_spill[fc], *t_attn)
            i = dve.tensor_sub(B["xio"][:, msl], x_fm[:, sl], m_b[:, msl])
            k.inc("dve", i)
            i = dve.tensor_mul(B["xio"][:, msl], B["xio"][:, msl], rstd_b[:, msl])
            k.inc("dve", i)
            i = dve.tensor_add(x_fm[:, sl], B["xio"][:, msl], ps_b[:, pcol:pcol + 512])
            t_woev[pj] = k.inc("dve", i)
            pj += 1
    t_x1pre = k.tok("dve")
    if dbg is not None:
        k.wait("pool", t_x1pre)
        i = pool.dma_start(out=dbg["x1p"], in_=x_fm[:])
        k.inc("pool", i, 16)
        t_x1pre = (k.tok("pool"), t_x1pre)
        k.wait("dve", t_x1pre[0])
        t_x1pre = t_x1pre[1]

    def emit_rms(ncol, toks_in):
        t_s = None
        for c in range(NCH):
            en, eng = ("act", act) if c % 2 == 0 else ("dve", dve)
            k.wait(en, t_s, *toks_in)
            if en == "act":
                i = act.activation(B["xio"][:], x_fm[:, c * T:(c + 1) * T], AF.Square)
            else:
                i = dve.tensor_mul(B["xio"][:], x_fm[:, c * T:(c + 1) * T],
                                   x_fm[:, c * T:(c + 1) * T])
            t_sq = k.inc(en, i)
            k.wait("pe", t_sq)
            for ts4 in range(4):
                tile = (ps_a if ts4 < 2 else ps_b)[:, (ts4 % 2) * 512:(ts4 % 2) * 512 + 512]
                inst = pe.matmul(tile, cs["ones128"][:],
                                 B["xio"][:, ts4 * 512:(ts4 + 1) * 512],
                                 start=(c == 0), stop=(c == NCH - 1))
            t_s = k.inc("pe", inst)
        k.wait("dve", t_s)
        for ts4 in range(4):
            tile = (ps_a if ts4 < 2 else ps_b)[:, (ts4 % 2) * 512:(ts4 % 2) * 512 + 512]
            i = dve.tensor_scalar(scr[1][:, ts4 * 512:(ts4 + 1) * 512], tile,
                                  1.0 / D, None, ALU.mult)
            t_r = k.inc("dve", i)
        k.wait("act", t_r)
        i = act.activation(scr[1][:], scr[1][:], AF.Ln, bias=cs["epscols"][:, 1:2])
        k.inc("act", i)
        i = act.activation(scr[1][:], scr[1][:], AF.Exp, scale=-0.5)
        return k.inc("act", i)

    t_rms1 = emit_rms(0, (t_x1pre,))
    if dbg is not None:
        k.wait("pool", t_rms1)
        i = pool.dma_start(out=dbg["rrow"], in_=scr[1][:])
        k.inc("pool", i, 16)
        k.wait("dve", k.tok("pool"))
    t_apply = {}
    for c in range(NCH):
        k.wait("dve", t_rms1)
        i = dve.tensor_mul(x_fm[:, c * T:(c + 1) * T], x_fm[:, c * T:(c + 1) * T], scr[1][:])
        k.inc("dve", i)
        i = dve.tensor_scalar(x_fm[:, c * T:(c + 1) * T], x_fm[:, c * T:(c + 1) * T],
                              B["o_n12"][:, c:c + 1], None, ALU.mult)
        k.inc("dve", i)
        i = dve.tensor_copy(y_bf[:, c * T:(c + 1) * T], x_fm[:, c * T:(c + 1) * T])
        t_apply[c] = k.inc("dve", i)
    t_x1 = k.tok("dve")
    if dbg is not None:
        k.wait("pool", t_x1)
        i = pool.dma_start(out=dbg["x1"], in_=x_fm[:])
        k.inc("pool", i, 16)
    return t_x1, m_b, rstd_b, t_apply


def l0ffn_in_map(inputs, shard):
    return {
        "x_in": np.ascontiguousarray(shard),
        "Wq": inputs["L0_ffn_Wq"], "Wk_c": inputs["L0_ffn_Wk_c"],
        "Wv_c": inputs["L0_ffn_Wv_c"], "Wk": inputs["L0_ffn_Wk"],
        "Wv": inputs["L0_ffn_Wv"], "Wo": inputs["L0_ffn_Wo"],
        "W_shared_in": inputs["L0_ffn_W_shared_in"],
        "W_shared_out": inputs["L0_ffn_W_shared_out"],
        "W1_shared": inputs["L0_ffn_W1_shared"],
        "W2_experts": inputs["L0_ffn_W2_experts"],
        "Wg_gate": inputs["L0_ffn_Wg_gate"], "We_gate": inputs["L0_ffn_We_gate"],
        "group_bias": inputs["L0_ffn_group_bias"],
        "expert_bias": inputs["L0_ffn_expert_bias"],
        "norm1_w": inputs["L0_ffn_norm1_w"], "norm2_w": inputs["L0_ffn_norm2_w"],
    }


def emit_l0ffn_part2(k, B, W, t_x1, m_b, rstd_b, t_apply, dbg=None):
    nc = k.nc
    pe = k.eng["pe"]
    act, dve, pool, sp = k.eng["act"], k.eng["dve"], k.eng["pool"], k.eng["sp"]
    cs = B["cs"]
    x_fm, y_bf, scr = B["x_fm"], B["y_bf"], B["scr"]
    ps_a, ps_b, ps_up, ps_dn = B["ps_a"], B["ps_b"], B["ps_up"], B["ps_dn"]
    gbuf = B["o_big"]
    wst = B["o_wst"]
    gsc = B["o_gsc"]
    w8 = B["o_w8"]
    B["o_mbc"] = B["xio"][:, 1536:2048]

    gw = B["o_gw"].rearrange("p (c n) -> p c n", n=10)
    i = dve.tensor_copy(gsc[:, 10:64], cs["ones128"][:, 10:64])
    k.inc("dve", i)
    t_w8 = {}
    for ts in range(NTOK):
        if ts >= 1:
            k.wait("pe", t_w8[ts - 1])
        k.wait("pe", t_x1, t_apply.get(7))
        for c in range(NCH):
            inst = pe.matmul(ps_dn[:, 0:16][:, 0:10],
                             x_fm[:, c * T + ts * 128: c * T + (ts + 1) * 128],
                             gw[:, c, :], start=(c == 0), stop=(c == NCH - 1),
                             skip_group_check=True)
        t_A = k.inc("pe", inst)
        k.wait("dve", t_A)
        i = dve.tensor_add(gsc[:, 16:18], ps_dn[:, 0:2], gsc[:, 0:2])
        k.inc("dve", i)
        i = dve.tensor_add(gsc[:, 24:32], ps_dn[:, 2:10], gsc[:, 2:10])
        t_log = k.inc("dve", i)
        i = dve.tensor_reduce(gsc[:, 18:19], gsc[:, 16:18], AX.X, ALU.max, negate=True)
        t_n = k.inc("dve", i)
        k.wait("act", t_n)
        i = act.activation(gsc[:, 20:22], gsc[:, 16:18], AF.Exp, bias=gsc[:, 18:19])
        t_e2 = k.inc("act", i)
        k.wait("dve", t_e2)
        i = dve.tensor_reduce(gsc[:, 22:23], gsc[:, 20:22], AX.X, ALU.add)
        k.inc("dve", i)
        i = dve.tensor_reduce(gsc[:, 19:20], gsc[:, 20:22], AX.X, ALU.max)
        k.inc("dve", i)
        i = dve.tensor_scalar(gsc[:, 20:22], gsc[:, 20:22], gsc[:, 19:20], None, ALU.is_ge)
        k.inc("dve", i)
        i = dve.tensor_scalar(gsc[:, 32:36], B["o_ig4"][:], gsc[:, 20:21], None, ALU.mult)
        k.inc("dve", i)
        i = dve.tensor_scalar(gsc[:, 36:40], B["o_ig4"][:], gsc[:, 21:22], None, ALU.mult)
        k.inc("dve", i)
        i = dve.tensor_scalar(gsc[:, 32:40], gsc[:, 32:40], 1e9, -1e9, ALU.mult, ALU.add)
        k.inc("dve", i)
        i = dve.tensor_add(gsc[:, 24:32], gsc[:, 24:32], gsc[:, 32:40])
        k.inc("dve", i)
        i = dve.tensor_reduce(gsc[:, 40:41], gsc[:, 24:32], AX.X, ALU.max, negate=True)
        t_me = k.inc("dve", i)
        k.wait("act", t_me)
        i = act.activation(gsc[:, 24:32], gsc[:, 24:32], AF.Exp, bias=gsc[:, 40:41])
        t_ee = k.inc("act", i)
        k.wait("dve", t_ee)
        i = dve.tensor_reduce(gsc[:, 41:42], gsc[:, 24:32], AX.X, ALU.add)
        k.inc("dve", i)
        i = dve.tensor_reduce(gsc[:, 42:43], gsc[:, 24:32], AX.X, ALU.max)
        k.inc("dve", i)
        i = dve.tensor_scalar(gsc[:, 48:56], gsc[:, 24:32], gsc[:, 42:43], -1e9,
                              ALU.is_ge, ALU.mult)
        k.inc("dve", i)
        i = dve.tensor_add(gsc[:, 48:56], gsc[:, 48:56], gsc[:, 24:32])
        k.inc("dve", i)
        i = dve.tensor_reduce(gsc[:, 43:44], gsc[:, 48:56], AX.X, ALU.max)
        k.inc("dve", i)
        i = dve.tensor_scalar(gsc[:, 48:56], gsc[:, 24:32], gsc[:, 43:44], None, ALU.is_ge)
        k.inc("dve", i)
        i = dve.tensor_copy(gsc[:, 44:46], gsc[:, 22:24])
        k.inc("dve", i)
        i = dve.reciprocal(gsc[:, 40:48], gsc[:, 40:48])
        k.inc("dve", i)
        i = dve.tensor_mul(gsc[:, 24:32], gsc[:, 24:32], gsc[:, 48:56])
        k.inc("dve", i)
        i = dve.tensor_scalar(gsc[:, 24:32], gsc[:, 24:32], gsc[:, 19:20], None,
                              ALU.mult)
        k.inc("dve", i)
        i = dve.tensor_scalar(gsc[:, 24:32], gsc[:, 24:32], gsc[:, 41:42], None, ALU.mult)
        k.inc("dve", i)
        i = dve.tensor_scalar(w8[:, ts * 8:(ts + 1) * 8], gsc[:, 24:32], gsc[:, 44:45],
                              None, ALU.mult)
        t_w8[ts] = k.inc("dve", i)

    if dbg is not None:
        k.wait("pool", t_w8[NTOK - 1])
        i = pool.dma_start(out=dbg["w8"], in_=w8[:])
        k.inc("pool", i, 16)
    k.wait("pe", t_w8[NTOK - 1])
    inst = pe.transpose(ps_b[:, 0:128], w8[:], cs["ident_f32"][:])
    t_t = k.inc("pe", inst)
    k.wait("dve", t_t)
    i = dve.tensor_copy(B["xio"][:, 0:128], ps_b[:, 0:128])
    t_mT = k.inc("dve", i)
    k.wait("sp", t_mT)
    i = sp.dma_start(out=W["mask_scratch2"], in_=B["xio"][:, 0:128])
    t_wfm = k.inc("sp", i, 16)

    mbc = B["o_mbc2"]
    wrows = W["mask_scratch2"].rearrange("(ts e) p -> e ts p", e=8)
    t_gread = None
    t_dn_ev = {}
    dn_i = [0]
    t_mbc_read = None
    t_mbc_last = None
    for branch in range(2):
        win = W["wsi_c"] if branch == 0 else W["w1s_c"]
        if branch == 1 and dbg is not None:
            k.wait("pool", k.tok("dve"))
            i = pool.dma_start(out=dbg["sh0"], in_=x_fm[:])
            k.inc("pool", i, 16)
            k.wait("dve", k.tok("pool"))
        for tq in range(4):
            tok0 = tq * 512
            t_gc = {}
            for gc in range(16):
                bi = gc % 2
                prev = t_gc.get(gc - 2)
                if prev is not None:
                    k.wait("sp", prev)
                k.wait("sp", t_gread, t_x1, W.get("bg_ffn"))
                i = sp.dma_start(out=wst[bi][:, 0:1024],
                                 in_=win[:, gc * 128:(gc + 1) * 128]
                                 .rearrange("(c p) m -> p c m", p=128))
                t_wa = k.inc("sp", i, 16)
                k.wait("pool", t_gread, t_x1, W.get("bg_ffn"))
                if prev is not None:
                    k.wait("pool", prev)
                if k.cnt["pwd"] > 0:
                    k.raw["pool"].wait_ge(k.sem["pwd"], k.cnt["pwd"])
                i = k.raw["pool"].dma_start(
                    out=wst[bi][:, 1024:2048],
                    in_=win[:, 2048 + gc * 128:2048 + (gc + 1) * 128]
                    .rearrange("(c p) m -> p c m", p=128))
                k.cnt["pwd"] += 16
                i.then_inc(k.sem["pwd"], 16)
                t_w = ("pwd", k.cnt["pwd"])
                k.wait("pe", t_wa, t_w, t_x1, t_gread)
                if gc >= 2:
                    k.wait("pe", t_gc[gc - 2])
                for kc in range(NCH):
                    inst = pe.matmul(ps_up[:, (gc % 2) * 512:(gc % 2) * 512 + 512],
                                     wst[bi][:, kc * 128:(kc + 1) * 128],
                                     y_bf[:, kc * T + tok0: kc * T + tok0 + 512],
                                     start=(kc == 0), stop=(kc == NCH - 1))
                k.inc("pe", inst)
                for kc in range(NCH):
                    inst = pe.matmul(ps_dn[:, (gc % 2) * 512:(gc % 2) * 512 + 512],
                                     wst[bi][:, 1024 + kc * 128:1024 + (kc + 1) * 128],
                                     y_bf[:, kc * T + tok0: kc * T + tok0 + 512],
                                     start=(kc == 0), stop=(kc == NCH - 1))
                t_ab = k.inc("pe", inst)
                k.wait("act", t_ab, t_gc.get(gc - 1))
                i = act.activation(B["xio"][:, 0:512],
                                   ps_up[:, (gc % 2) * 512:(gc % 2) * 512 + 512], AF.Silu)
                t_sa = k.inc("act", i)
                k.wait("dve", t_sa)
                i = dve.tensor_mul(gbuf[:, gc * 512:(gc + 1) * 512], B["xio"][:, 0:512],
                                   ps_dn[:, (gc % 2) * 512:(gc % 2) * 512 + 512])
                t_gc[gc] = k.inc("dve", i)
            t_g = k.tok("dve")
            t_up_pe = k.tok("pe")
            if branch == 0 and tq == 0 and dbg is not None:
                k.wait("pool", t_g)
                i = pool.dma_start(out=dbg["gb0"], in_=gbuf[:, 0:16 * 512])
                k.inc("pool", i, 16)
                k.wait("pe", k.tok("pool"))
            if branch == 0:
                wouts = [(W["wso_c"], None)]
            else:
                wouts = [(W["w2e_c"][e], e) for e in range(8)]
            for wq_i, (wout, e) in enumerate(wouts):
                if e is not None:
                    k.wait("sp", t_wfm, t_mbc_read)
                    i = sp.dma_start(out=mbc[:].rearrange("p (a b) -> p a b", a=4),
                                     in_=wrows[e, 4 * tq:4 * tq + 4, :]
                                     .partition_broadcast(128))
                    t_mbc = t_mbc_last = k.inc("sp", i, 16)
                for fc in range(8):
                    bi = dn_i[0] % 2
                    prev = t_dn_ev.get(dn_i[0] - 2)
                    if prev is not None:
                        k.wait("sp", prev)
                    k.wait("sp", t_up_pe, W.get("bg_w2e"))
                    i = sp.dma_start(out=wst[bi][:, 0:2048],
                                     in_=wout[:, fc * 128:(fc + 1) * 128]
                                     .rearrange("(c p) m -> p c m", p=128))
                    t_w = k.inc("sp", i, 16)
                    pcol = (dn_i[0] % 2) * 512
                    prev = t_dn_ev.get(dn_i[0] - 2)
                    if prev is not None:
                        k.wait("pe", prev)
                    k.wait("pe", t_w, t_g)
                    for gc in range(16):
                        inst = pe.matmul(ps_a[:, pcol:pcol + 512],
                                         wst[bi][:, gc * 128:(gc + 1) * 128],
                                         gbuf[:, gc * 512:(gc + 1) * 512],
                                         start=(gc == 0), stop=(gc == 15))
                    t_mm = k.inc("pe", inst)
                    sl = slice(fc * T + tok0, fc * T + tok0 + 512)
                    if e is None:
                        k.wait("dve", t_mm)
                        i = dve.tensor_add(x_fm[:, sl], x_fm[:, sl], ps_a[:, pcol:pcol + 512])
                    else:
                        k.wait("dve", t_mm, t_mbc)
                        i = dve.tensor_mul(B["xio"][:, 512:1024], ps_a[:, pcol:pcol + 512],
                                           mbc[:])
                        k.inc("dve", i)
                        i = dve.tensor_add(x_fm[:, sl], x_fm[:, sl], B["xio"][:, 512:1024])
                    t_dn_ev[dn_i[0]] = k.inc("dve", i)
                    dn_i[0] += 1
                if e is not None:
                    t_mbc_read = k.tok("dve")
            t_gread = k.tok("pe")

    t_all = (k.tok("dve"), k.tok("act"), k.tok("pool"))
    if dbg is not None:
        k.wait("pool", k.tok("dve"))
        i = pool.dma_start(out=dbg["pre"], in_=x_fm[:])
        k.inc("pool", i, 16)
    t_s = None
    for c in range(NCH):
        en, eng = ("act", act) if c % 2 == 0 else ("dve", dve)
        k.wait(en, t_s, *t_all)
        if en == "act":
            i = act.activation(B["xio"][:], x_fm[:, c * T:(c + 1) * T], AF.Square)
        else:
            i = dve.tensor_mul(B["xio"][:], x_fm[:, c * T:(c + 1) * T],
                               x_fm[:, c * T:(c + 1) * T])
        t_sq = k.inc(en, i)
        k.wait("pe", t_sq)
        for ts4 in range(4):
            tile = (ps_a if ts4 < 2 else ps_b)[:, (ts4 % 2) * 512:(ts4 % 2) * 512 + 512]
            inst = pe.matmul(tile, cs["ones128"][:],
                             B["xio"][:, ts4 * 512:(ts4 + 1) * 512],
                             start=(c == 0), stop=(c == NCH - 1))
        t_s = k.inc("pe", inst)
    k.wait("dve", t_s)
    for ts4 in range(4):
        tile = (ps_a if ts4 < 2 else ps_b)[:, (ts4 % 2) * 512:(ts4 % 2) * 512 + 512]
        i = dve.tensor_scalar(scr[1][:, ts4 * 512:(ts4 + 1) * 512], tile,
                              1.0 / D, None, ALU.mult)
        t_r = k.inc("dve", i)
    k.wait("act", t_r)
    i = act.activation(scr[1][:], scr[1][:], AF.Ln, bias=cs["epscols"][:, 1:2])
    k.inc("act", i)
    i = act.activation(scr[1][:], scr[1][:], AF.Exp, scale=-0.5)
    t_rms2 = k.inc("act", i)
    if dbg is not None:
        k.wait("pool", t_rms2, k.tok("dve"))
        i = pool.dma_start(out=dbg["pf"], in_=x_fm[:])
        k.inc("pool", i, 16)
        i = pool.dma_start(out=dbg["r2"], in_=scr[1][:])
        k.inc("pool", i, 16)
        k.wait("dve", k.tok("pool"))
    toks = []
    k.wait("sp", t_s)
    for c in range(NCH):
        i = sp.dma_start(out=B["xio"][:], in_=W["x_spill"][:, c * T:(c + 1) * T])
        t_xl = k.inc("sp", i, 16)
        k.wait("dve", t_rms2, t_xl)
        i = dve.tensor_mul(x_fm[:, c * T:(c + 1) * T], x_fm[:, c * T:(c + 1) * T], scr[1][:])
        k.inc("dve", i)
        i = dve.tensor_scalar(x_fm[:, c * T:(c + 1) * T], x_fm[:, c * T:(c + 1) * T],
                              B["o_n12"][:, 8 + c:9 + c], 0.5, ALU.mult, ALU.mult)
        k.inc("dve", i)
        i = dve.tensor_add(x_fm[:, c * T:(c + 1) * T], x_fm[:, c * T:(c + 1) * T],
                           B["xio"][:])
        toks.append(k.inc("dve", i))
        k.wait("sp", toks[-1])
    if dbg is not None:
        k.wait("pool", *toks)
        i = pool.dma_start(out=dbg["xf2"], in_=x_fm[:])
        k.inc("pool", i, 16)
        toks.append(k.tok("pool"))
    return tuple(toks)


def build_l0ffn_program(debug=False):
    nc = bass.Bass("TRN2", target_bir_lowering=False)
    x_in = nc.dram_tensor("x_in", [T, D], F32, kind="ExternalInput").ap()
    x_out = nc.dram_tensor("x_out", [T, D], F32, kind="ExternalOutput").ap()
    W = declare_l0ffn_weights(nc)
    W.update(make_l0ffn_consts(nc))
    k = Ctx(nc)
    B = build_common(nc, k)
    B["cast_b"] = [B["y_bf"][:, 0:2048], B["y_bf"][:, 2048:4096]]
    build_l0ffn_bufs(nc, B)
    if debug:
        B["dbg"] = {
            "ao": nc.dram_tensor("dbg_ao", [128, NCH * T], BF16, kind="ExternalOutput").ap(),
            "x1": nc.dram_tensor("dbg_x1", [128, NCH * T], F32, kind="ExternalOutput").ap(),
            "w8": nc.dram_tensor("dbg_w8", [128, 128], F32, kind="ExternalOutput").ap(),
            "pre": nc.dram_tensor("dbg_pre", [128, NCH * T], F32, kind="ExternalOutput").ap(),
            "q0": nc.dram_tensor("dbg_q0", [128, T], BF16, kind="ExternalOutput").ap(),
            "k0": nc.dram_tensor("dbg_k0", [128, T], BF16, kind="ExternalOutput").ap(),
            "v0": nc.dram_tensor("dbg_v0", [128, 16 * 64], BF16, kind="ExternalOutput").ap(),
            "m0": nc.dram_tensor("dbg_m0", [128, 48], F32, kind="ExternalOutput").ap(),
            "x1p": nc.dram_tensor("dbg_x1p", [128, NCH * T], F32, kind="ExternalOutput").ap(),
            "sh0": nc.dram_tensor("dbg_sh0", [128, NCH * T], F32, kind="ExternalOutput").ap(),
            "gb0": nc.dram_tensor("dbg_gb0", [128, 16 * 512], BF16, kind="ExternalOutput").ap(),
            "pf": nc.dram_tensor("dbg_pf", [128, NCH * T], F32, kind="ExternalOutput").ap(),
            "r2": nc.dram_tensor("dbg_r2", [128, T], F32, kind="ExternalOutput").ap(),
            "xf2": nc.dram_tensor("dbg_xf2", [128, NCH * T], F32, kind="ExternalOutput").ap(),
            "rrow": nc.dram_tensor("dbg_rrow", [128, T], F32, kind="ExternalOutput").ap(),
        }
    t_cast = emit_cast_l0ffn(k, B, W)
    t_load = emit_load_x(k, B, x_in)
    after = (t_load,) + tuple(t_cast)
    t_x1, m_b, rstd_b, t_apply = emit_l0ffn(k, B, W, after, dbg=B.get("dbg"))
    toks = emit_l0ffn_part2(k, B, W, t_x1, m_b, rstd_b, t_apply, dbg=B.get("dbg"))
    t_store = emit_store_x(k, B, x_out, toks)
    k.eng["sp"].wait_ge(k.sem["sp"], k.cnt["sp"])
    k.eng["pool"].wait_ge(k.sem["pool"], k.cnt["pool"])
    return nc


ARENA_COLS = 22224


def phase_barrier(k):
    toks = {e: k.tok(e) for e in Ctx.ENGS}
    for e in Ctx.ENGS:
        k.wait(e, *[toks[p] for p in Ctx.ENGS if p != e])
    return tuple(toks.values())


class Carver:

    def __init__(self, nc, cols_f32=ARENA_COLS):
        self.t32 = nc.alloc_sbuf_tensor("arena", [128, cols_f32], F32)
        self.t16 = self.t32.bitcast(BF16)
        self.limit = cols_f32 * 4
        self.off = 0

    def reset(self):
        self.off = 0

    def take(self, shape, dt):
        self.off = (self.off + 31) & ~31
        bp = 4 if dt == F32 else 2
        col0 = self.off // bp
        self.off += shape[1] * bp
        assert self.off <= self.limit, f"arena overflow: {self.off} > {self.limit}"
        src = self.t32 if dt == F32 else self.t16
        return src[0:shape[0], col0:col0 + shape[1]]

    def view(self, off_bytes, shape, dt):
        bp = 4 if dt == F32 else 2
        src = self.t32 if dt == F32 else self.t16
        return src[0:shape[0], off_bytes // bp: off_bytes // bp + shape[1]]


def carve_l1ffn_bufs(cv, B):
    t = cv.take
    B["g_gwT"] = t([128, 64], F32)
    B["g_grow"] = t([128, 8], F32)
    B["g_biasbc"] = t([128, 16], F32)
    B["g_rstdtok"] = t([128, 128], F32)
    B["g_mask"] = t([128, 16 * 8], F32)
    B["g_maskfm"] = t([8, T], F32)
    B["g_l"] = t([128, 8], F32)
    B["g_mx"] = t([128, 1], F32)
    B["f_b1cols"] = t([128, 8 * 32], F32)
    B["f_b2cols"] = t([128, 8 * 8], F32)
    B["f_w1c"] = [t([128, 4096], BF16) for _ in range(2)]
    B["f_w2c"] = [t([128, 2 * 1024], BF16) for _ in range(2)]
    B["f_h"] = t([128, 32 * 512], BF16)
    B["f_mbc"] = t([128, 512], F32)
    B["g_stok"] = t([128, 128], F32)
    B["g_gterm"] = t([128, 8], F32)
    B["g_maskT"] = t([128, 128], F32)
    B["sel8_sb"] = t([8, 8 * 128], F32)
    B["cast_b"] = [B["y_bf"][:, 0:2048], B["y_bf"][:, 2048:4096]]


def carve_l0attn_bufs(cv, B):
    t = cv.take
    B["a_qT"] = t([128, 8 * T], BF16)
    B["a_kT"] = t([128, 2 * T], BF16)
    B["a_vtok"] = t([128, 4 * 16 * 64], BF16)
    B["a_wbuf"] = t([128, 8 * 1024], BF16)
    B["a_kaug"] = t([4, T], BF16)
    B["a_qaug"] = t([4, T], BF16)
    B["a_qscr"] = t([64, 512], BF16)
    B["a_kscr"] = t([64, T], BF16)
    B["a_stexp"] = t([128, 2 * 512], BF16)
    B["a_iota"] = t([128, 1024], BF16)
    B["a_onesbf"] = t([128, 128], BF16)
    B["a_blockones"] = t([128, 128], F32)
    B["a_cmask"] = t([128, 4 * 512], BF16)
    B["a_identbf"] = t([128, 128], BF16)
    B["a_mcol"] = t([128, 18], F32)
    B["cast_b"] = [B["y_bf"][:, 0:2048], B["y_bf"][:, 2048:4096]]


def carve_l1attn_bufs(cv, B):
    t = cv.take
    B["m_scr"] = [t([128, T], F32) for _ in range(5)]
    B["m_conv"] = t([128, T + 8], F32)
    B["ln2_R"] = B["m_scr"][0]
    B["m_bf"] = t([128, T], BF16)
    B["m_dtr"] = t([128, T], BF16)
    B["m_BC"] = t([32, T], BF16)
    B["m_dtw"] = t([128, T], BF16)
    off0 = (cv.off + 31) & ~31
    B["m_xpw"] = t([128, 16 * 160], BF16)
    t([128, 1536], BF16)
    B["m_sel32r"] = cv.view(off0, [32, 4096], BF16)
    B["m_wst"] = [t([128, T], BF16) for _ in range(2)]
    B["m_ywin"] = t([128, 16 * 128], BF16)
    B["m_cols"] = t([128, 256 + 64 + 16 * 4 + 16], F32)
    B["cast_b"] = [B["y_bf"][:, 0:2048], B["y_bf"][:, 2048:4096]]


def carve_l0ffn_bufs(cv, B):
    t = cv.take
    B["o_big"] = t([128, NCH * T], BF16)
    B["o_kc"] = t([128, 2 * T], BF16)
    B["o_vc"] = t([128, 2 * T], BF16)
    B["o_q"] = t([128, T], BF16)
    B["o_k"] = t([128, T], BF16)
    B["o_qsw"] = t([128, 1024], BF16)
    B["o_qtmp"] = t([128, 1024], BF16)
    B["o_vtok"] = t([128, 16 * 64], BF16)
    B["o_cos"] = t([128, T], BF16)
    B["o_sin"] = t([128, T], BF16)
    B["o_stexp"] = t([128, 1024], BF16)
    B["o_wst"] = [t([128, T], BF16) for _ in range(2)]
    B["o_blockones"] = t([128, 128], F32)
    B["o_onesbf"] = t([128, 128], BF16)
    B["o_mcol"] = t([128, 48], F32)
    B["o_gw"] = t([128, 8 * 10], F32)
    B["o_gsc"] = t([128, 64], F32)
    B["o_w8"] = t([128, 16 * 8], F32)
    B["o_n12"] = t([128, 16], F32)
    B["o_ig4"] = t([128, 4], F32)
    B["o_wvhead"] = t([128, 128], BF16)
    B["o_mbc2"] = t([128, 512], F32)
    B["cast_b"] = [B["y_bf"][:, 0:2048], B["y_bf"][:, 2048:4096]]


def bg_cast(k, pairs):
    raw = k.raw["pool"]
    for src, dst in pairs:
        if k.cnt["bgc"] > 0:
            raw.wait_ge(k.sem["bgc"], k.cnt["bgc"])
        inst = raw.dma_start(out=dst, in_=src)
        k.cnt["bgc"] += 16
        inst.then_inc(k.sem["bgc"], 16)
    return ("bgc", k.cnt["bgc"])


def emit_bg_casts(k, W0f, W0a, W1f, W1a):
    W0f["bg_kcvc"] = bg_cast(k, [
        (W0f["Wk_c"], W0f["wkc_c"]), (W0f["Wv_c"], W0f["wvc_c"]),
        (W0f["Wk"], W0f["wk_c2"]), (W0f["Wv"], W0f["wv_c2"])])
    W0f["bg_wo"] = bg_cast(k, [(W0f["Wo"], W0f["wo_c"])])
    W0f["bg_ffn"] = bg_cast(k, [
        (W0f["W_shared_in"], W0f["wsi_c"]), (W0f["W1_shared"], W0f["w1s_c"]),
        (W0f["W_shared_out"], W0f["wso_c"])])
    W0f["bg_w2e"] = bg_cast(k, [(W0f["W2_experts"], W0f["w2e_c"])])
    W0a["bg_a0"] = bg_cast(k, [(W0a["wkvT"], W0a["wkvT_c"]),
                               (W0a["woT"], W0a["woT_c"])])
    prs = []
    for e in range(8):
        for q in range(8):
            prs.append((W1f["w1"][e][:, q * 512:(q + 1) * 512],
                        W1f["w1cache"][e, q]))
        prs.append((W1f["w2"][e].rearrange("(hc p) m -> hc p m", p=128),
                    W1f["w2cache"][e]))
    W1f["bg_e1"] = bg_cast(k, prs)
    prs = [(W1a["x_proj_w"], W1a["xp_cache"]),
           (W1a["dt_proj_w"], W1a["dt_cache"])]
    for j in range(32):
        prs.append((W1a["in_proj_w"][:, j * 128:(j + 1) * 128], W1a["in_cache"][j]))
    for oc in range(8):
        prs.append((W1a["out_proj_w"][:, oc * 128:(oc + 1) * 128], W1a["out_cache"][oc]))
    W1a["bg_m1"] = bg_cast(k, prs)


def emit_cast_wq(k, B, W0f, W0a):
    pairs = []
    for kc in range(8):
        r = slice(kc * 128, (kc + 1) * 128)
        pairs.append((W0f["Wq"][r, :], W0f["wq_c"][r, :], 0.125))
        pairs.append((W0a["wqT"][r, :], W0a["wqT_c"][r, :], 0.125))
    return emit_cast_pairs(k, B, pairs, engines=("dve",))


def build_fused_program():
    nc = bass.Bass("TRN2", target_bir_lowering=False)
    x_in = nc.dram_tensor("x_in", [T, D], F32, kind="ExternalInput").ap()
    x_out = nc.dram_tensor("x_out", [T, D], F32, kind="ExternalOutput").ap()
    W0f = declare_l0ffn_weights(nc, "f0_")
    W0f.update(make_l0ffn_consts(nc))
    W0a = declare_l0attn_weights(nc, "a0_")
    W0a.update(make_l0attn_consts(nc))
    W1f = declare_l1ffn_weights(nc, "e1_")
    W1a = declare_l1attn_weights(nc, "m1_")
    W1a.update(make_l1attn_consts(nc))
    k = Ctx(nc)
    B = build_common(nc, k)
    B["cast_b"] = [B["y_bf"][:, 0:2048], B["y_bf"][:, 2048:4096]]
    cv = Carver(nc)
    Bf0 = dict(B); carve_l0ffn_bufs(cv, Bf0)
    Ba0 = dict(B); cv.reset(); carve_l0attn_bufs(cv, Ba0)
    Bm1 = dict(B); cv.reset(); carve_l1attn_bufs(cv, Bm1)
    Be1 = dict(B); cv.reset(); carve_l1ffn_bufs(cv, Be1)

    emit_bg_casts(k, W0f, W0a, W1f, W1a)
    t_cast = emit_cast_wq(k, B, W0f, W0a)
    t_load = emit_load_x(k, B, x_in)
    after = (t_load,) + tuple(t_cast)
    t_x1, m_b, rstd_b, t_apply = emit_l0ffn(k, Bf0, W0f, after)
    emit_l0ffn_part2(k, Bf0, W0f, t_x1, m_b, rstd_b, t_apply)
    after = phase_barrier(k)
    emit_l0attn(k, Ba0, W0a, after)
    after = phase_barrier(k)
    t_x1, m_b, rstd_b, t_apply = emit_l0ffn(k, Bf0, W0f, after)
    emit_l0ffn_part2(k, Bf0, W0f, t_x1, m_b, rstd_b, t_apply)
    after = phase_barrier(k)
    emit_l1ffn(k, Be1, W1f, after)
    after = phase_barrier(k)
    emit_l1attn(k, Bm1, W1a, after)
    after = phase_barrier(k)
    toks = emit_l1ffn(k, Be1, W1f, after)
    emit_store_x(k, B, x_out, toks)
    k.eng["sp"].wait_ge(k.sem["sp"], k.cnt["sp"])
    k.eng["pool"].wait_ge(k.sem["pool"], k.cnt["pool"])
    return nc


def fused_in_map(inputs, shard):
    d = {"x_in": np.ascontiguousarray(shard)}
    for pfx, im in (("f0_", l0ffn_in_map), ("a0_", l0attn_in_map),
                    ("m1_", l1attn_in_map), ("e1_", l1ffn_in_map)):
        sub = im(inputs, shard)
        sub.pop("x_in")
        d.update({pfx + name: v for name, v in sub.items()})
    return d


_PROGS = {}


def _get_prog(name):
    if name not in _PROGS:
        _PROGS[name] = {
            "l0ffn": build_l0ffn_program, "l0attn": build_l0attn_program,
            "l1ffn": build_l1ffn_program, "l1attn": build_l1attn_program,
            "fused": build_fused_program,
        }[name]()
    return _PROGS[name]


def kernel(**inputs):
    x = np.asarray(inputs["L0_attn_x"], dtype=np.float32)
    nc = _get_prog("fused")
    res = run_bass_kernel_spmd(nc, [fused_in_map(inputs, x[b]) for b in range(8)],
                               core_ids=list(range(8)))
    return np.stack([np.asarray(res.results[b]["x_out"]) for b in range(8)],
                    0).astype(np.float32)
